# version 02
# baseline (speedup 1.0000x reference)
import math
from contextlib import ExitStack

import numpy as np
import ml_dtypes

import concourse.bass as bass
import concourse.mybir as mybir
from concourse.bass_utils import run_bass_kernel_spmd

F32 = mybir.dt.float32
BF16 = mybir.dt.bfloat16
I16 = mybir.dt.int16
U16 = mybir.dt.uint16
AF = mybir.ActivationFunctionType
ALU = mybir.AluOpType
AX = mybir.AxisListType

NCORES = 8
T = 2048
NT = 16
D = 1024
NIC = 8
H = 8
NCH = 16
FF = 3072
NOC = 24
WRU = FF // NCORES
WRD = D // NCORES
AGN = 3 * 384 * 1024
U_OFF, G_OFF, DN_OFF = 0, 393216, 786432
BF = ml_dtypes.bfloat16

DBG = None
STOP = None
PHASES = ["pw", "wload", "ln1", "proj", "schain", "ophase", "ln2", "x2q", "alpha", "beta",
          "gamma", "full"]

DBG_SPEC = {
    "x1n": [D, 1024], "kfm": [D, T], "vtm": [T, D], "qfm": [D, T],
    "sbf": [128, NCH * 1024], "resid": [T, D], "x2n": [T, D], "x2q": [D, T],
    "wt": [128, NIC * FF], "wdt": [128, NOC * 1024],
    "upg": [FF, T], "tb": [FF, T], "ub": [FF, T], "scl": [128, 16],
    "lnst": [128, 128], "lnst2": [128, 128], "oqd": [128, NOC * 1024],
}

DMA_POOL = ["q%02d" % i for i in range(40)]
SEM_NAMES = DMA_POOL + [
    "sd", "gd", "cc",
    "w_dq", "w_tr", "w_ev",
    "ln_st1", "ln_sq", "ln_nm", "ln_cp", "x_fr", "tr", "trev",
    "pj_mm", "pj_ev", "pj_done",
    "sc_kj", "sc_st", "sc_sn", "sc_mm", "sc_tr",
    "oph_sc", "oph_dm", "oph_o", "oph_oe", "op_mm", "op_re",
    "corr", "msc", "brd",
    "ffa_mm", "ffa_ev", "ffa_mx",
    "ffq", "ffs", "ffv", "dn_mm", "dn_ad", "w_sw", "w_rw", "w_ln",
]


class Bld:
    def __init__(self, nc, ctx):
        self.nc = nc
        self.sems = {n: ctx.enter_context(nc.semaphore(n)) for n in SEM_NAMES}
        self.cnt = {n: 0 for n in SEM_NAMES}
        self._dma_assign = {}
        self._dma_next = 0

    def inc(self, inst, name, n=1):
        inst.then_inc(self.sems[name], n)
        self.cnt[name] += n
        return self.cnt[name]

    def dsem(self, stream, slot=0):
        key = (stream, slot)
        if key not in self._dma_assign:
            self._dma_assign[key] = DMA_POOL[self._dma_next]
            self._dma_next += 1
        return self._dma_assign[key]

    def dma(self, eng, stream, slot, out, in_):
        sem = self.dsem(stream, slot)
        v = self.inc(eng.dma_start(out=out, in_=in_), sem, 16)
        return (sem, v)

    def waitv(self, eng, sv):
        if sv is not None:
            self.wait(eng, sv[0], sv[1])

    def dfinal(self, stream, nslots):
        return [(self.dsem(stream, i), self.cnt[self.dsem(stream, i)])
                for i in range(nslots)]

    def wait(self, eng, name, val):
        if val > 0:
            eng.wait_ge(self.sems[name], val)

    def snap(self):
        return dict(self.cnt)

    def wait_snap(self, eng, snap):
        for n, v in snap.items():
            if v > 0:
                eng.wait_ge(self.sems[n], v)


def _consts():
    g = 1.0 - np.exp2(-5.0 - np.arange(H, dtype=np.float64))
    jj = np.arange(128, dtype=np.float64)
    ii = np.arange(128, dtype=np.float64)
    dmask = np.zeros((128, H * 128), np.float64)
    dq = np.zeros((128, H * 128), np.float64)
    dk = np.zeros((128, H), np.float64)
    ds = np.zeros((128, H * 128), np.float64)
    for h in range(H):
        dmask[:, h * 128:(h + 1) * 128] = (ii[None, :] >= jj[:, None]) * \
            (g[h] ** (-(jj[:, None] + 1.0)))
        dq[:, h * 128:(h + 1) * 128] = (g[h] ** (ii[None, :] + 1.0)) / math.sqrt(128.0)
        dk[:, h] = g[h] ** (127.0 - jj)
        ds[:, h * 128:(h + 1) * 128] = g[h] ** 128.0
    return {
        "cst_dmask": dmask.astype(np.float32),
        "cst_dq": dq.astype(np.float32),
        "cst_dk": dk.astype(np.float32),
        "cst_ds": ds.astype(np.float32).astype(BF),
        "cst_ident": np.eye(128, dtype=BF),
        "cst_identf": np.eye(128, dtype=np.float32),
        "cst_ones": np.ones((1, 128), np.float32),
    }, [float(np.float32(g[h] ** 128.0)) for h in range(H)]


def _ph_on(name):
    if STOP is None:
        return True
    return PHASES.index(name) <= PHASES.index(STOP)


def build_nc(dbg=None):
    nc = bass.Bass("TRN2", target_bir_lowering=False, debug=False,
                   detect_race_conditions=False)

    xin = nc.dram_tensor("x", [T, D], F32, kind="ExternalInput").ap()
    Wmap = {nm: nc.dram_tensor(nm, [D, D], F32, kind="ExternalInput").ap()
            for nm in ("Wq", "Wk", "Wv", "Wo")}
    wup_blk = nc.dram_tensor("wup_blk", [WRU, D], F32, kind="ExternalInput").ap()
    wgt_blk = nc.dram_tensor("wgt_blk", [WRU, D], F32, kind="ExternalInput").ap()
    wdn_blk = nc.dram_tensor("wdn_blk", [WRD, FF], F32, kind="ExternalInput").ap()
    cdt = {"cst_ds": BF16, "cst_ident": BF16}
    cin = {}
    for nm, shape in (("cst_dmask", [128, H * 128]), ("cst_dq", [128, H * 128]),
                      ("cst_dk", [128, H]), ("cst_ds", [128, H * 128]),
                      ("cst_ident", [128, 128]), ("cst_identf", [128, 128]),
                      ("cst_ones", [1, 128])):
        cin[nm] = nc.dram_tensor(nm, shape, cdt.get(nm, F32), kind="ExternalInput").ap()
    masks_in = nc.dram_tensor("masks", [1, 2], F32, kind="ExternalInput").ap()

    out = nc.dram_tensor("out", [T, D], F32, kind="ExternalOutput").ap()
    dbg_out = None
    if dbg is not None:
        dbg_out = nc.dram_tensor("dbg", DBG_SPEC[dbg], F32, kind="ExternalOutput").ap()

    ag_in = nc.dram_tensor("ag_in", [AGN], BF16)
    ag_out = nc.dram_tensor("ag_out", [NCORES * AGN], BF16, addr_space="Shared")
    ccp_in = nc.dram_tensor("ccp_in", [128, 1024], F32)
    ccp_out = nc.dram_tensor("ccp_out", [128, 1024], F32)
    mxio = [(nc.dram_tensor("mxi%d" % k, [1, 2], F32),
             nc.dram_tensor("mxo%d" % k, [1, 2], F32)) for k in range(4)]
    resid_hbm = nc.dram_tensor("resid_hbm", [T, D], F32)
    sp_up = nc.dram_tensor("sp_up", [FF, T], BF16)
    sp_gate = nc.dram_tensor("sp_gate", [FF, T], BF16)
    sp_t = nc.dram_tensor("sp_t", [FF, T], F32)
    sp_u = nc.dram_tensor("sp_u", [FF, T], F32)

    consts, ds_py = _consts()
    del consts

    with ExitStack() as ctx:
        e = ctx.enter_context
        b = Bld(nc, ctx)
        block = e(nc.Block())

        ATT_CST = ("cst_dmask", "cst_dq", "cst_dk", "cst_ds")
        cst = {}
        for nm in cin:
            if nm in ATT_CST:
                continue
            shp = list(cin[nm].tensor.shape)
            cst[nm] = e(nc.sbuf_tensor(nm + "_s", shp, cdt.get(nm, F32)))
        msk_sb = e(nc.sbuf_tensor("msk_sb", [1, 2], F32))
        col_msk = e(nc.sbuf_tensor("col_msk", [128, 2], F32))
        row_scr = e(nc.sbuf_tensor("row_scr", [1, 16], F32))
        col_scl = e(nc.sbuf_tensor("col_scl", [128, 16], F32))
        mxcol = e(nc.sbuf_tensor("mxcol", [128, 4], F32))
        stat = e(nc.sbuf_tensor("stat", [128, 64], F32))
        lnb = [e(nc.sbuf_tensor("lnb%d" % i, [128, 32], F32)) for i in range(2)]
        sqd = e(nc.sbuf_tensor("sqd", [128, 512], BF16))
        xpool = [e(nc.sbuf_tensor("xp%d" % i, [128, D], F32)) for i in range(3)]
        psXh = {}
        psX_stack = ExitStack()
        psXh["t"] = psX_stack.enter_context(nc.psum_tensor("psX", [128, 512], F32))

        S1, R1, RUP, RG, SG, C2, C3, SO = range(8)

        def row_recip(src, dst, tmp):
            @block.scalar
            def _(a):
                b.wait(a, "msc", b.cnt["msc"])
                b.inc(a.activation(row_scr[0:1, tmp:tmp + 1],
                                   row_scr[0:1, src:src + 1], AF.Ln), "ffs")
            rrv = [None]

            @block.sync
            def _(s):
                b.wait(s, "ffs", b.cnt["ffs"])
                rrv[0] = b.dma(s, "lnb", 2, out=row_scr[0:1, 14:15],
                               in_=row_scr[0:1, tmp:tmp + 1])

            @block.scalar
            def _(a):
                b.waitv(a, rrv[0])
                b.inc(a.activation(row_scr[0:1, dst:dst + 1],
                                   row_scr[0:1, 14:15], AF.Exp, scale=-1.0),
                      "ffs")

            @block.vector
            def _(v):
                b.wait(v, "ffs", b.cnt["ffs"])
                v.tensor_tensor(row_scr[0:1, tmp:tmp + 1],
                                row_scr[0:1, src:src + 1],
                                row_scr[0:1, dst:dst + 1], ALU.mult)
                v.nop()
                v.tensor_scalar(row_scr[0:1, tmp:tmp + 1],
                                row_scr[0:1, tmp:tmp + 1], -1.0, 2.0,
                                ALU.mult, ALU.add)
                v.nop()
                b.inc(v.tensor_tensor(row_scr[0:1, dst:dst + 1],
                                      row_scr[0:1, dst:dst + 1],
                                      row_scr[0:1, tmp:tmp + 1], ALU.mult), "msc")

        def bcast_cols(tag, src_lo, n, dst_lo):
            @block.tensor
            def _(t):
                b.wait(t, "msc", b.cnt["msc"])
                b.inc(t.matmul(psXh["t"][:, dst_lo:dst_lo + n], cst["cst_ones"][:, :],
                               row_scr[0:1, src_lo:src_lo + n],
                               start=True, stop=True), "brd")

            @block.vector
            def _(v):
                b.wait(v, "brd", b.cnt["brd"])
                b.inc(v.tensor_copy(col_scl[:, dst_lo:dst_lo + n],
                                    psXh["t"][:, dst_lo:dst_lo + n]), "msc")

        def finalize_max(slot):
            @block.tensor
            def _(t):
                b.wait(t, "msc", b.cnt["msc"])
                b.inc(t.transpose(psXh["t"][0:1, 128:256], mxcol[:, slot:slot + 1],
                                  cst["cst_identf"][:, :]), "brd")

            @block.vector
            def _(v):
                b.wait(v, "brd", b.cnt["brd"])
                b.inc(v.reduce_max(row_scr[0:1, slot:slot + 1],
                                   psXh["t"][0:1, 128:256], AX.X), "msc")

        def run_max_ar(k, src_lo, n, tag):
            vs = [None]

            @block.sync
            def _(s):
                b.wait(s, "msc", b.cnt["msc"])
                vs[0] = b.dma(s, "ar", 0, out=mxio[k][0][0:1, 0:n],
                              in_=row_scr[0:1, src_lo:src_lo + n])
            v_cc = [None]

            @block.gpsimd
            def _(g):
                b.waitv(g, vs[0])
                b.inc(g.collective_compute(
                    "AllReduce", ALU.max, replica_groups=[list(range(NCORES))],
                    ins=[mxio[k][0][:, :]], outs=[mxio[k][1][:, :]]), "cc", 1)
                v_cc[0] = b.cnt["cc"]

            @block.sync
            def _(s):
                b.wait(s, "cc", v_cc[0])
                vs[0] = b.dma(s, "ar", 0, out=row_scr[0:1, src_lo:src_lo + n],
                              in_=mxio[k][1][0:1, 0:n])

            @block.vector
            def _(v):
                b.waitv(v, vs[0])
                b.inc(v.nop(), "msc")

        @block.sync
        def _(s):
            for nm in cst:
                b.dma(s, "boot", 0, out=cst[nm][:, :], in_=cin[nm][:, :])
            b.dma(s, "boot", 0, out=msk_sb[:, :], in_=masks_in[:, :])
        boot_fin = b.dfinal("boot", 1)[0]

        @block.tensor
        def _(t):
            b.waitv(t, boot_fin)
            b.inc(t.matmul(psXh["t"][:, 0:2], cst["cst_ones"][:, :], msk_sb[:, :],
                           start=True, stop=True), "brd")

        @block.vector
        def _(v):
            b.wait(v, "brd", b.cnt["brd"])
            v.memset(mxcol[:, :], 0.0)
            b.inc(v.tensor_copy(col_msk[:, :], psXh["t"][:, 0:2]), "msc")

        ag_done = 0
        with ExitStack() as pw:
            ew = pw.enter_context
            wld = [ew(nc.sbuf_tensor("wld%d" % i, [128, 1024], F32)) for i in range(2)]
            wsc = ew(nc.sbuf_tensor("wsc", [128, 96], F32))
            wsc2 = ew(nc.sbuf_tensor("wsc2", [128, 32], F32))
            wsc3 = ew(nc.sbuf_tensor("wsc3", [128, 32], F32))
            wsc4 = ew(nc.sbuf_tensor("wsc4", [128, 32], F32))
            wkq = ew(nc.sbuf_tensor("wkq", [128, 1024], F32))
            wki = ew(nc.sbuf_tensor("wki", [128, 1024], I16))
            wdq = [ew(nc.sbuf_tensor("wdq%d" % i, [128, 1024], BF16)) for i in range(2)]
            agst = [ew(nc.sbuf_tensor("agst%d" % i, [128, 128], BF16)) for i in range(2)]
            psW = [ew(nc.psum_tensor("psW%d" % i, [128, 128], BF16)) for i in range(2)]

            wtiles = []
            for t_ in range(3):
                wtiles.append(("up", wup_blk[t_ * 128:(t_ + 1) * 128, :], t_))
            for t_ in range(3):
                wtiles.append(("gt", wgt_blk[t_ * 128:(t_ + 1) * 128, :], t_))
            for t_ in range(3):
                wtiles.append(("dn", wdn_blk[:, t_ * 1024:(t_ + 1) * 1024], t_))

            dq0, tr0, ev0 = (b.cnt[k] for k in ("w_dq", "w_tr", "w_ev"))
            wld_v = []
            ags_v = [(b.dsem("ags", k % 2),
                      b.cnt[b.dsem("ags", k % 2)] + 16 * (k // 2 + 1))
                     for k in range(72)]

            ln0 = b.cnt["w_ln"]
            rw0p = b.cnt["w_rw"]
            bn_sem = b.dsem("lnb", 1)
            bn0 = b.cnt[bn_sem]
            bn_v = [(bn_sem, bn0 + 16 * (i + 1)) for i in range(9)]
            bq_sem = b.dsem("lnb", 3)
            bq0 = b.cnt[bq_sem]
            bq_v = [(bq_sem, bq0 + 16 * (i + 1)) for i in range(9)]

            @block.sync
            def _(s):
                for i, (_k, src, _t) in enumerate(wtiles):
                    if i >= 2:
                        b.wait(s, "w_dq", dq0 + i - 1)
                    wld_v.append(b.dma(s, "wld", i % 2, out=wld[i % 2][:, :], in_=src))
                    if i >= 1:
                        b.wait(s, "w_ln", ln0 + i)
                        b.dma(s, "lnb", 1, out=wsc2[:, :], in_=wsc[:, 64:96])
                        b.wait(s, "w_rw", rw0p + i)
                        b.dma(s, "lnb", 3, out=wsc4[:, :], in_=wsc3[:, :])
                b.wait(s, "w_ln", ln0 + 9)
                b.dma(s, "lnb", 1, out=wsc2[:, :], in_=wsc[:, 64:96])
                b.wait(s, "w_rw", rw0p + 9)
                b.dma(s, "lnb", 3, out=wsc4[:, :], in_=wsc3[:, :])

            sw0 = b.cnt["w_sw"]
            rw0 = b.cnt["w_rw"]

            @block.vector
            def _(v):
                rb = wsc4[:, :].rearrange("p (bl el) -> p bl el", el=1) \
                    .broadcast_to([128, 32, 32])
                sbb = wsc[:, 32:64].rearrange("p (bl el) -> p bl el", el=1) \
                    .broadcast_to([128, 32, 32])
                kq3 = wkq[:, :].rearrange("p (bl el) -> p bl el", el=32)
                ki3 = wki[:, :].rearrange("p (bl el) -> p bl el", el=32)

                def dequant(j):
                    w3p = wld[j % 2][:, :].rearrange("p (bl el) -> p bl el", el=32)
                    dq3 = wdq[j % 2][:, :].rearrange("p (bl el) -> p bl el", el=32)
                    b.waitv(v, bq_v[j])
                    if j >= 2:
                        b.wait(v, "w_tr", tr0 + 8 * (j - 1))
                    v.tensor_tensor(wsc[:, 64:96], wsc[:, 32:64], wsc4[:, :],
                                    ALU.mult)
                    v.nop()
                    v.tensor_scalar(wsc[:, 64:96], wsc[:, 64:96], -1.0, 2.0,
                                    ALU.mult, ALU.add)
                    v.nop()
                    v.tensor_tensor(wsc4[:, :], wsc4[:, :], wsc[:, 64:96],
                                    ALU.mult)
                    v.nop()
                    v.tensor_tensor(kq3, w3p, rb, ALU.mult)
                    v.nop()
                    v.tensor_scalar(wki[:, :], wkq[:, :], 7.0, -8.0,
                                    ALU.min, ALU.max)
                    v.nop()
                    b.inc(v.tensor_tensor(dq3, ki3, sbb, ALU.mult), "w_dq")

                for i in range(9):
                    b.waitv(v, wld_v[i])
                    w3 = wld[i % 2][:, :].rearrange("p (bl el) -> p bl el", el=32)
                    v.reduce_max(wsc[:, 0:32], w3, AX.X, apply_absolute_value=True)
                    if i >= 1:
                        dequant(i - 1)
                    else:
                        v.reduce_sum(wsc[:, 64:65], wld[0][:, :], AX.X)
                        v.nop()
                        v.reduce_sum(wsc[:, 64:65], wld[0][:, :], AX.X)
                    b.inc(v.tensor_scalar(wsc[:, 32:64], wsc[:, 0:32], 1.0 / 7.0,
                                          1e-12, ALU.mult, ALU.max), "w_sw")
                dequant(8)

            @block.tensor
            def _(t):
                for i in range(9):
                    b.wait(t, "w_dq", dq0 + i + 1)
                    for ic in range(8):
                        k = i * 8 + ic
                        if k >= 2:
                            b.wait(t, "w_ev", ev0 + k - 1)
                        b.inc(t.transpose(psW[k % 2][:, :],
                                          wdq[i % 2][:, ic * 128:(ic + 1) * 128],
                                          cst["cst_ident"][:, :]), "w_tr")

            @block.scalar
            def _(a):
                for i in range(9):
                    b.wait(a, "w_sw", sw0 + i + 1)
                    b.inc(a.activation(wsc[:, 64:96], wsc[:, 32:64], AF.Ln), "w_ln")
                    b.waitv(a, bn_v[i])
                    b.inc(a.activation(wsc3[:, :], wsc2[:, :], AF.Exp,
                                       scale=-1.0), "w_rw")
                    for ic in range(8):
                        k = i * 8 + ic
                        b.wait(a, "w_tr", tr0 + k + 1)
                        if k >= 2:
                            b.waitv(a, ags_v[k - 2])
                        b.inc(a.copy(agst[k % 2][:, :], psW[k % 2][:, :]), "w_ev")

            @block.gpsimd
            def _(g):
                for i, (kind, _src, tt) in enumerate(wtiles):
                    for ic in range(8):
                        k = i * 8 + ic
                        b.wait(g, "w_ev", ev0 + k + 1)
                        if kind in ("up", "gt"):
                            base = (U_OFF if kind == "up" else G_OFF) \
                                + ic * 128 * 384 + tt * 128
                            dst = ag_in[base:base + 128 * 384] \
                                .rearrange("(p c) -> p c", c=384)[:, 0:128]
                        else:
                            icol = tt * 8 + ic
                            base = DN_OFF + icol * 128 * 128
                            dst = ag_in[base:base + 128 * 128] \
                                .rearrange("(p c) -> p c", c=128)
                        b.dma(g, "ags", k % 2,
                              out=dst, in_=agst[k % 2][:, :])
                for sv in b.dfinal("ags", 2):
                    b.waitv(g, sv)
                b.inc(g.collective_compute(
                    "AllGather", ALU.bypass, replica_groups=[list(range(NCORES))],
                    ins=[ag_in[:]], outs=[ag_out[:]]), "cc", 1)
            ag_done = b.cnt["cc"]

        psX_stack.close()

        pa_alloc = ExitStack()
        ea = pa_alloc.enter_context
        Wslot = [ea(nc.sbuf_tensor("Wsl0", [128, NIC * 1024], BF16))]
        k_fm = ea(nc.sbuf_tensor("k_fm", [128, H * T], BF16))
        v_tm = ea(nc.sbuf_tensor("v_tm", [128, NT * 1024], BF16))
        q_fm = ea(nc.sbuf_tensor("q_fm", [128, H * T], BF16))
        S_fp = ea(nc.sbuf_tensor("S_fp", [128, H * 128], F32))
        S_bf = ea(nc.sbuf_tensor("S_bf", [128, NCH * 1024], BF16))
        x1v = [S_bf[:, 0:1024], S_bf[:, 1024:2048]]
        for nm in ATT_CST:
            shp = list(cin[nm].tensor.shape)
            cst[nm] = ea(nc.sbuf_tensor(nm + "_s", shp, cdt.get(nm, F32)))

        @block.sync
        def _(s):
            for nm in ATT_CST:
                b.dma(s, "boot", 1, out=cst[nm][:, :], in_=cin[nm][:, :])
        att_cst_fin = b.dfinal("boot", 2)[1]
        x1n_tm = [ea(nc.sbuf_tensor("x1ntm%d" % i, [128, 1024], BF16))
                  for i in range(2)]
        x1n = ea(nc.sbuf_tensor("x1n_sh", [128, NIC * 1024], BF16))

        kadj = [ea(nc.sbuf_tensor("kadj%d" % i, [128, 128], BF16)) for i in range(2)]
        sc_bf = [ea(nc.sbuf_tensor("scbf%d" % i, [128, 512], BF16)) for i in range(2)]
        o_sb = [ea(nc.sbuf_tensor("osb%d" % i, [128, 1024], BF16)) for i in range(2)]
        T_bf = o_sb[0]
        rs_sb = [ea(nc.sbuf_tensor("rssb%d" % i, [128, 1024], F32)) for i in range(2)]
        E_scr = rs_sb[0]
        psT = [ea(nc.psum_tensor("psT%d" % i, [128, 128], BF16)) for i in range(2)]
        psK = [ea(nc.psum_tensor("psK%d" % i, [128, 512], F32)) for i in range(4)]
        psA = [ea(nc.psum_tensor("psA%d" % i, [128, 512], F32)) for i in range(2)]

        if _ph_on("wload"):
            wl_gd = []
            pjd0 = b.cnt["pj_done"]
            trev_pa0 = b.cnt["trev"]
            mm_pa0 = b.cnt["pj_mm"]
            x1l_v = {}

            wnames = ["Wk", "Wv", "Wq", "Wk", "Wv", "Wq", "Wo"]
            if not _ph_on("proj"):
                wnames = wnames[:1]
            elif not _ph_on("ophase"):
                wnames = wnames[:6]

            @block.gpsimd
            def _(g):
                for p, nm in enumerate(wnames):
                    if p >= 1:
                        b.wait(g, "pj_done", pjd0 + p)
                    src = Wmap[nm][:, :].rearrange("(ic p) o -> p ic o", p=128)
                    dst = Wslot[0][:, :].rearrange("p (ic o) -> p ic o", o=1024)
                    wl_gd.append(b.dma(g, "wsl", 0, out=dst, in_=src))
                    if nm == "Wv" and _ph_on("proj"):
                        w_ = p // 3
                        for j in range(8):
                            b.wait(g, "trev", trev_pa0 + 64 * w_ + 8 * (j + 1))
                            if j >= 2:
                                b.wait(g, "pj_mm",
                                       mm_pa0 + 384 * w_ + 128 + 16 * (j - 1))
                            lsrc = x1n[:, :].rearrange(
                                "p (ic jj el) -> p ic jj el", jj=8, el=128)[:, :, j, :]
                            ldst = x1v[j % 2][:, :].rearrange(
                                "p (ic el) -> p ic el", el=128)
                            x1l_v[(w_, j)] = b.dma(g, "x1l", j % 2,
                                                   out=ldst, in_=lsrc)

            nwin = 2 if _ph_on("proj") else (1 if _ph_on("ln1") else 0)
            for w in range(nwin):

                st10 = b.cnt["ln_st1"]
                sq0 = b.cnt["ln_sq"]
                nm0 = b.cnt["ln_nm"]
                cp0 = b.cnt["ln_cp"]
                tr0 = b.cnt["tr"]
                tev0 = b.cnt["trev"]
                xv = []
                xv2 = []
                lnv = []

                @block.sync
                def _(s):
                    if w == 1:
                        b.wait(s, "pj_done", pjd0 + 3)
                        b.wait(s, "ln_cp", cp0)
                    for j in range(8):
                        tt = w * 8 + j
                        if j >= 3:
                            b.wait(s, "ln_st1", st10 + j - 2)
                            b.wait(s, "ln_sq", sq0 + j - 2)
                        xv.append(b.dma(s, "xp", j % 3, out=xpool[j % 3][:, :],
                                        in_=xin[tt * 128:(tt + 1) * 128, :]))

                @block.scalar
                def _(a):
                    for j in range(8):
                        b.waitv(a, xv[j])
                        xt = xpool[j % 3]
                        a.activation(sqd[:, :], xt[:, 0:512], AF.Square,
                                     accum_out=stat[:, 8 + j:9 + j])
                        b.inc(a.activation(sqd[:, :], xt[:, 512:1024], AF.Square,
                                           accum_out=stat[:, 16 + j:17 + j]),
                              "ln_sq")

                @block.vector
                def _(v):
                    for j in range(8):
                        b.waitv(v, xv[j])
                        b.inc(v.reduce_sum(stat[:, j:j + 1], xpool[j % 3][:, :],
                                           AX.X), "ln_st1")
                    b.wait(v, "ln_sq", sq0 + 8)
                    v.tensor_tensor(stat[:, 8:16], stat[:, 8:16], stat[:, 16:24],
                                    ALU.add)
                    v.nop()
                    v.tensor_scalar(stat[:, 8:16], stat[:, 8:16], 1.0 / D, None,
                                    ALU.mult)
                    v.tensor_scalar(stat[:, 24:32], stat[:, 0:8], 1.0 / D, None,
                                    ALU.mult)
                    v.nop()
                    v.tensor_tensor(stat[:, 0:8], stat[:, 24:32], stat[:, 24:32],
                                    ALU.mult)
                    v.nop()
                    v.scalar_tensor_tensor(stat[:, 8:16], stat[:, 0:8], -1.0,
                                           stat[:, 8:16], ALU.mult, ALU.add)
                    v.nop()
                    b.inc(v.tensor_scalar(stat[:, 8:16], stat[:, 8:16], 1e-5, None,
                                          ALU.add), "ln_nm")

                @block.scalar
                def _(a):
                    b.wait(a, "ln_nm", nm0 + 1)
                    b.inc(a.activation(lnb[0][:, 0:8], stat[:, 8:16], AF.Ln), "ln_sq")

                @block.sync
                def _(s):
                    b.wait(s, "ln_sq", sq0 + 9)
                    lnv.append(b.dma(s, "lnb", 0, out=lnb[1][:, 0:8],
                                     in_=lnb[0][:, 0:8]))

                @block.scalar
                def _(a):
                    b.waitv(a, lnv[-1])
                    b.inc(a.activation(lnb[0][:, 8:16], lnb[1][:, 0:8], AF.Exp,
                                       scale=-0.5), "ln_sq")

                @block.sync
                def _(s):
                    b.wait(s, "ln_sq", sq0 + 10)
                    lnv.append(b.dma(s, "lnb", 0, out=lnb[1][:, 8:16],
                                     in_=lnb[0][:, 8:16]))
                    for j in range(8):
                        tt = w * 8 + j
                        if j >= 3:
                            b.wait(s, "ln_cp", cp0 + j - 2)
                        xv2.append(b.dma(s, "xp", j % 3, out=xpool[j % 3][:, :],
                                         in_=xin[tt * 128:(tt + 1) * 128, :]))

                @block.vector
                def _(v):
                    b.waitv(v, lnv[-1])
                    b.inc(v.scalar_tensor_tensor(stat[:, 24:32], stat[:, 24:32],
                                                 -1.0, lnb[1][:, 8:16],
                                                 ALU.mult, ALU.mult), "ln_nm")
                    v.nop()
                    for j in range(8):
                        b.waitv(v, xv2[j])
                        if w > 0 or j >= 2:
                            b.wait(v, "tr", tr0 + 8 * (j - 1))
                        xt = xpool[j % 3]
                        ngb = stat[:, 24 + j:25 + j] \
                            .rearrange("p (r el) -> p r el", el=1) \
                            .broadcast_to([128, 1024, 1])
                        x3 = xt[:, :].rearrange("p (r el) -> p r el", el=1)
                        o3 = x1n_tm[j % 2][:, :].rearrange("p (r el) -> p r el", el=1)
                        b.inc(v.scalar_tensor_tensor(
                            o3, x3, lnb[1][:, 8 + j:9 + j], ngb,
                            ALU.mult, ALU.add), "ln_cp")

                @block.tensor
                def _(t):
                    for j in range(8):
                        b.wait(t, "ln_cp", cp0 + j + 1)
                        for ic in range(NIC):
                            kk = j * NIC + ic
                            if kk >= 2:
                                b.wait(t, "trev", tev0 + kk - 1)
                            b.inc(t.transpose(psT[kk % 2][:, :],
                                              x1n_tm[j % 2][:, ic * 128:(ic + 1) * 128],
                                              cst["cst_ident"][:, :]), "tr")

                @block.scalar
                def _(a):
                    for j in range(8):
                        for ic in range(NIC):
                            kk = j * NIC + ic
                            b.wait(a, "tr", tr0 + kk + 1)
                            b.inc(a.copy(x1n[:, ic * 1024 + j * 128:
                                             ic * 1024 + (j + 1) * 128],
                                         psT[kk % 2][:, :]), "trev")

                win_ready = tev0 + 64

                if dbg == "x1n" and w == 0:
                    x1nsnap = b.snap()

                    @block.gpsimd
                    def _(g):
                        b.wait_snap(g, x1nsnap)
                        dv = dbg_out[:, :].rearrange("(ic p) t -> p ic t", p=128)
                        sv = x1n[:, :].rearrange("p (ic t) -> p ic t", t=1024)
                        b.dma(g, "dbg", 0, out=dv, in_=sv)

                if not _ph_on("proj"):
                    continue

                mm0 = b.cnt["pj_mm"]
                ev0p = b.cnt["pj_ev"]

                @block.tensor
                def _(t):
                    b.wait(t, "trev", win_ready)
                    b.waitv(t, wl_gd[3 * w + 0])
                    for oc in range(H):
                        if oc >= 2:
                            b.wait(t, "pj_ev", ev0p + 2 * (oc - 1))
                        bank = (oc % 2) * 2
                        for ic in range(NIC):
                            for tb in range(2):
                                b.inc(t.matmul(
                                    psK[bank + tb][:, :],
                                    Wslot[0][:, ic * 1024 + oc * 128:
                                                       ic * 1024 + (oc + 1) * 128],
                                    x1n[:, ic * 1024 + tb * 512:
                                        ic * 1024 + (tb + 1) * 512],
                                    start=(ic == 0), stop=(ic == NIC - 1)), "pj_mm")
                    b.inc(t.nop(), "pj_done")

                @block.scalar
                def _(a):
                    for oc in range(H):
                        bank = (oc % 2) * 2
                        for tb in range(2):
                            b.wait(a, "pj_mm", mm0 + oc * 16 + 15 + tb)
                            dst = k_fm[:, oc * T + w * 1024 + tb * 512:
                                       oc * T + w * 1024 + (tb + 1) * 512]
                            b.inc(a.copy(dst, psK[bank + tb][:, :]), "pj_ev")

                mm1 = b.cnt["pj_mm"]
                ev1 = b.cnt["pj_ev"]

                @block.tensor
                def _(t):
                    b.waitv(t, wl_gd[3 * w + 1])
                    b.wait(t, "pj_ev", ev1)
                    for j in range(8):
                        b.waitv(t, x1l_v[(w, j)])
                        if j >= 2:
                            b.wait(t, "pj_ev", ev1 + 2 * (j - 1))
                        bank = (j % 2) * 2
                        for ic in range(NIC):
                            for hf in range(2):
                                b.inc(t.matmul(
                                    psK[bank + hf][:, :],
                                    x1v[j % 2][:, ic * 128:(ic + 1) * 128],
                                    Wslot[0][:, ic * 1024 + hf * 512:
                                                           ic * 1024 + (hf + 1) * 512],
                                    start=(ic == 0), stop=(ic == NIC - 1)), "pj_mm")
                    b.inc(t.nop(), "pj_done")

                @block.scalar
                def _(a):
                    for j in range(8):
                        tt = w * 8 + j
                        bank = (j % 2) * 2
                        for hf in range(2):
                            b.wait(a, "pj_mm", mm1 + j * 16 + 15 + hf)
                            dst = v_tm[:, tt * 1024 + hf * 512:
                                       tt * 1024 + (hf + 1) * 512]
                            b.inc(a.copy(dst, psK[bank + hf][:, :]), "pj_ev")

                mm2 = b.cnt["pj_mm"]
                ev2 = b.cnt["pj_ev"]

                @block.tensor
                def _(t):
                    b.waitv(t, wl_gd[3 * w + 2])
                    b.wait(t, "pj_ev", ev2)
                    for oc in range(H):
                        if oc >= 2:
                            b.wait(t, "pj_ev", ev2 + 2 * (oc - 1))
                        bank = (oc % 2) * 2
                        for ic in range(NIC):
                            for tb in range(2):
                                b.inc(t.matmul(
                                    psK[bank + tb][:, :],
                                    Wslot[0][:, ic * 1024 + oc * 128:
                                                           ic * 1024 + (oc + 1) * 128],
                                    x1n[:, ic * 1024 + tb * 512:
                                        ic * 1024 + (tb + 1) * 512],
                                    start=(ic == 0), stop=(ic == NIC - 1)), "pj_mm")
                    b.inc(t.nop(), "pj_done")

                @block.vector
                def _(v):
                    if w == 0:
                        b.waitv(v, att_cst_fin)
                    for oc in range(H):
                        dqb = cst["cst_dq"][:, oc * 128:(oc + 1) * 128] \
                            .rearrange("p (r el) -> p r el", el=128) \
                            .broadcast_to([128, 4, 128])
                        bank = (oc % 2) * 2
                        for tb in range(2):
                            b.wait(v, "pj_mm", mm2 + oc * 16 + 15 + tb)
                            dst = q_fm[:, oc * T + w * 1024 + tb * 512:
                                       oc * T + w * 1024 + (tb + 1) * 512] \
                                .rearrange("p (r el) -> p r el", el=128)
                            src = psK[bank + tb][:, :] \
                                .rearrange("p (r el) -> p r el", el=128)
                            b.inc(v.tensor_tensor(dst, src, dqb, ALU.mult), "pj_ev")

        if _ph_on("schain"):
            kj0 = b.cnt["sc_kj"]
            st0 = b.cnt["sc_st"]
            sn0 = b.cnt["sc_sn"]
            smm0 = b.cnt["sc_mm"]
            str0 = b.cnt["sc_tr"]

            @block.vector
            def _(v):
                b.wait(v, "pj_done", b.cnt["pj_done"])
                v.memset(S_fp[:, :], 0.0)
                b.inc(v.memset(S_bf[:, 0:1024], 0.0), "sc_kj")
            sci = b.cnt["sc_kj"]

            @block.tensor
            def _(t):
                b.wait(t, "sc_kj", sci)
                for c in range(NCH):
                    for h in range(H):
                        k = c * H + h
                        if k >= 2:
                            b.wait(t, "sc_kj", sci + k - 1)
                        b.inc(t.transpose(psT[k % 2][:, :],
                                          k_fm[:, h * T + c * 128:h * T + (c + 1) * 128],
                                          cst["cst_ident"][:, :]), "sc_tr")
                        b.wait(t, "sc_kj", sci + k + 1)
                        if k >= 8:
                            b.wait(t, "sc_st", st0 + 4 * (k // 4 - 1))
                        b.inc(t.matmul(
                            psA[(k // 4) % 2][:, (k % 4) * 128:(k % 4 + 1) * 128],
                            kadj[k % 2][:, :],
                            v_tm[:, c * 1024 + h * 128:c * 1024 + (h + 1) * 128],
                            start=True, stop=True), "sc_mm")

            @block.vector
            def _(v):
                for c in range(NCH):
                    for h in range(H):
                        k = c * H + h
                        b.wait(v, "sc_tr", str0 + k + 1)
                        if k >= 2:
                            b.wait(v, "sc_mm", smm0 + k - 1)
                        b.inc(v.tensor_scalar(kadj[k % 2][:, :], psT[k % 2][:, :],
                                              cst["cst_dk"][:, h:h + 1], None,
                                              ALU.mult), "sc_kj")
                        b.wait(v, "sc_mm", smm0 + k + 1)
                        if k >= 8:
                            b.wait(v, "sc_sn", sn0 + k - 8 + 1)
                        b.inc(v.scalar_tensor_tensor(
                            S_fp[:, h * 128:(h + 1) * 128],
                            S_fp[:, h * 128:(h + 1) * 128], ds_py[h],
                            psA[(k // 4) % 2][:, (k % 4) * 128:(k % 4 + 1) * 128],
                            ALU.mult, ALU.add), "sc_st")

            @block.scalar
            def _(a):
                for c in range(NCH - 1):
                    for h in range(H):
                        k = c * H + h
                        b.wait(a, "sc_st", st0 + k + 1)
                        b.inc(a.copy(S_bf[:, (c + 1) * 1024 + h * 128:
                                         (c + 1) * 1024 + (h + 1) * 128],
                                     S_fp[:, h * 128:(h + 1) * 128]), "sc_sn")
            schain_st = b.cnt["sc_st"]
            schain_sn = b.cnt["sc_sn"]

            @block.vector
            def _(v):
                b.wait(v, "sc_st", schain_st)
                b.wait(v, "msc", 1)
                b.inc(v.tensor_scalar(E_scr[:, :], S_fp[:, :], col_msk[:, 0:1], None,
                                      ALU.mult), "corr")

            hand_v = [None]

            @block.sync
            def _(s):
                b.wait(s, "corr", b.cnt["corr"])
                hand_v[0] = b.dma(s, "ar", 0, out=ccp_in[:, :], in_=E_scr[:, :])

            @block.gpsimd
            def _(g):
                b.waitv(g, hand_v[0])
                b.inc(g.collective_compute(
                    "AllReduce", ALU.add,
                    replica_groups=[[0, 1], [2, 3], [4, 5], [6, 7]],
                    ins=[ccp_in[:, :]], outs=[ccp_out[:, :]]), "cc", 1)
            hand_cc = b.cnt["cc"]

            @block.sync
            def _(s):
                b.wait(s, "cc", hand_cc)
                hand_v[0] = b.dma(s, "ar", 0, out=E_scr[:, :], in_=ccp_out[:, :])

            @block.vector
            def _(v):
                b.waitv(v, hand_v[0])
                b.wait(v, "sc_sn", schain_sn)
                b.inc(v.tensor_scalar(T_bf[:, :], E_scr[:, :], col_msk[:, 1:2], None,
                                      ALU.mult), "corr")
                for c in range(NCH):
                    v.nop()
                    b.inc(v.tensor_tensor(S_bf[:, c * 1024:(c + 1) * 1024],
                                          S_bf[:, c * 1024:(c + 1) * 1024],
                                          T_bf[:, :], ALU.add), "corr")
                    if c < NCH - 1:
                        v.nop()
                        b.inc(v.tensor_tensor(T_bf[:, :], T_bf[:, :],
                                              cst["cst_ds"][:, :], ALU.mult), "corr")
            corr_done = b.cnt["corr"]

        if _ph_on("ophase"):
            osc0 = b.cnt["oph_sc"]
            odm0 = b.cnt["oph_dm"]
            oo0 = b.cnt["oph_o"]
            ooe0 = b.cnt["oph_oe"]
            opm0 = b.cnt["op_mm"]
            ore0 = b.cnt["op_re"]
            xv5 = []
            rsp_v = [(b.dsem("rsp", c % 2),
                      b.cnt[b.dsem("rsp", c % 2)] + 16 * (c // 2 + 1))
                     for c in range(NCH)]

            @block.sync
            def _(s):
                b.wait(s, "pj_done", pjd0 + 6)
                for c in range(NCH):
                    if c >= 3:
                        b.wait(s, "op_re", ore0 + c - 2)
                    xv5.append(b.dma(s, "xp", c % 3, out=xpool[c % 3][:, :],
                                     in_=xin[c * 128:(c + 1) * 128, :]))

            @block.gpsimd
            def _(g):
                for c in range(NCH):
                    b.wait(g, "op_re", ore0 + c + 1)
                    b.dma(g, "rsp", c % 2,
                          out=resid_hbm[c * 128:(c + 1) * 128, :],
                          in_=rs_sb[c % 2][:, :])

            @block.tensor
            def _(t):
                b.wait(t, "corr", corr_done)
                b.waitv(t, wl_gd[6])
                for c in range(NCH):
                    for g_ in range(2):
                        if c >= 1:
                            b.wait(t, "oph_dm", odm0 + 2 * (c - 1) + g_ + 1)
                        for hh in range(4):
                            h = g_ * 4 + hh
                            b.inc(t.matmul(
                                psA[g_][:, hh * 128:(hh + 1) * 128],
                                k_fm[:, h * T + c * 128:h * T + (c + 1) * 128],
                                q_fm[:, h * T + c * 128:h * T + (c + 1) * 128],
                                start=True, stop=True), "oph_sc")
                    for g_ in range(2):
                        b.wait(t, "oph_dm", odm0 + 2 * c + g_ + 1)
                        if c >= 1:
                            b.wait(t, "oph_oe", ooe0 + 2 * (c - 1) + g_ + 1)
                        for hh in range(4):
                            h = g_ * 4 + hh
                            b.inc(t.matmul(
                                psK[g_][:, hh * 128:(hh + 1) * 128],
                                S_bf[:, c * 1024 + h * 128:c * 1024 + (h + 1) * 128],
                                q_fm[:, h * T + c * 128:h * T + (c + 1) * 128],
                                start=True, stop=False), "oph_o")
                            b.inc(t.matmul(
                                psK[g_][:, hh * 128:(hh + 1) * 128],
                                v_tm[:, c * 1024 + h * 128:c * 1024 + (h + 1) * 128],
                                sc_bf[g_][:, hh * 128:(hh + 1) * 128],
                                start=False, stop=True), "oph_o")
                    b.wait(t, "oph_oe", ooe0 + 2 * c + 2)
                    if c >= 1:
                        b.wait(t, "op_re", ore0 + c)
                    for h in range(H):
                        for hf in range(2):
                            b.inc(t.matmul(
                                psK[2 + hf][:, :],
                                o_sb[c % 2][:, h * 128:(h + 1) * 128],
                                Wslot[0][:, h * 1024 + hf * 512:
                                       h * 1024 + (hf + 1) * 512],
                                start=(h == 0), stop=(h == H - 1)), "op_mm")

            @block.vector
            def _(v):
                for c in range(NCH):
                    for g_ in range(2):
                        kk = 2 * c + g_
                        b.wait(v, "oph_sc", osc0 + 4 * kk + 4)
                        b.inc(v.tensor_tensor(
                            sc_bf[g_][:, :], psA[g_][:, :],
                            cst["cst_dmask"][:, g_ * 512:(g_ + 1) * 512],
                            ALU.mult), "oph_dm")
                    b.wait(v, "op_mm", opm0 + 16 * (c + 1))
                    b.waitv(v, xv5[c])
                    if c >= 2:
                        b.waitv(v, rsp_v[c - 2])
                    for hf in range(2):
                        inst = v.tensor_tensor(
                            rs_sb[c % 2][:, hf * 512:(hf + 1) * 512],
                            psK[2 + hf][:, :],
                            xpool[c % 3][:, hf * 512:(hf + 1) * 512], ALU.add)
                        if hf == 1:
                            b.inc(inst, "op_re")

            @block.scalar
            def _(a):
                for c in range(NCH):
                    for g_ in range(2):
                        kk = 2 * c + g_
                        b.wait(a, "oph_o", oo0 + 8 * (kk + 1))
                        if c >= 2:
                            b.wait(a, "op_mm", opm0 + 16 * (c - 1))
                        b.inc(a.copy(o_sb[c % 2][:, g_ * 512:(g_ + 1) * 512],
                                     psK[g_][:, :]), "oph_oe")
            ophase_done = b.snap()

        if dbg in ("x1n", "kfm", "vtm", "qfm", "sbf", "resid"):
            snap = b.snap()

            @block.gpsimd
            def _(g):
                b.wait_snap(g, snap)
                if dbg == "kfm":
                    dv = dbg_out[:, :].rearrange("(h p) t -> p h t", p=128)
                    sv = k_fm[:, :].rearrange("p (h t) -> p h t", t=T)
                    b.dma(g, "dbg", 0, out=dv, in_=sv)
                elif dbg == "qfm":
                    dv = dbg_out[:, :].rearrange("(h p) t -> p h t", p=128)
                    sv = q_fm[:, :].rearrange("p (h t) -> p h t", t=T)
                    b.dma(g, "dbg", 0, out=dv, in_=sv)
                elif dbg == "vtm":
                    dv = dbg_out[:, :].rearrange("(tt p) e -> p tt e", p=128)
                    sv = v_tm[:, :].rearrange("p (tt e) -> p tt e", e=1024)
                    b.dma(g, "dbg", 0, out=dv, in_=sv)
                elif dbg == "sbf":
                    b.dma(g, "dbg", 0, out=dbg_out[:, :], in_=S_bf[:, :])
                elif dbg == "resid":
                    b.dma(g, "dbg", 0, out=dbg_out[:, :], in_=resid_hbm[:, :])

        if not _ph_on("ln2"):
            _finish(nc, b, block, out)
            pa_alloc.close()
            return nc

        pa_done = b.snap()
        pa_alloc.close()
        pb = ExitStack()
        eb = pb.enter_context
        psXh["t"] = eb(nc.psum_tensor("psX2", [128, 512], F32))
        x2q_fm = eb(nc.sbuf_tensor("x2q_fm", [128, NIC * T], BF16))
        accU = eb(nc.sbuf_tensor("accU", [128, 2048], BF16))
        accG = eb(nc.sbuf_tensor("accG", [128, 2048], BF16))
        spb = [eb(nc.sbuf_tensor("spb%d" % i, [128, 2048], BF16)) for i in range(4)]
        psF = [eb(nc.psum_tensor("psF%d" % i, [128, 512], F32)) for i in range(4)]
        pwt = ExitStack()
        wT_up = pwt.enter_context(nc.sbuf_tensor("wT_up", [128, NIC * FF], BF16))
        wT_gt = pwt.enter_context(nc.sbuf_tensor("wT_gt", [128, NIC * FF], BF16))

        @block.sync
        def _(s):
            b.wait(s, "cc", ag_done)
            b.wait_snap(s, pa_done)
            for blk in range(NCORES):
                src = ag_out[blk * AGN + U_OFF:blk * AGN + U_OFF + 393216] \
                    .rearrange("(ic p o) -> p ic o", p=128, o=384)
                dst = wT_up[:, :].rearrange("p (ic og) -> p ic og", og=FF) \
                    [:, :, blk * 384:(blk + 1) * 384]
                b.dma(s, "wtl", 0, out=dst, in_=src)
                src = ag_out[blk * AGN + G_OFF:blk * AGN + G_OFF + 393216] \
                    .rearrange("(ic p o) -> p ic o", p=128, o=384)
                dst = wT_gt[:, :].rearrange("p (ic og) -> p ic og", og=FF) \
                    [:, :, blk * 384:(blk + 1) * 384]
                b.dma(s, "wtl", 0, out=dst, in_=src)
        wt_loaded = b.dfinal("wtl", 1)[0]

        if dbg == "wt":
            @block.gpsimd
            def _(g):
                b.waitv(g, wt_loaded)
                b.dma(g, "dbg", 0, out=dbg_out[:, :], in_=wT_up[:, :])

        with ExitStack() as pln:
            el = pln.enter_context
            x2n = el(nc.sbuf_tensor("x2n", [128, NT * 1024], BF16))

            st10 = b.cnt["ln_st1"]
            sq0 = b.cnt["ln_sq"]
            nm0 = b.cnt["ln_nm"]
            cp0 = b.cnt["ln_cp"]
            rsp_fin = b.dfinal("rsp", 2)
            xv = []
            xv2 = []
            lnv = []

            @block.sync
            def _(s):
                for sv in rsp_fin:
                    b.waitv(s, sv)
                for j in range(NT):
                    if j >= 3:
                        b.wait(s, "ln_st1", st10 + j - 2)
                        b.wait(s, "ln_sq", sq0 + j - 2)
                    xv.append(b.dma(s, "xp", j % 3, out=xpool[j % 3][:, :],
                                    in_=resid_hbm[j * 128:(j + 1) * 128, :]))

            @block.scalar
            def _(a):
                for j in range(NT):
                    b.waitv(a, xv[j])
                    xt = xpool[j % 3]
                    a.activation(sqd[:, :], xt[:, 0:512], AF.Square,
                                 accum_out=stat[:, 16 + j:17 + j])
                    b.inc(a.activation(sqd[:, :], xt[:, 512:1024], AF.Square,
                                       accum_out=stat[:, 32 + j:33 + j]), "ln_sq")

            @block.vector
            def _(v):
                for j in range(NT):
                    b.waitv(v, xv[j])
                    b.inc(v.reduce_sum(stat[:, j:j + 1], xpool[j % 3][:, :], AX.X),
                          "ln_st1")
                b.wait(v, "ln_sq", sq0 + NT)
                if dbg == "lnst2":
                    b.wait(v, "gd", 1)
                v.tensor_tensor(stat[:, 16:32], stat[:, 16:32], stat[:, 32:48],
                                ALU.add)
                v.nop()
                v.tensor_scalar(stat[:, 16:32], stat[:, 16:32], 1.0 / D, None,
                                ALU.mult)
                v.tensor_scalar(stat[:, 48:64], stat[:, 0:16], 1.0 / D, None,
                                ALU.mult)
                v.nop()
                v.tensor_tensor(stat[:, 0:16], stat[:, 48:64], stat[:, 48:64],
                                ALU.mult)
                v.nop()
                v.scalar_tensor_tensor(stat[:, 16:32], stat[:, 0:16], -1.0,
                                       stat[:, 16:32], ALU.mult, ALU.add)
                v.nop()
                b.inc(v.tensor_scalar(stat[:, 16:32], stat[:, 16:32], 1e-5, None,
                                      ALU.add), "ln_nm")

            @block.scalar
            def _(a):
                b.wait(a, "ln_nm", nm0 + 1)
                if dbg == "lnst2":
                    b.wait(a, "gd", 2)
                b.inc(a.activation(lnb[0][:, 0:16], stat[:, 16:32], AF.Ln), "ln_sq")

            @block.sync
            def _(s):
                b.wait(s, "ln_sq", sq0 + NT + 1)
                lnv.append(b.dma(s, "lnb", 0, out=lnb[1][:, 0:16],
                                 in_=lnb[0][:, 0:16]))

            @block.scalar
            def _(a):
                b.waitv(a, lnv[-1])
                b.inc(a.activation(lnb[0][:, 16:32], lnb[1][:, 0:16], AF.Exp,
                                   scale=-0.5), "ln_sq")

            @block.sync
            def _(s):
                b.wait(s, "ln_sq", sq0 + NT + 2)
                lnv.append(b.dma(s, "lnb", 0, out=lnb[1][:, 16:32],
                                 in_=lnb[0][:, 16:32]))
                for j in range(NT):
                    if j >= 3:
                        b.wait(s, "ln_cp", cp0 + j - 2)
                    xv2.append(b.dma(s, "xp", j % 3, out=xpool[j % 3][:, :],
                                     in_=resid_hbm[j * 128:(j + 1) * 128, :]))

            @block.vector
            def _(v):
                b.waitv(v, lnv[-1])
                b.inc(v.scalar_tensor_tensor(stat[:, 48:64], stat[:, 48:64],
                                             -1.0, lnb[1][:, 16:32],
                                             ALU.mult, ALU.mult), "ln_nm")
                v.nop()
                for j in range(NT):
                    b.waitv(v, xv2[j])
                    xt = xpool[j % 3]
                    scr = x2n[:, j * 1024:(j + 1) * 1024]
                    ngb = stat[:, 48 + j:49 + j] \
                        .rearrange("p (r el) -> p r el", el=1) \
                        .broadcast_to([128, 1024, 1])
                    x3 = xt[:, :].rearrange("p (r el) -> p r el", el=1)
                    o3 = scr.rearrange("p (r el) -> p r el", el=1)
                    b.inc(v.scalar_tensor_tensor(
                        o3, x3, lnb[1][:, 16 + j:17 + j], ngb,
                        ALU.mult, ALU.add), "ln_cp")
                    v.nop()
                    v.reduce_max(stat[:, 0:1], scr, AX.X,
                                 apply_absolute_value=True)
                    v.nop()
                    b.inc(v.tensor_tensor(mxcol[:, 0:1], mxcol[:, 0:1],
                                          stat[:, 0:1], ALU.max), "msc")
            ln2_done = b.snap()

            if dbg == "x2n":
                @block.gpsimd
                def _(g):
                    b.wait_snap(g, ln2_done)
                    dv = dbg_out[:, :].rearrange("(tt p) i -> p tt i", p=128)
                    sv = x2n[:, :].rearrange("p (tt i) -> p tt i", i=1024)
                    b.dma(g, "dbg", 0, out=dv, in_=sv)

            if dbg == "lnst2":
                @block.gpsimd
                def _(g):
                    b.wait(g, "ln_sq", sq0 + NT)
                    b.wait(g, "ln_st1", st10 + NT)
                    dv1 = b.dma(g, "dbg", 0, out=dbg_out[:, 0:48],
                                in_=stat[:, 16:64])
                    b.waitv(g, dv1)
                    b.inc(g.nop(), "gd")
                    b.wait(g, "ln_nm", nm0 + 1)
                    dv2 = b.dma(g, "dbg", 0, out=dbg_out[:, 48:80],
                                in_=stat[:, 16:48])
                    dv3 = b.dma(g, "dbg", 0, out=dbg_out[:, 80:96],
                                in_=stat[:, 0:16])
                    b.waitv(g, dv3)
                    b.inc(g.nop(), "gd")
                    b.wait(g, "ln_sq", sq0 + NT + 2)
                    b.dma(g, "dbg", 0, out=dbg_out[:, 96:112],
                          in_=lnb[0][:, 16:32])
                    b.dma(g, "dbg", 0, out=dbg_out[:, 112:128],
                          in_=lnb[1][:, 16:32])

            if dbg == "lnst":
                @block.gpsimd
                def _(g):
                    b.wait_snap(g, ln2_done)
                    b.dma(g, "dbg", 0, out=dbg_out[:, 0:64], in_=stat[:, :])
                    b.dma(g, "dbg", 0, out=dbg_out[:, 64:96], in_=lnb[0][:, :])
                    b.dma(g, "dbg", 0, out=dbg_out[:, 96:128], in_=lnb[1][:, :])

            if not _ph_on("x2q"):
                pln.close()
                pwt.close()
                _finish(nc, b, block, out)
                pb.close()
                return nc

            finalize_max(0)
            run_max_ar(0, 0, 1, "s1")

            @block.vector
            def _(v):
                b.wait(v, "msc", b.cnt["msc"])
                b.inc(v.tensor_scalar(row_scr[0:1, S1:S1 + 1], row_scr[0:1, 0:1],
                                      1.0 / 32767.0, 1e-12, ALU.mult, ALU.max),
                      "msc")
            row_recip(S1, R1, 15)
            bcast_cols("s1r1", S1, 2, S1)
            s1_ready = b.cnt["msc"]

            with ExitStack() as px:
                ex = px.enter_context
                xq16 = [ex(nc.sbuf_tensor("xq16%d" % i, [128, 1024], I16))
                        for i in range(2)]
                xqbf = [ex(nc.sbuf_tensor("xqbf%d" % i, [128, 1024], BF16))
                        for i in range(2)]
                psT2 = [ex(nc.psum_tensor("psT2%d" % i, [128, 128], BF16))
                        for i in range(2)]

                ffq0 = b.cnt["ffq"]
                tr0 = b.cnt["tr"]
                tev0 = b.cnt["trev"]

                @block.vector
                def _(v):
                    b.wait(v, "msc", s1_ready)
                    for j in range(NT):
                        if j >= 2:
                            b.wait(v, "tr", tr0 + 8 * (j - 1))
                        v.tensor_scalar(xq16[j % 2][:, :],
                                        x2n[:, j * 1024:(j + 1) * 1024],
                                        col_scl[:, R1:R1 + 1], None, ALU.mult)
                        v.nop()
                        b.inc(v.tensor_scalar(xqbf[j % 2][:, :], xq16[j % 2][:, :],
                                              col_scl[:, S1:S1 + 1], None, ALU.mult),
                              "ffq")

                @block.tensor
                def _(t):
                    for j in range(NT):
                        b.wait(t, "ffq", ffq0 + j + 1)
                        for ic in range(NIC):
                            kk = j * NIC + ic
                            if kk >= 2:
                                b.wait(t, "trev", tev0 + kk - 1)
                            b.inc(t.transpose(psT2[kk % 2][:, :],
                                              xqbf[j % 2][:, ic * 128:(ic + 1) * 128],
                                              cst["cst_ident"][:, :]), "tr")

                @block.scalar
                def _(a):
                    for j in range(NT):
                        for ic in range(NIC):
                            kk = j * NIC + ic
                            b.wait(a, "tr", tr0 + kk + 1)
                            b.inc(a.copy(x2q_fm[:, ic * T + j * 128:
                                             ic * T + (j + 1) * 128],
                                         psT2[kk % 2][:, :]), "trev")
                x2q_done = b.snap()

            pln.close()

        if dbg == "x2q":
            @block.gpsimd
            def _(g):
                b.wait_snap(g, x2q_done)
                dv = dbg_out[:, :].rearrange("(ic p) t -> p ic t", p=128)
                sv = x2q_fm[:, :].rearrange("p (ic t) -> p ic t", t=T)
                b.dma(g, "dbg", 0, out=dv, in_=sv)

        if not _ph_on("alpha"):
            pwt.close()
            _finish(nc, b, block, out)
            pb.close()
            return nc

        fam0 = b.cnt["ffa_mm"]
        fae0 = b.cnt["ffa_ev"]
        x2qr = b.cnt["trev"]
        spl_v = [(b.dsem("spl", u % 4),
                  b.cnt[b.dsem("spl", u % 4)] + 16 * (u // 4 + 1))
                 for u in range(2 * NOC)]

        @block.vector
        def _(v):
            v.memset(accU[:, :], 0.0)
            b.inc(v.memset(accG[:, :], 0.0), "ffa_mx")
        famx_base = b.cnt["ffa_mx"]

        @block.tensor
        def _(t):
            b.wait(t, "trev", x2qr)
            b.waitv(t, wt_loaded)
            for u in range(2 * NOC):
                oc, wsel = u // 2, u % 2
                wt = wT_up if wsel == 0 else wT_gt
                for tbp in range(2):
                    su = 2 * u + tbp
                    grp = (su % 2) * 2
                    if su >= 2:
                        b.wait(t, "ffa_ev", fae0 + 2 * (su - 1))
                    for ic in range(NIC):
                        for tb2 in range(2):
                            tb = tbp * 2 + tb2
                            b.inc(t.matmul(
                                psF[grp + tb2][:, :],
                                wt[:, ic * FF + oc * 128:ic * FF + (oc + 1) * 128],
                                x2q_fm[:, ic * T + tb * 512:ic * T + (tb + 1) * 512],
                                start=(ic == 0), stop=(ic == NIC - 1)), "ffa_mm")

        @block.scalar
        def _(a):
            for u in range(2 * NOC):
                if u >= 4:
                    b.waitv(a, spl_v[u - 4])
                    b.wait(a, "ffa_mx", famx_base + (u - 4) + 1)
                for tbp in range(2):
                    su = 2 * u + tbp
                    grp = (su % 2) * 2
                    for tb2 in range(2):
                        tb = tbp * 2 + tb2
                        b.wait(a, "ffa_mm", fam0 + 16 * su + 14 + tb2 + 1)
                        b.inc(a.copy(spb[u % 4][:, tb * 512:(tb + 1) * 512],
                                     psF[grp + tb2][:, :]), "ffa_ev")

        @block.vector
        def _(v):
            for u in range(2 * NOC):
                b.wait(v, "ffa_ev", fae0 + 4 * (u + 1))
                acc = accU if u % 2 == 0 else accG
                v.scalar_tensor_tensor(acc[:, :], spb[u % 4][:, :], -1.0,
                                       acc[:, :], ALU.mult, ALU.max)
                v.nop()
                b.inc(v.tensor_tensor(acc[:, :], acc[:, :], spb[u % 4][:, :],
                                      ALU.max), "ffa_mx")

        @block.sync
        def _(s):
            for u in range(2 * NOC):
                oc = u // 2
                sp = sp_up if u % 2 == 0 else sp_gate
                b.wait(s, "ffa_ev", fae0 + 4 * (u + 1))
                b.dma(s, "spl", u % 4, out=sp[oc * 128:(oc + 1) * 128, :],
                      in_=spb[u % 4][:, :])
        alpha_done = b.snap()

        if dbg == "upg":
            @block.gpsimd
            def _(g):
                b.wait_snap(g, alpha_done)
                b.dma(g, "dbg", 0, out=dbg_out[:, :], in_=sp_up[:, :])

        if not _ph_on("beta"):
            pwt.close()
            _finish(nc, b, block, out)
            pb.close()
            return nc

        pwt.close()

        @block.vector
        def _(v):
            b.wait(v, "ffa_mx", b.cnt["ffa_mx"])
            v.reduce_max(mxcol[:, 1:2], accU[:, :], AX.X, apply_absolute_value=True)
            b.inc(v.reduce_max(mxcol[:, 2:3], accG[:, :], AX.X,
                               apply_absolute_value=True), "msc")
        finalize_max(1)
        finalize_max(2)

        run_max_ar(1, 1, 2, "upgate")

        @block.vector
        def _(v):
            b.wait(v, "msc", b.cnt["msc"])
            v.memset(accU[:, :], 0.0)
            v.tensor_scalar(row_scr[0:1, SG:SG + 1], row_scr[0:1, 2:3],
                            1.0 / 32767.0, 1e-12, ALU.mult, ALU.max)
            b.inc(v.tensor_scalar(row_scr[0:1, 9:10], row_scr[0:1, 1:2],
                                  1.0 / 32767.0, 1e-12, ALU.mult, ALU.max),
                  "msc")
        row_recip(9, RUP, 15)
        row_recip(SG, RG, 15)
        bcast_cols("b2", RUP, 1, RUP)
        bcast_cols("b3", RG, 1, RG)
        bcast_cols("b4", SG, 1, SG)
        ar2_ready = b.cnt["msc"]

        with ExitStack() as pbx:
            ex = pbx.enter_context
            gld = [ex(nc.sbuf_tensor("gld%d" % i, [128, 2048], BF16)) for i in range(2)]
            gk16 = [ex(nc.sbuf_tensor("gk%d" % i, [128, 2048], I16)) for i in range(2)]
            sigf = [ex(nc.sbuf_tensor("sigf%d" % i, [128, 2048], F32)) for i in range(2)]
            sgk = [ex(nc.sbuf_tensor("sgk%d" % i, [128, 2048], U16)) for i in range(2)]
            tbuf = [ex(nc.sbuf_tensor("tbuf%d" % i, [128, 2048], F32)) for i in range(2)]

            fq0 = b.cnt["ffq"]
            fs0 = b.cnt["ffs"]
            fv0 = b.cnt["ffv"]
            ffa_fin = b.cnt["ffa_mm"]
            gld_v = []
            tsp_v = [(b.dsem("tsp", oc % 2),
                      b.cnt[b.dsem("tsp", oc % 2)] + 16 * (oc // 2 + 1))
                     for oc in range(NOC)]

            @block.sync
            def _(s):
                b.wait(s, "ffa_mm", ffa_fin)
                for oc in range(NOC):
                    if oc >= 2:
                        b.wait(s, "ffq", fq0 + oc - 1)
                    gld_v.append(b.dma(s, "gld", oc % 2, out=gld[oc % 2][:, :],
                                       in_=sp_gate[oc * 128:(oc + 1) * 128, :]))

            @block.vector
            def _(v):
                b.wait(v, "msc", ar2_ready)
                for oc in range(NOC):
                    b.waitv(v, gld_v[oc])
                    if oc >= 2:
                        b.wait(v, "ffs", fs0 + oc - 1)
                    b.inc(v.tensor_scalar(gk16[oc % 2][:, :], gld[oc % 2][:, :],
                                          col_scl[:, RG:RG + 1], None, ALU.mult),
                          "ffq")
                    b.wait(v, "ffs", fs0 + oc + 1)
                    v.tensor_scalar(sgk[oc % 2][:, :], sigf[oc % 2][:, :],
                                    65536.0, 65535.0, ALU.mult, ALU.min)
                    if oc >= 2:
                        b.waitv(v, tsp_v[oc - 2])
                    v.nop()
                    v.tensor_tensor(tbuf[oc % 2][:, :], gk16[oc % 2][:, :],
                                    sgk[oc % 2][:, :], ALU.mult)
                    v.nop()
                    v.scalar_tensor_tensor(accU[:, :], tbuf[oc % 2][:, :], -1.0,
                                           accU[:, :], ALU.mult, ALU.max)
                    v.nop()
                    b.inc(v.tensor_tensor(accU[:, :], accU[:, :], tbuf[oc % 2][:, :],
                                          ALU.max), "ffv")

            @block.scalar
            def _(a):
                for oc in range(NOC):
                    b.wait(a, "ffq", fq0 + oc + 1)
                    if oc >= 2:
                        b.wait(a, "ffv", fv0 + oc - 1)
                    b.inc(a.activation(sigf[oc % 2][:, :], gk16[oc % 2][:, :],
                                       AF.Sigmoid, scale=col_scl[:, SG:SG + 1]),
                          "ffs")

            @block.gpsimd
            def _(g):
                for oc in range(NOC):
                    b.wait(g, "ffv", fv0 + oc + 1)
                    b.dma(g, "tsp", oc % 2, out=sp_t[oc * 128:(oc + 1) * 128, :],
                          in_=tbuf[oc % 2][:, :])
            beta_done = b.snap()

        if dbg == "tb":
            @block.gpsimd
            def _(g):
                b.wait_snap(g, beta_done)
                b.dma(g, "dbg", 0, out=dbg_out[:, :], in_=sp_t[:, :])

        if not _ph_on("gamma"):
            _finish(nc, b, block, out)
            pb.close()
            return nc

        @block.vector
        def _(v):
            b.wait(v, "ffv", b.cnt["ffv"])
            b.inc(v.reduce_max(mxcol[:, 3:4], accU[:, :], AX.X,
                               apply_absolute_value=True), "msc")
        finalize_max(3)
        run_max_ar(2, 3, 1, "t")

        @block.vector
        def _(v):
            b.wait(v, "msc", b.cnt["msc"])
            v.tensor_scalar(row_scr[0:1, 10:11], row_scr[0:1, SG:SG + 1],
                            1.0 / 65536.0, None, ALU.mult)
            v.nop()
            v.tensor_tensor(row_scr[0:1, 11:12], row_scr[0:1, 3:4],
                            row_scr[0:1, 10:11], ALU.mult)
            v.nop()
            b.inc(v.tensor_scalar(row_scr[0:1, 12:13], row_scr[0:1, 11:12],
                                  1.0 / 32767.0, 1e-12, ALU.mult, ALU.max),
                  "msc")
        row_recip(12, 13, 15)

        @block.vector
        def _(v):
            b.wait(v, "msc", b.cnt["msc"])
            b.inc(v.tensor_tensor(row_scr[0:1, C2:C2 + 1], row_scr[0:1, 10:11],
                                  row_scr[0:1, 13:14], ALU.mult), "msc")
        bcast_cols("c2", C2, 1, C2)
        ar3_ready = b.cnt["msc"]

        with ExitStack() as pgx:
            ex = pgx.enter_context
            tld = [ex(nc.sbuf_tensor("tld%d" % i, [128, 2048], F32)) for i in range(2)]
            uld = [ex(nc.sbuf_tensor("uld%d" % i, [128, 2048], BF16)) for i in range(2)]
            g2k = [ex(nc.sbuf_tensor("g2k%d" % i, [128, 2048], I16)) for i in range(2)]
            upk = [ex(nc.sbuf_tensor("upk%d" % i, [128, 2048], I16)) for i in range(2)]
            ubuf = [ex(nc.sbuf_tensor("ubuf%d" % i, [128, 2048], F32)) for i in range(2)]

            fvg0 = b.cnt["ffv"]
            tld_v = []
            uld_v = []
            usp_v = [(b.dsem("usp", oc % 2),
                      b.cnt[b.dsem("usp", oc % 2)] + 16 * (oc // 2 + 1))
                     for oc in range(NOC)]

            @block.vector
            def _(v):
                b.wait(v, "msc", ar3_ready)
                b.inc(v.memset(accG[:, :], 0.0), "ffq")
            accg_z = b.cnt["ffq"]

            @block.sync
            def _(s):
                for oc in range(NOC):
                    if oc >= 2:
                        b.wait(s, "ffv", fvg0 + oc - 1)
                    tld_v.append(b.dma(s, "tld", oc % 2, out=tld[oc % 2][:, :],
                                       in_=sp_t[oc * 128:(oc + 1) * 128, :]))
                    uld_v.append(b.dma(s, "uld", oc % 2, out=uld[oc % 2][:, :],
                                       in_=sp_up[oc * 128:(oc + 1) * 128, :]))

            @block.vector
            def _(v):
                b.wait(v, "ffq", accg_z)
                for oc in range(NOC):
                    b.waitv(v, tld_v[oc])
                    b.waitv(v, uld_v[oc])
                    v.tensor_scalar(g2k[oc % 2][:, :], tld[oc % 2][:, :],
                                    col_scl[:, C2:C2 + 1], None, ALU.mult)
                    v.tensor_scalar(upk[oc % 2][:, :], uld[oc % 2][:, :],
                                    col_scl[:, RUP:RUP + 1], None, ALU.mult)
                    if oc >= 2:
                        b.waitv(v, usp_v[oc - 2])
                    v.nop()
                    v.tensor_tensor(ubuf[oc % 2][:, :], g2k[oc % 2][:, :],
                                    upk[oc % 2][:, :], ALU.mult)
                    v.nop()
                    v.scalar_tensor_tensor(accG[:, :], ubuf[oc % 2][:, :], -1.0,
                                           accG[:, :], ALU.mult, ALU.max)
                    v.nop()
                    b.inc(v.tensor_tensor(accG[:, :], accG[:, :], ubuf[oc % 2][:, :],
                                          ALU.max), "ffv")

            @block.gpsimd
            def _(g):
                for oc in range(NOC):
                    b.wait(g, "ffv", fvg0 + oc + 1)
                    b.dma(g, "usp", oc % 2, out=sp_u[oc * 128:(oc + 1) * 128, :],
                          in_=ubuf[oc % 2][:, :])
            gamma_done = b.snap()

        if dbg == "ub":
            @block.gpsimd
            def _(g):
                b.wait_snap(g, gamma_done)
                b.dma(g, "dbg", 0, out=dbg_out[:, :], in_=sp_u[:, :])

        if not _ph_on("full"):
            _finish(nc, b, block, out)
            pb.close()
            return nc

        @block.vector
        def _(v):
            b.wait(v, "ffv", b.cnt["ffv"])
            b.inc(v.reduce_max(mxcol[:, 0:1], accG[:, :], AX.X,
                               apply_absolute_value=True), "msc")
        finalize_max(0)
        run_max_ar(3, 0, 1, "u")

        @block.vector
        def _(v):
            b.wait(v, "msc", b.cnt["msc"])
            v.tensor_tensor(row_scr[0:1, 8:9], row_scr[0:1, 12:13],
                            row_scr[0:1, 9:10], ALU.mult)
            v.nop()
            v.tensor_tensor(row_scr[0:1, 15:16], row_scr[0:1, 0:1],
                            row_scr[0:1, 8:9], ALU.mult)
            v.nop()
            b.inc(v.tensor_scalar(row_scr[0:1, SO:SO + 1], row_scr[0:1, 15:16],
                                  1.0 / 32767.0, 1e-12, ALU.mult, ALU.max),
                  "msc")
        row_recip(SO, 11, 10)

        @block.vector
        def _(v):
            b.wait(v, "msc", b.cnt["msc"])
            b.inc(v.tensor_tensor(row_scr[0:1, C3:C3 + 1], row_scr[0:1, 8:9],
                                  row_scr[0:1, 11:12], ALU.mult), "msc")
        bcast_cols("c3", C3, 1, C3)
        bcast_cols("so", SO, 1, SO)
        ar4_ready = b.cnt["msc"]

        if dbg == "scl":
            @block.gpsimd
            def _(g):
                b.wait(g, "msc", ar4_ready)
                b.dma(g, "dbg", 0, out=dbg_out[:, :], in_=col_scl[:, :])

        with ExitStack() as pdx:
            ex = pdx.enter_context
            wdT = ex(nc.sbuf_tensor("wdT", [128, NOC * 1024], BF16))
            oq = ex(nc.sbuf_tensor("oq", [128, NOC * 1024], BF16))
            uld2 = [ex(nc.sbuf_tensor("uld2%d" % i, [128, 1024], F32))
                    for i in range(2)]
            ok16 = [ex(nc.sbuf_tensor("ok16%d" % i, [128, 1024], I16))
                    for i in range(2)]
            outsb = [ex(nc.sbuf_tensor("outsb%d" % i, [128, 1024], F32))
                     for i in range(2)]

            @block.sync
            def _(s):
                b.wait(s, "cc", ag_done)
                for blk in range(NCORES):
                    src = ag_out[blk * AGN + DN_OFF:blk * AGN + DN_OFF + 393216] \
                        .rearrange("(ic p hh) -> p ic hh", p=128, hh=128)
                    dst = wdT[:, :].rearrange("p (ic hg) -> p ic hg", hg=1024) \
                        [:, :, blk * 128:(blk + 1) * 128]
                    b.dma(s, "wdl", 0, out=dst, in_=src)
            wdt_loaded = b.dfinal("wdl", 1)[0]

            if dbg == "wdt":
                @block.gpsimd
                def _(g):
                    b.waitv(g, wdt_loaded)
                    b.dma(g, "dbg", 0, out=dbg_out[:, :], in_=wdT[:, :])

            ln_dn = b.cnt["ln_cp"]
            for hf2 in range(2):
                fqd0 = b.cnt["ffq"]
                dnm0 = b.cnt["dn_mm"]
                dna0 = b.cnt["dn_ad"]
                u2_v = []
                xv3 = []
                ost_v = [(b.dsem("ost", t8 % 2),
                          b.cnt[b.dsem("ost", t8 % 2)] + 16 * (t8 // 2 + 1))
                         for t8 in range(8)]

                @block.sync
                def _(s):
                    for oc in range(NOC):
                        if oc >= 2:
                            b.wait(s, "ffq", fqd0 + oc - 1)
                        u2_v.append(b.dma(
                            s, "u2l", oc % 2, out=uld2[oc % 2][:, :],
                            in_=sp_u[oc * 128:(oc + 1) * 128,
                                     hf2 * 1024:(hf2 + 1) * 1024]))

                @block.vector
                def _(v):
                    b.wait(v, "msc", ar4_ready)
                    if hf2 == 1:
                        b.wait(v, "dn_mm", dnm0)
                    for oc in range(NOC):
                        b.waitv(v, u2_v[oc])
                        v.tensor_scalar(ok16[oc % 2][:, :], uld2[oc % 2][:, :],
                                        col_scl[:, C3:C3 + 1], None, ALU.mult)
                        v.nop()
                        b.inc(v.tensor_scalar(oq[:, oc * 1024:(oc + 1) * 1024],
                                              ok16[oc % 2][:, :],
                                              col_scl[:, SO:SO + 1], None, ALU.mult),
                              "ffq")
                oq_ready = b.cnt["ffq"]

                if dbg == "oqd" and hf2 == 1:
                    @block.gpsimd
                    def _(g):
                        b.wait(g, "ffq", oq_ready)
                        b.dma(g, "dbg", 0, out=dbg_out[:, :], in_=oq[:, :])

                @block.sync
                def _(s):
                    if hf2 == 0:
                        b.wait(s, "ln_cp", ln_dn)
                    for t8 in range(8):
                        tt = hf2 * 8 + t8
                        if t8 >= 3:
                            b.wait(s, "dn_ad", dna0 + t8 - 2)
                        elif hf2 == 1:
                            b.wait(s, "dn_ad", dna0)
                        xv3.append(b.dma(s, "xp", t8 % 3, out=xpool[t8 % 3][:, :],
                                         in_=resid_hbm[tt * 128:(tt + 1) * 128, :]))

                @block.tensor
                def _(t):
                    b.wait(t, "ffq", oq_ready)
                    b.waitv(t, wdt_loaded)
                    for t8 in range(8):
                        bank = (t8 % 2) * 2
                        if t8 >= 2:
                            b.wait(t, "dn_ad", dna0 + t8 - 1)
                        for oc in range(NOC):
                            for hf in range(2):
                                b.inc(t.matmul(
                                    psF[bank + hf][:, :],
                                    oq[:, oc * 1024 + t8 * 128:oc * 1024 + (t8 + 1) * 128],
                                    wdT[:, oc * 1024 + hf * 512:
                                        oc * 1024 + (hf + 1) * 512],
                                    start=(oc == 0), stop=(oc == NOC - 1)), "dn_mm")

                @block.vector
                def _(v):
                    for t8 in range(8):
                        bank = (t8 % 2) * 2
                        b.wait(v, "dn_mm", dnm0 + 48 * (t8 + 1))
                        b.waitv(v, xv3[t8])
                        if t8 >= 2:
                            b.waitv(v, ost_v[t8 - 2])
                        for hf in range(2):
                            inst = v.tensor_tensor(
                                outsb[t8 % 2][:, hf * 512:(hf + 1) * 512],
                                psF[bank + hf][:, :],
                                xpool[t8 % 3][:, hf * 512:(hf + 1) * 512], ALU.add)
                            if hf == 1:
                                b.inc(inst, "dn_ad")

                @block.gpsimd
                def _(g):
                    for t8 in range(8):
                        tt = hf2 * 8 + t8
                        b.wait(g, "dn_ad", dna0 + t8 + 1)
                        b.dma(g, "ost", t8 % 2, out=out[tt * 128:(tt + 1) * 128, :],
                              in_=outsb[t8 % 2][:, :])

        pb.close()
        _finish(nc, b, block, None)

    return nc


def _finish(nc, b, block, out_ap):
    snap = b.snap()

    @block.sync
    def _(s):
        b.wait_snap(s, snap)

    @block.gpsimd
    def _(g):
        b.wait_snap(g, snap)


def _shard_inputs(inputs):
    x = np.asarray(inputs["L0_attn_hidden_states"], dtype=np.float32)
    xf = np.ascontiguousarray(x.reshape(NCORES * T, D))
    consts, _ = _consts()
    w_up = np.asarray(inputs["L0_ffn_w_up"], np.float32)
    w_gate = np.asarray(inputs["L0_ffn_w_gate"], np.float32)
    w_down = np.asarray(inputs["L0_ffn_w_down"], np.float32)
    in_maps = []
    for c in range(NCORES):
        m = {
            "x": np.ascontiguousarray(xf[c * T:(c + 1) * T]),
            "Wq": np.asarray(inputs["L0_attn_Wq"], np.float32),
            "Wk": np.asarray(inputs["L0_attn_Wk"], np.float32),
            "Wv": np.asarray(inputs["L0_attn_Wv"], np.float32),
            "Wo": np.asarray(inputs["L0_attn_Wo"], np.float32),
            "wup_blk": np.ascontiguousarray(w_up[c * WRU:(c + 1) * WRU]),
            "wgt_blk": np.ascontiguousarray(w_gate[c * WRU:(c + 1) * WRU]),
            "wdn_blk": np.ascontiguousarray(w_down[c * WRD:(c + 1) * WRD]),
            "masks": np.array([[1.0, 0.0]] if c % 2 == 0 else [[0.0, 1.0]],
                              np.float32),
        }
        m.update({k: v for k, v in consts.items()})
        in_maps.append(m)
    return in_maps


def kernel(**inputs):
    in_maps = _shard_inputs(inputs)
    nc = build_nc(dbg=DBG)
    res = run_bass_kernel_spmd(nc, in_maps, core_ids=list(range(NCORES)))
    if DBG is not None:
        return [r.get("dbg") for r in res.results], \
            [r.get("out") for r in res.results]
    outs = np.concatenate([res.results[c]["out"] for c in range(NCORES)], axis=0)
    return outs.reshape(4, 4096, 1024)


# version 03
# speedup vs baseline: 1.0229x; 1.0229x over previous
import math
from contextlib import ExitStack

import numpy as np
import ml_dtypes

import concourse.bass as bass
import concourse.mybir as mybir
from concourse.bass_utils import run_bass_kernel_spmd

F32 = mybir.dt.float32
BF16 = mybir.dt.bfloat16
I16 = mybir.dt.int16
U16 = mybir.dt.uint16
AF = mybir.ActivationFunctionType
ALU = mybir.AluOpType
AX = mybir.AxisListType

NCORES = 8
T = 2048
NT = 16
D = 1024
NIC = 8
H = 8
NCH = 16
FF = 3072
NOC = 24
WRU = FF // NCORES
WRD = D // NCORES
AGN = 3 * 384 * 1024
U_OFF, G_OFF, DN_OFF = 0, 393216, 786432
BF = ml_dtypes.bfloat16

DBG = None
STOP = None
PHASES = ["pw", "wload", "ln1", "proj", "schain", "ophase", "ln2", "x2q", "alpha", "beta",
          "gamma", "full"]

DBG_SPEC = {
    "x1n": [D, 1024], "kfm": [D, T], "vtm": [T, D], "qfm": [D, T],
    "sbf": [128, NCH * 1024], "resid": [T, D], "x2n": [T, D], "x2q": [D, T],
    "wt": [128, NIC * FF], "wdt": [128, NOC * 1024],
    "upg": [FF, T], "tb": [FF, T], "ub": [FF, T], "scl": [128, 16],
    "lnst": [128, 128], "lnst2": [128, 128], "oqd": [128, NOC * 1024],
}

DMA_POOL = ["q%02d" % i for i in range(40)]
SEM_NAMES = DMA_POOL + [
    "sd", "gd", "cc",
    "w_dq", "w_tr", "w_ev",
    "ln_st1", "ln_sq", "ln_nm", "ln_cp", "x_fr", "tr", "trev",
    "pj_mm", "pj_ev", "pj_done",
    "sc_kj", "sc_st", "sc_sn", "sc_mm", "sc_tr",
    "oph_sc", "oph_dm", "oph_o", "oph_oe", "op_mm", "op_re",
    "corr", "msc", "brd",
    "ffa_mm", "ffa_ev", "ffa_mx",
    "ffq", "ffs", "ffv", "dn_mm", "dn_ad", "w_sw", "w_rw", "w_ln",
]


class Bld:
    def __init__(self, nc, ctx):
        self.nc = nc
        self.sems = {n: ctx.enter_context(nc.semaphore(n)) for n in SEM_NAMES}
        self.cnt = {n: 0 for n in SEM_NAMES}
        self._dma_assign = {}
        self._dma_next = 0

    def inc(self, inst, name, n=1):
        inst.then_inc(self.sems[name], n)
        self.cnt[name] += n
        return self.cnt[name]

    def dsem(self, stream, slot=0):
        key = (stream, slot)
        if key not in self._dma_assign:
            self._dma_assign[key] = DMA_POOL[self._dma_next]
            self._dma_next += 1
        return self._dma_assign[key]

    def dma(self, eng, stream, slot, out, in_):
        sem = self.dsem(stream, slot)
        v = self.inc(eng.dma_start(out=out, in_=in_), sem, 16)
        return (sem, v)

    def waitv(self, eng, sv):
        if sv is not None:
            self.wait(eng, sv[0], sv[1])

    def dfinal(self, stream, nslots):
        return [(self.dsem(stream, i), self.cnt[self.dsem(stream, i)])
                for i in range(nslots)]

    def wait(self, eng, name, val):
        if val > 0:
            eng.wait_ge(self.sems[name], val)

    def snap(self):
        return dict(self.cnt)

    def wait_snap(self, eng, snap):
        for n, v in snap.items():
            if v > 0:
                eng.wait_ge(self.sems[n], v)


def _consts():
    g = 1.0 - np.exp2(-5.0 - np.arange(H, dtype=np.float64))
    jj = np.arange(128, dtype=np.float64)
    ii = np.arange(128, dtype=np.float64)
    dmask = np.zeros((128, H * 128), np.float64)
    dq = np.zeros((128, H * 128), np.float64)
    dk = np.zeros((128, H), np.float64)
    ds = np.zeros((128, H * 128), np.float64)
    for h in range(H):
        dmask[:, h * 128:(h + 1) * 128] = (ii[None, :] >= jj[:, None]) * \
            (g[h] ** (-(jj[:, None] + 1.0)))
        dq[:, h * 128:(h + 1) * 128] = (g[h] ** (ii[None, :] + 1.0)) / math.sqrt(128.0)
        dk[:, h] = g[h] ** (127.0 - jj)
        ds[:, h * 128:(h + 1) * 128] = g[h] ** 128.0
    return {
        "cst_dmask": dmask.astype(np.float32),
        "cst_dq": dq.astype(np.float32),
        "cst_dk": dk.astype(np.float32),
        "cst_ds": ds.astype(np.float32).astype(BF),
        "cst_ident": np.eye(128, dtype=BF),
        "cst_identf": np.eye(128, dtype=np.float32),
        "cst_ones": np.ones((1, 128), np.float32),
    }, [float(np.float32(g[h] ** 128.0)) for h in range(H)]


def _ph_on(name):
    if STOP is None:
        return True
    return PHASES.index(name) <= PHASES.index(STOP)


def build_nc(dbg=None):
    nc = bass.Bass("TRN2", target_bir_lowering=False, debug=False,
                   detect_race_conditions=False)

    xin = nc.dram_tensor("x", [T, D], F32, kind="ExternalInput").ap()
    Wmap = {nm: nc.dram_tensor(nm, [D, D], F32, kind="ExternalInput").ap()
            for nm in ("Wq", "Wk", "Wv", "Wo")}
    wup_blk = nc.dram_tensor("wup_blk", [WRU, D], F32, kind="ExternalInput").ap()
    wgt_blk = nc.dram_tensor("wgt_blk", [WRU, D], F32, kind="ExternalInput").ap()
    wdn_blk = nc.dram_tensor("wdn_blk", [WRD, FF], F32, kind="ExternalInput").ap()
    cdt = {"cst_ds": BF16, "cst_ident": BF16}
    cin = {}
    for nm, shape in (("cst_dmask", [128, H * 128]), ("cst_dq", [128, H * 128]),
                      ("cst_dk", [128, H]), ("cst_ds", [128, H * 128]),
                      ("cst_ident", [128, 128]), ("cst_identf", [128, 128]),
                      ("cst_ones", [1, 128])):
        cin[nm] = nc.dram_tensor(nm, shape, cdt.get(nm, F32), kind="ExternalInput").ap()
    masks_in = nc.dram_tensor("masks", [1, 2], F32, kind="ExternalInput").ap()

    out = nc.dram_tensor("out", [T, D], F32, kind="ExternalOutput").ap()
    dbg_out = None
    if dbg is not None:
        dbg_out = nc.dram_tensor("dbg", DBG_SPEC[dbg], F32, kind="ExternalOutput").ap()

    ag_in = nc.dram_tensor("ag_in", [AGN], BF16)
    ag_out = nc.dram_tensor("ag_out", [NCORES * AGN], BF16, addr_space="Shared")
    ccp_in = nc.dram_tensor("ccp_in", [128, 1024], F32)
    ccp_out = nc.dram_tensor("ccp_out", [128, 1024], F32)
    mxio = [(nc.dram_tensor("mxi%d" % k, [1, 2], F32),
             nc.dram_tensor("mxo%d" % k, [1, 2], F32)) for k in range(4)]
    resid_hbm = nc.dram_tensor("resid_hbm", [T, D], F32)
    sp_up = nc.dram_tensor("sp_up", [FF, T], BF16)
    sp_gate = nc.dram_tensor("sp_gate", [FF, T], BF16)
    sp_t = nc.dram_tensor("sp_t", [FF, T], F32)
    sp_u = nc.dram_tensor("sp_u", [FF, T], F32)

    consts, ds_py = _consts()
    del consts

    with ExitStack() as ctx:
        e = ctx.enter_context
        b = Bld(nc, ctx)
        block = e(nc.Block())

        ATT_CST = ("cst_dmask", "cst_dq", "cst_dk", "cst_ds")
        cst = {}
        for nm in cin:
            if nm in ATT_CST:
                continue
            shp = list(cin[nm].tensor.shape)
            cst[nm] = e(nc.sbuf_tensor(nm + "_s", shp, cdt.get(nm, F32)))
        msk_sb = e(nc.sbuf_tensor("msk_sb", [1, 2], F32))
        col_msk = e(nc.sbuf_tensor("col_msk", [128, 2], F32))
        row_scr = e(nc.sbuf_tensor("row_scr", [1, 16], F32))
        col_scl = e(nc.sbuf_tensor("col_scl", [128, 16], F32))
        mxcol = e(nc.sbuf_tensor("mxcol", [128, 4], F32))
        stat = e(nc.sbuf_tensor("stat", [128, 64], F32))
        lnb = [e(nc.sbuf_tensor("lnb%d" % i, [128, 32], F32)) for i in range(2)]
        sqd = e(nc.sbuf_tensor("sqd", [128, 512], BF16))
        xpool = [e(nc.sbuf_tensor("xp%d" % i, [128, D], F32)) for i in range(3)]
        psXh = {}
        psX_stack = ExitStack()
        psXh["t"] = psX_stack.enter_context(nc.psum_tensor("psX", [128, 512], F32))

        S1, R1, RUP, RG, SG, C2, C3, SO = range(8)

        def row_recip(src, dst, tmp):
            @block.scalar
            def _(a):
                b.wait(a, "msc", b.cnt["msc"])
                b.inc(a.activation(row_scr[0:1, tmp:tmp + 1],
                                   row_scr[0:1, src:src + 1], AF.Ln), "ffs")
            rrv = [None]

            @block.sync
            def _(s):
                b.wait(s, "ffs", b.cnt["ffs"])
                rrv[0] = b.dma(s, "lnb", 2, out=row_scr[0:1, 14:15],
                               in_=row_scr[0:1, tmp:tmp + 1])

            @block.scalar
            def _(a):
                b.waitv(a, rrv[0])
                b.inc(a.activation(row_scr[0:1, dst:dst + 1],
                                   row_scr[0:1, 14:15], AF.Exp, scale=-1.0),
                      "ffs")

            @block.vector
            def _(v):
                b.wait(v, "ffs", b.cnt["ffs"])
                v.tensor_tensor(row_scr[0:1, tmp:tmp + 1],
                                row_scr[0:1, src:src + 1],
                                row_scr[0:1, dst:dst + 1], ALU.mult)
                v.nop()
                v.tensor_scalar(row_scr[0:1, tmp:tmp + 1],
                                row_scr[0:1, tmp:tmp + 1], -1.0, 2.0,
                                ALU.mult, ALU.add)
                v.nop()
                b.inc(v.tensor_tensor(row_scr[0:1, dst:dst + 1],
                                      row_scr[0:1, dst:dst + 1],
                                      row_scr[0:1, tmp:tmp + 1], ALU.mult), "msc")

        def bcast_cols(tag, src_lo, n, dst_lo):
            @block.tensor
            def _(t):
                b.wait(t, "msc", b.cnt["msc"])
                b.inc(t.matmul(psXh["t"][:, dst_lo:dst_lo + n], cst["cst_ones"][:, :],
                               row_scr[0:1, src_lo:src_lo + n],
                               start=True, stop=True), "brd")

            @block.vector
            def _(v):
                b.wait(v, "brd", b.cnt["brd"])
                b.inc(v.tensor_copy(col_scl[:, dst_lo:dst_lo + n],
                                    psXh["t"][:, dst_lo:dst_lo + n]), "msc")

        def finalize_max(slot):
            @block.tensor
            def _(t):
                b.wait(t, "msc", b.cnt["msc"])
                b.inc(t.transpose(psXh["t"][0:1, 128:256], mxcol[:, slot:slot + 1],
                                  cst["cst_identf"][:, :]), "brd")

            @block.vector
            def _(v):
                b.wait(v, "brd", b.cnt["brd"])
                b.inc(v.reduce_max(row_scr[0:1, slot:slot + 1],
                                   psXh["t"][0:1, 128:256], AX.X), "msc")

        def run_max_ar(k, src_lo, n, tag):
            vs = [None]

            @block.sync
            def _(s):
                b.wait(s, "msc", b.cnt["msc"])
                vs[0] = b.dma(s, "ar", 0, out=mxio[k][0][0:1, 0:n],
                              in_=row_scr[0:1, src_lo:src_lo + n])
            v_cc = [None]

            @block.gpsimd
            def _(g):
                b.waitv(g, vs[0])
                b.inc(g.collective_compute(
                    "AllReduce", ALU.max, replica_groups=[list(range(NCORES))],
                    ins=[mxio[k][0][:, :]], outs=[mxio[k][1][:, :]]), "cc", 1)
                v_cc[0] = b.cnt["cc"]

            @block.sync
            def _(s):
                b.wait(s, "cc", v_cc[0])
                vs[0] = b.dma(s, "ar", 0, out=row_scr[0:1, src_lo:src_lo + n],
                              in_=mxio[k][1][0:1, 0:n])

            @block.vector
            def _(v):
                b.waitv(v, vs[0])
                b.inc(v.nop(), "msc")

        @block.sync
        def _(s):
            for nm in cst:
                b.dma(s, "boot", 0, out=cst[nm][:, :], in_=cin[nm][:, :])
            b.dma(s, "boot", 0, out=msk_sb[:, :], in_=masks_in[:, :])
        boot_fin = b.dfinal("boot", 1)[0]

        @block.tensor
        def _(t):
            b.waitv(t, boot_fin)
            b.inc(t.matmul(psXh["t"][:, 0:2], cst["cst_ones"][:, :], msk_sb[:, :],
                           start=True, stop=True), "brd")

        @block.vector
        def _(v):
            b.wait(v, "brd", b.cnt["brd"])
            v.memset(mxcol[:, :], 0.0)
            b.inc(v.tensor_copy(col_msk[:, :], psXh["t"][:, 0:2]), "msc")

        ag_done = 0
        with ExitStack() as pw:
            ew = pw.enter_context
            wld = [ew(nc.sbuf_tensor("wld%d" % i, [128, 1024], F32)) for i in range(2)]
            wsc = ew(nc.sbuf_tensor("wsc", [128, 96], F32))
            wsc2 = ew(nc.sbuf_tensor("wsc2", [128, 32], F32))
            wsc3 = ew(nc.sbuf_tensor("wsc3", [128, 32], F32))
            wsc4 = ew(nc.sbuf_tensor("wsc4", [128, 32], F32))
            wkq = ew(nc.sbuf_tensor("wkq", [128, 1024], F32))
            wki = ew(nc.sbuf_tensor("wki", [128, 1024], I16))
            wdq = [ew(nc.sbuf_tensor("wdq%d" % i, [128, 1024], BF16)) for i in range(2)]
            agst = [ew(nc.sbuf_tensor("agst%d" % i, [128, 128], BF16)) for i in range(2)]
            psW = [ew(nc.psum_tensor("psW%d" % i, [128, 128], BF16)) for i in range(2)]

            wtiles = []
            for t_ in range(3):
                wtiles.append(("up", wup_blk[t_ * 128:(t_ + 1) * 128, :], t_))
            for t_ in range(3):
                wtiles.append(("gt", wgt_blk[t_ * 128:(t_ + 1) * 128, :], t_))
            for t_ in range(3):
                wtiles.append(("dn", wdn_blk[:, t_ * 1024:(t_ + 1) * 1024], t_))

            dq0, tr0, ev0 = (b.cnt[k] for k in ("w_dq", "w_tr", "w_ev"))
            wld_v = []
            ags_v = [(b.dsem("ags", k % 2),
                      b.cnt[b.dsem("ags", k % 2)] + 16 * (k // 2 + 1))
                     for k in range(72)]

            ln0 = b.cnt["w_ln"]
            rw0p = b.cnt["w_rw"]
            bn_sem = b.dsem("lnb", 1)
            bn0 = b.cnt[bn_sem]
            bn_v = [(bn_sem, bn0 + 16 * (i + 1)) for i in range(9)]
            bq_sem = b.dsem("lnb", 3)
            bq0 = b.cnt[bq_sem]
            bq_v = [(bq_sem, bq0 + 16 * (i + 1)) for i in range(9)]

            @block.sync
            def _(s):
                for i, (_k, src, _t) in enumerate(wtiles):
                    if i >= 2:
                        b.wait(s, "w_dq", dq0 + i - 1)
                    wld_v.append(b.dma(s, "wld", i % 2, out=wld[i % 2][:, :], in_=src))
                    if i >= 1:
                        b.wait(s, "w_ln", ln0 + i)
                        b.dma(s, "lnb", 1, out=wsc2[:, :], in_=wsc[:, 64:96])
                        b.wait(s, "w_rw", rw0p + i)
                        b.dma(s, "lnb", 3, out=wsc4[:, :], in_=wsc3[:, :])
                b.wait(s, "w_ln", ln0 + 9)
                b.dma(s, "lnb", 1, out=wsc2[:, :], in_=wsc[:, 64:96])
                b.wait(s, "w_rw", rw0p + 9)
                b.dma(s, "lnb", 3, out=wsc4[:, :], in_=wsc3[:, :])

            sw0 = b.cnt["w_sw"]
            rw0 = b.cnt["w_rw"]

            @block.vector
            def _(v):
                rb = wsc4[:, :].rearrange("p (bl el) -> p bl el", el=1) \
                    .broadcast_to([128, 32, 32])
                sbb = wsc[:, 32:64].rearrange("p (bl el) -> p bl el", el=1) \
                    .broadcast_to([128, 32, 32])
                kq3 = wkq[:, :].rearrange("p (bl el) -> p bl el", el=32)
                ki3 = wki[:, :].rearrange("p (bl el) -> p bl el", el=32)

                def dequant(j):
                    w3p = wld[j % 2][:, :].rearrange("p (bl el) -> p bl el", el=32)
                    dq3 = wdq[j % 2][:, :].rearrange("p (bl el) -> p bl el", el=32)
                    b.waitv(v, bq_v[j])
                    if j >= 2:
                        b.wait(v, "w_tr", tr0 + 8 * (j - 1))
                    v.tensor_tensor(wsc[:, 64:96], wsc[:, 32:64], wsc4[:, :],
                                    ALU.mult)
                    v.nop()
                    v.tensor_scalar(wsc[:, 64:96], wsc[:, 64:96], -1.0, 2.0,
                                    ALU.mult, ALU.add)
                    v.nop()
                    v.tensor_tensor(wsc4[:, :], wsc4[:, :], wsc[:, 64:96],
                                    ALU.mult)
                    v.nop()
                    v.tensor_tensor(kq3, w3p, rb, ALU.mult)
                    v.nop()
                    v.tensor_scalar(wki[:, :], wkq[:, :], 7.0, -8.0,
                                    ALU.min, ALU.max)
                    v.nop()
                    b.inc(v.tensor_tensor(dq3, ki3, sbb, ALU.mult), "w_dq")

                for i in range(9):
                    b.waitv(v, wld_v[i])
                    w3 = wld[i % 2][:, :].rearrange("p (bl el) -> p bl el", el=32)
                    v.reduce_max(wsc[:, 0:32], w3, AX.X, apply_absolute_value=True)
                    if i >= 1:
                        dequant(i - 1)
                    else:
                        v.reduce_sum(wsc[:, 64:65], wld[0][:, :], AX.X)
                        v.nop()
                        v.reduce_sum(wsc[:, 64:65], wld[0][:, :], AX.X)
                    b.inc(v.tensor_scalar(wsc[:, 32:64], wsc[:, 0:32], 1.0 / 7.0,
                                          1e-12, ALU.mult, ALU.max), "w_sw")
                dequant(8)

            @block.tensor
            def _(t):
                for i in range(9):
                    b.wait(t, "w_dq", dq0 + i + 1)
                    for ic in range(8):
                        k = i * 8 + ic
                        if k >= 2:
                            b.wait(t, "w_ev", ev0 + k - 1)
                        b.inc(t.transpose(psW[k % 2][:, :],
                                          wdq[i % 2][:, ic * 128:(ic + 1) * 128],
                                          cst["cst_ident"][:, :]), "w_tr")

            @block.scalar
            def _(a):
                for i in range(9):
                    b.wait(a, "w_sw", sw0 + i + 1)
                    b.inc(a.activation(wsc[:, 64:96], wsc[:, 32:64], AF.Ln), "w_ln")
                    b.waitv(a, bn_v[i])
                    b.inc(a.activation(wsc3[:, :], wsc2[:, :], AF.Exp,
                                       scale=-1.0), "w_rw")
                    for ic in range(8):
                        k = i * 8 + ic
                        b.wait(a, "w_tr", tr0 + k + 1)
                        if k >= 2:
                            b.waitv(a, ags_v[k - 2])
                        b.inc(a.copy(agst[k % 2][:, :], psW[k % 2][:, :]), "w_ev")

            @block.gpsimd
            def _(g):
                for i, (kind, _src, tt) in enumerate(wtiles):
                    for ic in range(8):
                        k = i * 8 + ic
                        b.wait(g, "w_ev", ev0 + k + 1)
                        if kind in ("up", "gt"):
                            base = (U_OFF if kind == "up" else G_OFF) \
                                + ic * 128 * 384 + tt * 128
                            dst = ag_in[base:base + 128 * 384] \
                                .rearrange("(p c) -> p c", c=384)[:, 0:128]
                        else:
                            icol = tt * 8 + ic
                            base = DN_OFF + icol * 128 * 128
                            dst = ag_in[base:base + 128 * 128] \
                                .rearrange("(p c) -> p c", c=128)
                        b.dma(g, "ags", k % 2,
                              out=dst, in_=agst[k % 2][:, :])
                for sv in b.dfinal("ags", 2):
                    b.waitv(g, sv)
                b.inc(g.collective_compute(
                    "AllGather", ALU.bypass, replica_groups=[list(range(NCORES))],
                    ins=[ag_in[:]], outs=[ag_out[:]]), "cc", 1)
            ag_done = b.cnt["cc"]

        psX_stack.close()

        pa_alloc = ExitStack()
        ea = pa_alloc.enter_context
        Wslot = [ea(nc.sbuf_tensor("Wsl0", [128, NIC * 1024], BF16))]
        k_fm = ea(nc.sbuf_tensor("k_fm", [128, H * T], BF16))
        v_tm = ea(nc.sbuf_tensor("v_tm", [128, NT * 1024], BF16))
        q_fm = ea(nc.sbuf_tensor("q_fm", [128, H * T], BF16))
        S_fp = ea(nc.sbuf_tensor("S_fp", [128, H * 128], F32))
        S_bf = ea(nc.sbuf_tensor("S_bf", [128, NCH * 1024], BF16))
        x1v = [S_bf[:, 0:1024], S_bf[:, 1024:2048]]
        for nm in ATT_CST:
            shp = list(cin[nm].tensor.shape)
            cst[nm] = ea(nc.sbuf_tensor(nm + "_s", shp, cdt.get(nm, F32)))

        @block.sync
        def _(s):
            for nm in ATT_CST:
                b.dma(s, "boot", 1, out=cst[nm][:, :], in_=cin[nm][:, :])
        att_cst_fin = b.dfinal("boot", 2)[1]
        x1n_tm = [ea(nc.sbuf_tensor("x1ntm%d" % i, [128, 1024], BF16))
                  for i in range(2)]
        x1n = ea(nc.sbuf_tensor("x1n_sh", [128, NIC * 1024], BF16))

        kadj = [ea(nc.sbuf_tensor("kadj%d" % i, [128, 128], BF16)) for i in range(2)]
        sc_bf = [ea(nc.sbuf_tensor("scbf%d" % i, [128, 512], BF16)) for i in range(2)]
        o_sb = [ea(nc.sbuf_tensor("osb%d" % i, [128, 1024], BF16)) for i in range(2)]
        T_bf = o_sb[0]
        rs_sb = [ea(nc.sbuf_tensor("rssb%d" % i, [128, 1024], F32)) for i in range(2)]
        E_scr = rs_sb[0]
        psT = [ea(nc.psum_tensor("psT%d" % i, [128, 128], BF16)) for i in range(2)]
        psK = [ea(nc.psum_tensor("psK%d" % i, [128, 512], F32)) for i in range(4)]
        psA = [ea(nc.psum_tensor("psA%d" % i, [128, 512], F32)) for i in range(2)]

        if _ph_on("wload"):
            wl_gd = []
            pjd0 = b.cnt["pj_done"]
            trev_pa0 = b.cnt["trev"]
            mm_pa0 = b.cnt["pj_mm"]
            x1l_v = {}

            wnames = ["Wk", "Wv", "Wq", "Wk", "Wv", "Wq", "Wo"]
            if not _ph_on("proj"):
                wnames = wnames[:1]
            elif not _ph_on("ophase"):
                wnames = wnames[:6]

            @block.gpsimd
            def _(g):
                for p, nm in enumerate(wnames):
                    if p >= 1:
                        b.wait(g, "pj_done", pjd0 + p)
                    src = Wmap[nm][:, :].rearrange("(ic p) o -> p ic o", p=128)
                    dst = Wslot[0][:, :].rearrange("p (ic o) -> p ic o", o=1024)
                    wl_gd.append(b.dma(g, "wsl", 0, out=dst, in_=src))
                    if nm == "Wv" and _ph_on("proj"):
                        w_ = p // 3
                        for j in range(8):
                            b.wait(g, "trev", trev_pa0 + 64 * w_ + 8 * (j + 1))
                            if j >= 2:
                                b.wait(g, "pj_mm",
                                       mm_pa0 + 384 * w_ + 128 + 16 * (j - 1))
                            lsrc = x1n[:, :].rearrange(
                                "p (ic jj el) -> p ic jj el", jj=8, el=128)[:, :, j, :]
                            ldst = x1v[j % 2][:, :].rearrange(
                                "p (ic el) -> p ic el", el=128)
                            x1l_v[(w_, j)] = b.dma(g, "x1l", j % 2,
                                                   out=ldst, in_=lsrc)

            nwin = 2 if _ph_on("proj") else (1 if _ph_on("ln1") else 0)
            for w in range(nwin):

                st10 = b.cnt["ln_st1"]
                sq0 = b.cnt["ln_sq"]
                nm0 = b.cnt["ln_nm"]
                cp0 = b.cnt["ln_cp"]
                tr0 = b.cnt["tr"]
                tev0 = b.cnt["trev"]
                xv = []
                xv2 = []
                lnv = []

                @block.sync
                def _(s):
                    if w == 1:
                        b.wait(s, "pj_done", pjd0 + 3)
                        b.wait(s, "ln_cp", cp0)
                    for j in range(8):
                        tt = w * 8 + j
                        if j >= 3:
                            b.wait(s, "ln_st1", st10 + j - 2)
                            b.wait(s, "ln_sq", sq0 + j - 2)
                        xv.append(b.dma(s, "xp", j % 3, out=xpool[j % 3][:, :],
                                        in_=xin[tt * 128:(tt + 1) * 128, :]))

                @block.scalar
                def _(a):
                    for j in range(8):
                        b.waitv(a, xv[j])
                        xt = xpool[j % 3]
                        a.activation(sqd[:, :], xt[:, 0:512], AF.Square,
                                     accum_out=stat[:, 8 + j:9 + j])
                        b.inc(a.activation(sqd[:, :], xt[:, 512:1024], AF.Square,
                                           accum_out=stat[:, 16 + j:17 + j]),
                              "ln_sq")

                @block.vector
                def _(v):
                    for j in range(8):
                        b.waitv(v, xv[j])
                        b.inc(v.reduce_sum(stat[:, j:j + 1], xpool[j % 3][:, :],
                                           AX.X), "ln_st1")
                    b.wait(v, "ln_sq", sq0 + 8)
                    v.tensor_tensor(stat[:, 8:16], stat[:, 8:16], stat[:, 16:24],
                                    ALU.add)
                    v.nop()
                    v.tensor_scalar(stat[:, 8:16], stat[:, 8:16], 1.0 / D, None,
                                    ALU.mult)
                    v.tensor_scalar(stat[:, 24:32], stat[:, 0:8], 1.0 / D, None,
                                    ALU.mult)
                    v.nop()
                    v.tensor_tensor(stat[:, 0:8], stat[:, 24:32], stat[:, 24:32],
                                    ALU.mult)
                    v.nop()
                    v.scalar_tensor_tensor(stat[:, 8:16], stat[:, 0:8], -1.0,
                                           stat[:, 8:16], ALU.mult, ALU.add)
                    v.nop()
                    b.inc(v.tensor_scalar(stat[:, 8:16], stat[:, 8:16], 1e-5, None,
                                          ALU.add), "ln_nm")

                @block.scalar
                def _(a):
                    b.wait(a, "ln_nm", nm0 + 1)
                    b.inc(a.activation(lnb[0][:, 0:8], stat[:, 8:16], AF.Ln), "ln_sq")

                @block.sync
                def _(s):
                    b.wait(s, "ln_sq", sq0 + 9)
                    lnv.append(b.dma(s, "lnb", 0, out=lnb[1][:, 0:8],
                                     in_=lnb[0][:, 0:8]))

                @block.scalar
                def _(a):
                    b.waitv(a, lnv[-1])
                    b.inc(a.activation(lnb[0][:, 8:16], lnb[1][:, 0:8], AF.Exp,
                                       scale=-0.5), "ln_sq")

                @block.sync
                def _(s):
                    b.wait(s, "ln_sq", sq0 + 10)
                    lnv.append(b.dma(s, "lnb", 0, out=lnb[1][:, 8:16],
                                     in_=lnb[0][:, 8:16]))
                    for j in range(8):
                        tt = w * 8 + j
                        if j >= 3:
                            b.wait(s, "ln_cp", cp0 + j - 2)
                        xv2.append(b.dma(s, "xp", j % 3, out=xpool[j % 3][:, :],
                                         in_=xin[tt * 128:(tt + 1) * 128, :]))

                @block.vector
                def _(v):
                    b.waitv(v, lnv[-1])
                    b.inc(v.scalar_tensor_tensor(stat[:, 24:32], stat[:, 24:32],
                                                 -1.0, lnb[1][:, 8:16],
                                                 ALU.mult, ALU.mult), "ln_nm")
                    v.nop()
                    for j in range(8):
                        b.waitv(v, xv2[j])
                        if w > 0 or j >= 2:
                            b.wait(v, "tr", tr0 + 8 * (j - 1))
                        xt = xpool[j % 3]
                        ngb = stat[:, 24 + j:25 + j] \
                            .rearrange("p (r el) -> p r el", el=1) \
                            .broadcast_to([128, 1024, 1])
                        x3 = xt[:, :].rearrange("p (r el) -> p r el", el=1)
                        o3 = x1n_tm[j % 2][:, :].rearrange("p (r el) -> p r el", el=1)
                        b.inc(v.scalar_tensor_tensor(
                            o3, x3, lnb[1][:, 8 + j:9 + j], ngb,
                            ALU.mult, ALU.add), "ln_cp")

                @block.tensor
                def _(t):
                    for j in range(8):
                        b.wait(t, "ln_cp", cp0 + j + 1)
                        for ic in range(NIC):
                            kk = j * NIC + ic
                            if kk >= 2:
                                b.wait(t, "trev", tev0 + kk - 1)
                            b.inc(t.transpose(psT[kk % 2][:, :],
                                              x1n_tm[j % 2][:, ic * 128:(ic + 1) * 128],
                                              cst["cst_ident"][:, :]), "tr")

                @block.scalar
                def _(a):
                    for j in range(8):
                        for ic in range(NIC):
                            kk = j * NIC + ic
                            b.wait(a, "tr", tr0 + kk + 1)
                            b.inc(a.copy(x1n[:, ic * 1024 + j * 128:
                                             ic * 1024 + (j + 1) * 128],
                                         psT[kk % 2][:, :]), "trev")

                win_ready = tev0 + 64

                if dbg == "x1n" and w == 0:
                    x1nsnap = b.snap()

                    @block.gpsimd
                    def _(g):
                        b.wait_snap(g, x1nsnap)
                        dv = dbg_out[:, :].rearrange("(ic p) t -> p ic t", p=128)
                        sv = x1n[:, :].rearrange("p (ic t) -> p ic t", t=1024)
                        b.dma(g, "dbg", 0, out=dv, in_=sv)

                if not _ph_on("proj"):
                    continue

                mm0 = b.cnt["pj_mm"]
                ev0p = b.cnt["pj_ev"]

                @block.tensor
                def _(t):
                    b.wait(t, "trev", win_ready)
                    b.waitv(t, wl_gd[3 * w + 0])
                    for oc in range(H):
                        if oc >= 2:
                            b.wait(t, "pj_ev", ev0p + 2 * (oc - 1))
                        bank = (oc % 2) * 2
                        for ic in range(NIC):
                            for tb in range(2):
                                b.inc(t.matmul(
                                    psK[bank + tb][:, :],
                                    Wslot[0][:, ic * 1024 + oc * 128:
                                                       ic * 1024 + (oc + 1) * 128],
                                    x1n[:, ic * 1024 + tb * 512:
                                        ic * 1024 + (tb + 1) * 512],
                                    start=(ic == 0), stop=(ic == NIC - 1)), "pj_mm")
                    b.inc(t.nop(), "pj_done")

                @block.scalar
                def _(a):
                    for oc in range(H):
                        bank = (oc % 2) * 2
                        for tb in range(2):
                            b.wait(a, "pj_mm", mm0 + oc * 16 + 15 + tb)
                            dst = k_fm[:, oc * T + w * 1024 + tb * 512:
                                       oc * T + w * 1024 + (tb + 1) * 512]
                            b.inc(a.copy(dst, psK[bank + tb][:, :]), "pj_ev")

                mm1 = b.cnt["pj_mm"]
                ev1 = b.cnt["pj_ev"]

                @block.tensor
                def _(t):
                    b.waitv(t, wl_gd[3 * w + 1])
                    b.wait(t, "pj_ev", ev1)
                    for j in range(8):
                        b.waitv(t, x1l_v[(w, j)])
                        if j >= 2:
                            b.wait(t, "pj_ev", ev1 + 2 * (j - 1))
                        bank = (j % 2) * 2
                        for ic in range(NIC):
                            for hf in range(2):
                                b.inc(t.matmul(
                                    psK[bank + hf][:, :],
                                    x1v[j % 2][:, ic * 128:(ic + 1) * 128],
                                    Wslot[0][:, ic * 1024 + hf * 512:
                                                           ic * 1024 + (hf + 1) * 512],
                                    start=(ic == 0), stop=(ic == NIC - 1)), "pj_mm")
                    b.inc(t.nop(), "pj_done")

                @block.scalar
                def _(a):
                    for j in range(8):
                        tt = w * 8 + j
                        bank = (j % 2) * 2
                        for hf in range(2):
                            b.wait(a, "pj_mm", mm1 + j * 16 + 15 + hf)
                            dst = v_tm[:, tt * 1024 + hf * 512:
                                       tt * 1024 + (hf + 1) * 512]
                            b.inc(a.copy(dst, psK[bank + hf][:, :]), "pj_ev")

                mm2 = b.cnt["pj_mm"]
                ev2 = b.cnt["pj_ev"]

                @block.tensor
                def _(t):
                    b.waitv(t, wl_gd[3 * w + 2])
                    b.wait(t, "pj_ev", ev2)
                    for oc in range(H):
                        if oc >= 2:
                            b.wait(t, "pj_ev", ev2 + 2 * (oc - 1))
                        bank = (oc % 2) * 2
                        for ic in range(NIC):
                            for tb in range(2):
                                b.inc(t.matmul(
                                    psK[bank + tb][:, :],
                                    Wslot[0][:, ic * 1024 + oc * 128:
                                                           ic * 1024 + (oc + 1) * 128],
                                    x1n[:, ic * 1024 + tb * 512:
                                        ic * 1024 + (tb + 1) * 512],
                                    start=(ic == 0), stop=(ic == NIC - 1)), "pj_mm")
                    b.inc(t.nop(), "pj_done")

                @block.vector
                def _(v):
                    if w == 0:
                        b.waitv(v, att_cst_fin)
                    for oc in range(H):
                        dqb = cst["cst_dq"][:, oc * 128:(oc + 1) * 128] \
                            .rearrange("p (r el) -> p r el", el=128) \
                            .broadcast_to([128, 4, 128])
                        bank = (oc % 2) * 2
                        for tb in range(2):
                            b.wait(v, "pj_mm", mm2 + oc * 16 + 15 + tb)
                            dst = q_fm[:, oc * T + w * 1024 + tb * 512:
                                       oc * T + w * 1024 + (tb + 1) * 512] \
                                .rearrange("p (r el) -> p r el", el=128)
                            src = psK[bank + tb][:, :] \
                                .rearrange("p (r el) -> p r el", el=128)
                            b.inc(v.tensor_tensor(dst, src, dqb, ALU.mult), "pj_ev")

        if _ph_on("schain"):
            kj0 = b.cnt["sc_kj"]
            st0 = b.cnt["sc_st"]
            sn0 = b.cnt["sc_sn"]
            smm0 = b.cnt["sc_mm"]
            str0 = b.cnt["sc_tr"]

            @block.vector
            def _(v):
                b.wait(v, "pj_done", b.cnt["pj_done"])
                v.memset(S_fp[:, :], 0.0)
                b.inc(v.memset(S_bf[:, 0:1024], 0.0), "sc_kj")
            sci = b.cnt["sc_kj"]

            @block.tensor
            def _(t):
                b.wait(t, "sc_kj", sci)
                for c in range(NCH):
                    for h in range(H):
                        k = c * H + h
                        if k >= 2:
                            b.wait(t, "sc_kj", sci + k - 1)
                        b.inc(t.transpose(psT[k % 2][:, :],
                                          k_fm[:, h * T + c * 128:h * T + (c + 1) * 128],
                                          cst["cst_ident"][:, :]), "sc_tr")
                        b.wait(t, "sc_kj", sci + k + 1)
                        if k >= 8:
                            b.wait(t, "sc_st", st0 + 4 * (k // 4 - 1))
                        b.inc(t.matmul(
                            psA[(k // 4) % 2][:, (k % 4) * 128:(k % 4 + 1) * 128],
                            kadj[k % 2][:, :],
                            v_tm[:, c * 1024 + h * 128:c * 1024 + (h + 1) * 128],
                            start=True, stop=True), "sc_mm")

            @block.vector
            def _(v):
                for c in range(NCH):
                    for h in range(H):
                        k = c * H + h
                        b.wait(v, "sc_tr", str0 + k + 1)
                        if k >= 2:
                            b.wait(v, "sc_mm", smm0 + k - 1)
                        b.inc(v.tensor_scalar(kadj[k % 2][:, :], psT[k % 2][:, :],
                                              cst["cst_dk"][:, h:h + 1], None,
                                              ALU.mult), "sc_kj")
                        b.wait(v, "sc_mm", smm0 + k + 1)
                        if k >= 8:
                            b.wait(v, "sc_sn", sn0 + k - 8 + 1)
                        b.inc(v.scalar_tensor_tensor(
                            S_fp[:, h * 128:(h + 1) * 128],
                            S_fp[:, h * 128:(h + 1) * 128], ds_py[h],
                            psA[(k // 4) % 2][:, (k % 4) * 128:(k % 4 + 1) * 128],
                            ALU.mult, ALU.add), "sc_st")

            @block.scalar
            def _(a):
                for c in range(NCH - 1):
                    for h in range(H):
                        k = c * H + h
                        b.wait(a, "sc_st", st0 + k + 1)
                        b.inc(a.copy(S_bf[:, (c + 1) * 1024 + h * 128:
                                         (c + 1) * 1024 + (h + 1) * 128],
                                     S_fp[:, h * 128:(h + 1) * 128]), "sc_sn")
            schain_st = b.cnt["sc_st"]
            schain_sn = b.cnt["sc_sn"]

            @block.vector
            def _(v):
                b.wait(v, "sc_st", schain_st)
                b.wait(v, "msc", 1)
                b.inc(v.tensor_scalar(E_scr[:, :], S_fp[:, :], col_msk[:, 0:1], None,
                                      ALU.mult), "corr")

            hand_v = [None]

            @block.sync
            def _(s):
                b.wait(s, "corr", b.cnt["corr"])
                hand_v[0] = b.dma(s, "ar", 0, out=ccp_in[:, :], in_=E_scr[:, :])

            @block.gpsimd
            def _(g):
                b.waitv(g, hand_v[0])
                b.inc(g.collective_compute(
                    "AllReduce", ALU.add,
                    replica_groups=[[0, 1], [2, 3], [4, 5], [6, 7]],
                    ins=[ccp_in[:, :]], outs=[ccp_out[:, :]]), "cc", 1)
            hand_cc = b.cnt["cc"]

            @block.sync
            def _(s):
                b.wait(s, "cc", hand_cc)
                hand_v[0] = b.dma(s, "ar", 0, out=E_scr[:, :], in_=ccp_out[:, :])

            @block.vector
            def _(v):
                b.waitv(v, hand_v[0])
                b.wait(v, "sc_sn", schain_sn)
                b.inc(v.tensor_scalar(T_bf[:, :], E_scr[:, :], col_msk[:, 1:2], None,
                                      ALU.mult), "corr")
                for c in range(NCH):
                    v.nop()
                    b.inc(v.tensor_tensor(S_bf[:, c * 1024:(c + 1) * 1024],
                                          S_bf[:, c * 1024:(c + 1) * 1024],
                                          T_bf[:, :], ALU.add), "corr")
                    if c < NCH - 1:
                        v.nop()
                        b.inc(v.tensor_tensor(T_bf[:, :], T_bf[:, :],
                                              cst["cst_ds"][:, :], ALU.mult), "corr")
            corr_done = b.cnt["corr"]

        if _ph_on("ophase"):
            osc0 = b.cnt["oph_sc"]
            odm0 = b.cnt["oph_dm"]
            oo0 = b.cnt["oph_o"]
            ooe0 = b.cnt["oph_oe"]
            opm0 = b.cnt["op_mm"]
            ore0 = b.cnt["op_re"]
            xv5 = []
            rsp_v = [(b.dsem("rsp", c % 2),
                      b.cnt[b.dsem("rsp", c % 2)] + 16 * (c // 2 + 1))
                     for c in range(NCH)]

            @block.sync
            def _(s):
                b.wait(s, "pj_done", pjd0 + 6)
                for c in range(NCH):
                    if c >= 3:
                        b.wait(s, "op_re", ore0 + c - 2)
                    xv5.append(b.dma(s, "xp", c % 3, out=xpool[c % 3][:, :],
                                     in_=xin[c * 128:(c + 1) * 128, :]))

            @block.gpsimd
            def _(g):
                for c in range(NCH):
                    b.wait(g, "op_re", ore0 + c + 1)
                    b.dma(g, "rsp", c % 2,
                          out=resid_hbm[c * 128:(c + 1) * 128, :],
                          in_=rs_sb[c % 2][:, :])

            @block.tensor
            def _(t):
                b.wait(t, "corr", corr_done)
                b.waitv(t, wl_gd[6])
                for c in range(NCH):
                    for g_ in range(2):
                        if c >= 1:
                            b.wait(t, "oph_dm", odm0 + 2 * (c - 1) + g_ + 1)
                        for hh in range(4):
                            h = g_ * 4 + hh
                            b.inc(t.matmul(
                                psA[g_][:, hh * 128:(hh + 1) * 128],
                                k_fm[:, h * T + c * 128:h * T + (c + 1) * 128],
                                q_fm[:, h * T + c * 128:h * T + (c + 1) * 128],
                                start=True, stop=True), "oph_sc")
                    for g_ in range(2):
                        b.wait(t, "oph_dm", odm0 + 2 * c + g_ + 1)
                        if c >= 1:
                            b.wait(t, "oph_oe", ooe0 + 2 * (c - 1) + g_ + 1)
                        for hh in range(4):
                            h = g_ * 4 + hh
                            b.inc(t.matmul(
                                psK[g_][:, hh * 128:(hh + 1) * 128],
                                S_bf[:, c * 1024 + h * 128:c * 1024 + (h + 1) * 128],
                                q_fm[:, h * T + c * 128:h * T + (c + 1) * 128],
                                start=True, stop=False), "oph_o")
                            b.inc(t.matmul(
                                psK[g_][:, hh * 128:(hh + 1) * 128],
                                v_tm[:, c * 1024 + h * 128:c * 1024 + (h + 1) * 128],
                                sc_bf[g_][:, hh * 128:(hh + 1) * 128],
                                start=False, stop=True), "oph_o")
                    b.wait(t, "oph_oe", ooe0 + 2 * c + 2)
                    if c >= 1:
                        b.wait(t, "op_re", ore0 + c)
                    for h in range(H):
                        for hf in range(2):
                            b.inc(t.matmul(
                                psK[2 + hf][:, :],
                                o_sb[c % 2][:, h * 128:(h + 1) * 128],
                                Wslot[0][:, h * 1024 + hf * 512:
                                       h * 1024 + (hf + 1) * 512],
                                start=(h == 0), stop=(h == H - 1)), "op_mm")

            @block.vector
            def _(v):
                for c in range(NCH):
                    for g_ in range(2):
                        kk = 2 * c + g_
                        b.wait(v, "oph_sc", osc0 + 4 * kk + 4)
                        b.inc(v.tensor_tensor(
                            sc_bf[g_][:, :], psA[g_][:, :],
                            cst["cst_dmask"][:, g_ * 512:(g_ + 1) * 512],
                            ALU.mult), "oph_dm")
                    b.wait(v, "op_mm", opm0 + 16 * (c + 1))
                    b.waitv(v, xv5[c])
                    if c >= 2:
                        b.waitv(v, rsp_v[c - 2])
                    for hf in range(2):
                        inst = v.tensor_tensor(
                            rs_sb[c % 2][:, hf * 512:(hf + 1) * 512],
                            psK[2 + hf][:, :],
                            xpool[c % 3][:, hf * 512:(hf + 1) * 512], ALU.add)
                        if hf == 1:
                            b.inc(inst, "op_re")

            @block.scalar
            def _(a):
                for c in range(NCH):
                    for g_ in range(2):
                        kk = 2 * c + g_
                        b.wait(a, "oph_o", oo0 + 8 * (kk + 1))
                        if c >= 2:
                            b.wait(a, "op_mm", opm0 + 16 * (c - 1))
                        b.inc(a.copy(o_sb[c % 2][:, g_ * 512:(g_ + 1) * 512],
                                     psK[g_][:, :]), "oph_oe")
            ophase_done = b.snap()

        if dbg in ("x1n", "kfm", "vtm", "qfm", "sbf", "resid"):
            snap = b.snap()

            @block.gpsimd
            def _(g):
                b.wait_snap(g, snap)
                if dbg == "kfm":
                    dv = dbg_out[:, :].rearrange("(h p) t -> p h t", p=128)
                    sv = k_fm[:, :].rearrange("p (h t) -> p h t", t=T)
                    b.dma(g, "dbg", 0, out=dv, in_=sv)
                elif dbg == "qfm":
                    dv = dbg_out[:, :].rearrange("(h p) t -> p h t", p=128)
                    sv = q_fm[:, :].rearrange("p (h t) -> p h t", t=T)
                    b.dma(g, "dbg", 0, out=dv, in_=sv)
                elif dbg == "vtm":
                    dv = dbg_out[:, :].rearrange("(tt p) e -> p tt e", p=128)
                    sv = v_tm[:, :].rearrange("p (tt e) -> p tt e", e=1024)
                    b.dma(g, "dbg", 0, out=dv, in_=sv)
                elif dbg == "sbf":
                    b.dma(g, "dbg", 0, out=dbg_out[:, :], in_=S_bf[:, :])
                elif dbg == "resid":
                    b.dma(g, "dbg", 0, out=dbg_out[:, :], in_=resid_hbm[:, :])

        if not _ph_on("ln2"):
            _finish(nc, b, block, out)
            pa_alloc.close()
            return nc

        pa_done = b.snap()
        pa_alloc.close()
        pb = ExitStack()
        eb = pb.enter_context
        psXh["t"] = eb(nc.psum_tensor("psX2", [128, 512], F32))
        x2q_fm = eb(nc.sbuf_tensor("x2q_fm", [128, NIC * T], BF16))
        accU = eb(nc.sbuf_tensor("accU", [128, 2048], BF16))
        accG = eb(nc.sbuf_tensor("accG", [128, 2048], BF16))
        spb = [eb(nc.sbuf_tensor("spb%d" % i, [128, 2048], BF16)) for i in range(4)]
        psF = [eb(nc.psum_tensor("psF%d" % i, [128, 512], F32)) for i in range(4)]
        pwt = ExitStack()
        wT_up = pwt.enter_context(nc.sbuf_tensor("wT_up", [128, NIC * FF], BF16))
        wT_gt = pwt.enter_context(nc.sbuf_tensor("wT_gt", [128, NIC * FF], BF16))

        @block.sync
        def _(s):
            b.wait(s, "cc", ag_done)
            b.wait_snap(s, pa_done)
            for blk in range(NCORES):
                src = ag_out[blk * AGN + U_OFF:blk * AGN + U_OFF + 393216] \
                    .rearrange("(ic p o) -> p ic o", p=128, o=384)
                dst = wT_up[:, :].rearrange("p (ic og) -> p ic og", og=FF) \
                    [:, :, blk * 384:(blk + 1) * 384]
                b.dma(s, "wtl", 0, out=dst, in_=src)
                src = ag_out[blk * AGN + G_OFF:blk * AGN + G_OFF + 393216] \
                    .rearrange("(ic p o) -> p ic o", p=128, o=384)
                dst = wT_gt[:, :].rearrange("p (ic og) -> p ic og", og=FF) \
                    [:, :, blk * 384:(blk + 1) * 384]
                b.dma(s, "wtl", 0, out=dst, in_=src)
        wt_loaded = b.dfinal("wtl", 1)[0]

        if dbg == "wt":
            @block.gpsimd
            def _(g):
                b.waitv(g, wt_loaded)
                b.dma(g, "dbg", 0, out=dbg_out[:, :], in_=wT_up[:, :])

        with ExitStack() as pln:
            el = pln.enter_context
            x2n = el(nc.sbuf_tensor("x2n", [128, NT * 1024], BF16))

            st10 = b.cnt["ln_st1"]
            sq0 = b.cnt["ln_sq"]
            nm0 = b.cnt["ln_nm"]
            cp0 = b.cnt["ln_cp"]
            rsp_fin = b.dfinal("rsp", 2)
            xv = []
            xv2 = []
            lnv = []

            @block.sync
            def _(s):
                for sv in rsp_fin:
                    b.waitv(s, sv)
                for j in range(NT):
                    if j >= 3:
                        b.wait(s, "ln_st1", st10 + j - 2)
                        b.wait(s, "ln_sq", sq0 + j - 2)
                    xv.append(b.dma(s, "xp", j % 3, out=xpool[j % 3][:, :],
                                    in_=resid_hbm[j * 128:(j + 1) * 128, :]))

            @block.scalar
            def _(a):
                for j in range(NT):
                    b.waitv(a, xv[j])
                    xt = xpool[j % 3]
                    a.activation(sqd[:, :], xt[:, 0:512], AF.Square,
                                 accum_out=stat[:, 16 + j:17 + j])
                    b.inc(a.activation(sqd[:, :], xt[:, 512:1024], AF.Square,
                                       accum_out=stat[:, 32 + j:33 + j]), "ln_sq")

            @block.vector
            def _(v):
                for j in range(NT):
                    b.waitv(v, xv[j])
                    b.inc(v.reduce_sum(stat[:, j:j + 1], xpool[j % 3][:, :], AX.X),
                          "ln_st1")
                b.wait(v, "ln_sq", sq0 + NT)
                if dbg == "lnst2":
                    b.wait(v, "gd", 1)
                v.tensor_tensor(stat[:, 16:32], stat[:, 16:32], stat[:, 32:48],
                                ALU.add)
                v.nop()
                v.tensor_scalar(stat[:, 16:32], stat[:, 16:32], 1.0 / D, None,
                                ALU.mult)
                v.tensor_scalar(stat[:, 48:64], stat[:, 0:16], 1.0 / D, None,
                                ALU.mult)
                v.nop()
                v.tensor_tensor(stat[:, 0:16], stat[:, 48:64], stat[:, 48:64],
                                ALU.mult)
                v.nop()
                v.scalar_tensor_tensor(stat[:, 16:32], stat[:, 0:16], -1.0,
                                       stat[:, 16:32], ALU.mult, ALU.add)
                v.nop()
                b.inc(v.tensor_scalar(stat[:, 16:32], stat[:, 16:32], 1e-5, None,
                                      ALU.add), "ln_nm")

            @block.scalar
            def _(a):
                b.wait(a, "ln_nm", nm0 + 1)
                if dbg == "lnst2":
                    b.wait(a, "gd", 2)
                b.inc(a.activation(lnb[0][:, 0:16], stat[:, 16:32], AF.Ln), "ln_sq")

            @block.sync
            def _(s):
                b.wait(s, "ln_sq", sq0 + NT + 1)
                lnv.append(b.dma(s, "lnb", 0, out=lnb[1][:, 0:16],
                                 in_=lnb[0][:, 0:16]))

            @block.scalar
            def _(a):
                b.waitv(a, lnv[-1])
                b.inc(a.activation(lnb[0][:, 16:32], lnb[1][:, 0:16], AF.Exp,
                                   scale=-0.5), "ln_sq")

            @block.sync
            def _(s):
                b.wait(s, "ln_sq", sq0 + NT + 2)
                lnv.append(b.dma(s, "lnb", 0, out=lnb[1][:, 16:32],
                                 in_=lnb[0][:, 16:32]))
                for j in range(NT):
                    if j >= 3:
                        b.wait(s, "ln_cp", cp0 + j - 2)
                    xv2.append(b.dma(s, "xp", j % 3, out=xpool[j % 3][:, :],
                                     in_=resid_hbm[j * 128:(j + 1) * 128, :]))

            @block.vector
            def _(v):
                b.waitv(v, lnv[-1])
                b.inc(v.scalar_tensor_tensor(stat[:, 48:64], stat[:, 48:64],
                                             -1.0, lnb[1][:, 16:32],
                                             ALU.mult, ALU.mult), "ln_nm")
                v.nop()
                for j in range(NT):
                    b.waitv(v, xv2[j])
                    xt = xpool[j % 3]
                    scr = x2n[:, j * 1024:(j + 1) * 1024]
                    ngb = stat[:, 48 + j:49 + j] \
                        .rearrange("p (r el) -> p r el", el=1) \
                        .broadcast_to([128, 1024, 1])
                    x3 = xt[:, :].rearrange("p (r el) -> p r el", el=1)
                    o3 = scr.rearrange("p (r el) -> p r el", el=1)
                    b.inc(v.scalar_tensor_tensor(
                        o3, x3, lnb[1][:, 16 + j:17 + j], ngb,
                        ALU.mult, ALU.add), "ln_cp")
                    v.nop()
                    v.reduce_max(stat[:, 0:1], scr, AX.X,
                                 apply_absolute_value=True)
                    v.nop()
                    b.inc(v.tensor_tensor(mxcol[:, 0:1], mxcol[:, 0:1],
                                          stat[:, 0:1], ALU.max), "msc")
            ln2_done = b.snap()

            if dbg == "x2n":
                @block.gpsimd
                def _(g):
                    b.wait_snap(g, ln2_done)
                    dv = dbg_out[:, :].rearrange("(tt p) i -> p tt i", p=128)
                    sv = x2n[:, :].rearrange("p (tt i) -> p tt i", i=1024)
                    b.dma(g, "dbg", 0, out=dv, in_=sv)

            if dbg == "lnst2":
                @block.gpsimd
                def _(g):
                    b.wait(g, "ln_sq", sq0 + NT)
                    b.wait(g, "ln_st1", st10 + NT)
                    dv1 = b.dma(g, "dbg", 0, out=dbg_out[:, 0:48],
                                in_=stat[:, 16:64])
                    b.waitv(g, dv1)
                    b.inc(g.nop(), "gd")
                    b.wait(g, "ln_nm", nm0 + 1)
                    dv2 = b.dma(g, "dbg", 0, out=dbg_out[:, 48:80],
                                in_=stat[:, 16:48])
                    dv3 = b.dma(g, "dbg", 0, out=dbg_out[:, 80:96],
                                in_=stat[:, 0:16])
                    b.waitv(g, dv3)
                    b.inc(g.nop(), "gd")
                    b.wait(g, "ln_sq", sq0 + NT + 2)
                    b.dma(g, "dbg", 0, out=dbg_out[:, 96:112],
                          in_=lnb[0][:, 16:32])
                    b.dma(g, "dbg", 0, out=dbg_out[:, 112:128],
                          in_=lnb[1][:, 16:32])

            if dbg == "lnst":
                @block.gpsimd
                def _(g):
                    b.wait_snap(g, ln2_done)
                    b.dma(g, "dbg", 0, out=dbg_out[:, 0:64], in_=stat[:, :])
                    b.dma(g, "dbg", 0, out=dbg_out[:, 64:96], in_=lnb[0][:, :])
                    b.dma(g, "dbg", 0, out=dbg_out[:, 96:128], in_=lnb[1][:, :])

            if not _ph_on("x2q"):
                pln.close()
                pwt.close()
                _finish(nc, b, block, out)
                pb.close()
                return nc

            finalize_max(0)
            run_max_ar(0, 0, 1, "s1")

            @block.vector
            def _(v):
                b.wait(v, "msc", b.cnt["msc"])
                b.inc(v.tensor_scalar(row_scr[0:1, S1:S1 + 1], row_scr[0:1, 0:1],
                                      1.0 / 32767.0, 1e-12, ALU.mult, ALU.max),
                      "msc")
            row_recip(S1, R1, 15)
            bcast_cols("s1r1", S1, 2, S1)
            s1_ready = b.cnt["msc"]

            with ExitStack() as px:
                ex = px.enter_context
                xq16 = [ex(nc.sbuf_tensor("xq16%d" % i, [128, 1024], I16))
                        for i in range(2)]
                xqbf = [ex(nc.sbuf_tensor("xqbf%d" % i, [128, 1024], BF16))
                        for i in range(2)]
                psT2 = [ex(nc.psum_tensor("psT2%d" % i, [128, 128], BF16))
                        for i in range(2)]

                ffq0 = b.cnt["ffq"]
                tr0 = b.cnt["tr"]
                tev0 = b.cnt["trev"]

                @block.vector
                def _(v):
                    b.wait(v, "msc", s1_ready)
                    for j in range(NT):
                        if j >= 2:
                            b.wait(v, "tr", tr0 + 8 * (j - 1))
                        v.tensor_scalar(xq16[j % 2][:, :],
                                        x2n[:, j * 1024:(j + 1) * 1024],
                                        col_scl[:, R1:R1 + 1], None, ALU.mult)
                        v.nop()
                        b.inc(v.tensor_scalar(xqbf[j % 2][:, :], xq16[j % 2][:, :],
                                              col_scl[:, S1:S1 + 1], None, ALU.mult),
                              "ffq")

                @block.tensor
                def _(t):
                    for j in range(NT):
                        b.wait(t, "ffq", ffq0 + j + 1)
                        for ic in range(NIC):
                            kk = j * NIC + ic
                            if kk >= 2:
                                b.wait(t, "trev", tev0 + kk - 1)
                            b.inc(t.transpose(psT2[kk % 2][:, :],
                                              xqbf[j % 2][:, ic * 128:(ic + 1) * 128],
                                              cst["cst_ident"][:, :]), "tr")

                @block.scalar
                def _(a):
                    for j in range(NT):
                        for ic in range(NIC):
                            kk = j * NIC + ic
                            b.wait(a, "tr", tr0 + kk + 1)
                            b.inc(a.copy(x2q_fm[:, ic * T + j * 128:
                                             ic * T + (j + 1) * 128],
                                         psT2[kk % 2][:, :]), "trev")
                x2q_done = b.snap()

            pln.close()

        if dbg == "x2q":
            @block.gpsimd
            def _(g):
                b.wait_snap(g, x2q_done)
                dv = dbg_out[:, :].rearrange("(ic p) t -> p ic t", p=128)
                sv = x2q_fm[:, :].rearrange("p (ic t) -> p ic t", t=T)
                b.dma(g, "dbg", 0, out=dv, in_=sv)

        if not _ph_on("alpha"):
            pwt.close()
            _finish(nc, b, block, out)
            pb.close()
            return nc

        fam0 = b.cnt["ffa_mm"]
        fae0 = b.cnt["ffa_ev"]
        x2qr = b.cnt["trev"]
        spl_v = [(b.dsem("spl", u % 4),
                  b.cnt[b.dsem("spl", u % 4)] + 16 * (u // 4 + 1))
                 for u in range(2 * NOC)]

        @block.vector
        def _(v):
            v.memset(accU[:, :], 0.0)
            b.inc(v.memset(accG[:, :], 0.0), "ffa_mx")
        famx_base = b.cnt["ffa_mx"]

        @block.tensor
        def _(t):
            b.wait(t, "trev", x2qr)
            b.waitv(t, wt_loaded)
            for u in range(2 * NOC):
                oc, wsel = u // 2, u % 2
                wt = wT_up if wsel == 0 else wT_gt
                for tbp in range(2):
                    su = 2 * u + tbp
                    grp = (su % 2) * 2
                    if su >= 2:
                        b.wait(t, "ffa_ev", fae0 + 2 * (su - 1))
                    for ic in range(NIC):
                        for tb2 in range(2):
                            tb = tbp * 2 + tb2
                            b.inc(t.matmul(
                                psF[grp + tb2][:, :],
                                wt[:, ic * FF + oc * 128:ic * FF + (oc + 1) * 128],
                                x2q_fm[:, ic * T + tb * 512:ic * T + (tb + 1) * 512],
                                start=(ic == 0), stop=(ic == NIC - 1)), "ffa_mm")

        @block.scalar
        def _(a):
            for u in range(2 * NOC):
                if u >= 4:
                    b.waitv(a, spl_v[u - 4])
                    b.wait(a, "ffa_mx", famx_base + (u - 4) + 1)
                for tbp in range(2):
                    su = 2 * u + tbp
                    grp = (su % 2) * 2
                    for tb2 in range(2):
                        tb = tbp * 2 + tb2
                        b.wait(a, "ffa_mm", fam0 + 16 * su + 14 + tb2 + 1)
                        b.inc(a.copy(spb[u % 4][:, tb * 512:(tb + 1) * 512],
                                     psF[grp + tb2][:, :]), "ffa_ev")

        @block.vector
        def _(v):
            for u in range(2 * NOC):
                b.wait(v, "ffa_ev", fae0 + 4 * (u + 1))
                acc = accU if u % 2 == 0 else accG
                v.scalar_tensor_tensor(acc[:, :], spb[u % 4][:, :], -1.0,
                                       acc[:, :], ALU.mult, ALU.max)
                v.nop()
                b.inc(v.tensor_tensor(acc[:, :], acc[:, :], spb[u % 4][:, :],
                                      ALU.max), "ffa_mx")

        @block.sync
        def _(s):
            for u in range(2 * NOC):
                oc = u // 2
                sp = sp_up if u % 2 == 0 else sp_gate
                b.wait(s, "ffa_ev", fae0 + 4 * (u + 1))
                b.dma(s, "spl", u % 4, out=sp[oc * 128:(oc + 1) * 128, :],
                      in_=spb[u % 4][:, :])
        alpha_done = b.snap()

        if dbg == "upg":
            @block.gpsimd
            def _(g):
                b.wait_snap(g, alpha_done)
                b.dma(g, "dbg", 0, out=dbg_out[:, :], in_=sp_up[:, :])

        if not _ph_on("beta"):
            pwt.close()
            _finish(nc, b, block, out)
            pb.close()
            return nc

        pwt.close()

        @block.vector
        def _(v):
            b.wait(v, "ffa_mx", b.cnt["ffa_mx"])
            v.reduce_max(mxcol[:, 1:2], accU[:, :], AX.X, apply_absolute_value=True)
            b.inc(v.reduce_max(mxcol[:, 2:3], accG[:, :], AX.X,
                               apply_absolute_value=True), "msc")
        finalize_max(1)
        finalize_max(2)

        run_max_ar(1, 1, 2, "upgate")

        @block.vector
        def _(v):
            b.wait(v, "msc", b.cnt["msc"])
            v.memset(mxcol[:, 3:4], 0.0)
            v.tensor_scalar(row_scr[0:1, SG:SG + 1], row_scr[0:1, 2:3],
                            1.0 / 32767.0, 1e-12, ALU.mult, ALU.max)
            b.inc(v.tensor_scalar(row_scr[0:1, 9:10], row_scr[0:1, 1:2],
                                  1.0 / 32767.0, 1e-12, ALU.mult, ALU.max),
                  "msc")
        row_recip(9, RUP, 15)
        row_recip(SG, RG, 15)
        bcast_cols("b2", RUP, 1, RUP)
        bcast_cols("b3", RG, 1, RG)
        bcast_cols("b4", SG, 1, SG)
        ar2_ready = b.cnt["msc"]

        with ExitStack() as pbx:
            ex = pbx.enter_context
            gld = [ex(nc.sbuf_tensor("gld%d" % i, [128, 2048], BF16)) for i in range(2)]
            gk16 = [ex(nc.sbuf_tensor("gk%d" % i, [128, 2048], I16)) for i in range(2)]
            sigf = [ex(nc.sbuf_tensor("sigf%d" % i, [128, 2048], F32)) for i in range(2)]
            sgk = [ex(nc.sbuf_tensor("sgk%d" % i, [128, 2048], U16)) for i in range(2)]
            tbuf = [ex(nc.sbuf_tensor("tbuf%d" % i, [128, 2048], F32)) for i in range(2)]

            fq0 = b.cnt["ffq"]
            fs0 = b.cnt["ffs"]
            fv0 = b.cnt["ffv"]
            ffa_fin = b.cnt["ffa_mm"]
            gld_v = []
            tsp_v = [(b.dsem("tsp", oc % 2),
                      b.cnt[b.dsem("tsp", oc % 2)] + 16 * (oc // 2 + 1))
                     for oc in range(NOC)]

            @block.sync
            def _(s):
                b.wait(s, "ffa_mm", ffa_fin)
                for oc in range(NOC):
                    if oc >= 2:
                        b.wait(s, "ffq", fq0 + oc - 1)
                    gld_v.append(b.dma(s, "gld", oc % 2, out=gld[oc % 2][:, :],
                                       in_=sp_gate[oc * 128:(oc + 1) * 128, :]))

            @block.vector
            def _(v):
                b.wait(v, "msc", ar2_ready)
                for oc in range(NOC):
                    b.waitv(v, gld_v[oc])
                    if oc >= 2:
                        b.wait(v, "ffs", fs0 + oc - 1)
                    b.inc(v.tensor_scalar(gk16[oc % 2][:, :], gld[oc % 2][:, :],
                                          col_scl[:, RG:RG + 1], None, ALU.mult),
                          "ffq")
                    b.wait(v, "ffs", fs0 + oc + 1)
                    v.tensor_scalar(sgk[oc % 2][:, :], sigf[oc % 2][:, :],
                                    65536.0, 65535.0, ALU.mult, ALU.min)
                    if oc >= 2:
                        b.waitv(v, tsp_v[oc - 2])
                    v.nop()
                    v.tensor_tensor(tbuf[oc % 2][:, :], gk16[oc % 2][:, :],
                                    sgk[oc % 2][:, :], ALU.mult)
                    v.nop()
                    v.reduce_max(stat[:, 0:1], tbuf[oc % 2][:, :], AX.X,
                                 apply_absolute_value=True)
                    v.nop()
                    b.inc(v.tensor_tensor(mxcol[:, 3:4], mxcol[:, 3:4],
                                          stat[:, 0:1], ALU.max), "ffv")

            @block.scalar
            def _(a):
                for oc in range(NOC):
                    b.wait(a, "ffq", fq0 + oc + 1)
                    if oc >= 2:
                        b.wait(a, "ffv", fv0 + oc - 1)
                    b.inc(a.activation(sigf[oc % 2][:, :], gk16[oc % 2][:, :],
                                       AF.Sigmoid, scale=col_scl[:, SG:SG + 1]),
                          "ffs")

            @block.gpsimd
            def _(g):
                for oc in range(NOC):
                    b.wait(g, "ffv", fv0 + oc + 1)
                    b.dma(g, "tsp", oc % 2, out=sp_t[oc * 128:(oc + 1) * 128, :],
                          in_=tbuf[oc % 2][:, :])
            beta_done = b.snap()

        if dbg == "tb":
            @block.gpsimd
            def _(g):
                b.wait_snap(g, beta_done)
                b.dma(g, "dbg", 0, out=dbg_out[:, :], in_=sp_t[:, :])

        if not _ph_on("gamma"):
            _finish(nc, b, block, out)
            pb.close()
            return nc

        @block.vector
        def _(v):
            b.wait(v, "ffv", b.cnt["ffv"])
            b.inc(v.nop(), "msc")
        finalize_max(3)
        run_max_ar(2, 3, 1, "t")

        @block.vector
        def _(v):
            b.wait(v, "msc", b.cnt["msc"])
            v.tensor_scalar(row_scr[0:1, 10:11], row_scr[0:1, SG:SG + 1],
                            1.0 / 65536.0, None, ALU.mult)
            v.nop()
            v.tensor_tensor(row_scr[0:1, 11:12], row_scr[0:1, 3:4],
                            row_scr[0:1, 10:11], ALU.mult)
            v.nop()
            b.inc(v.tensor_scalar(row_scr[0:1, 12:13], row_scr[0:1, 11:12],
                                  1.0 / 32767.0, 1e-12, ALU.mult, ALU.max),
                  "msc")
        row_recip(12, 13, 15)

        @block.vector
        def _(v):
            b.wait(v, "msc", b.cnt["msc"])
            b.inc(v.tensor_tensor(row_scr[0:1, C2:C2 + 1], row_scr[0:1, 10:11],
                                  row_scr[0:1, 13:14], ALU.mult), "msc")
        bcast_cols("c2", C2, 1, C2)
        ar3_ready = b.cnt["msc"]

        with ExitStack() as pgx:
            ex = pgx.enter_context
            tld = [ex(nc.sbuf_tensor("tld%d" % i, [128, 2048], F32)) for i in range(2)]
            uld = [ex(nc.sbuf_tensor("uld%d" % i, [128, 2048], BF16)) for i in range(2)]
            g2k = [ex(nc.sbuf_tensor("g2k%d" % i, [128, 2048], I16)) for i in range(2)]
            upk = [ex(nc.sbuf_tensor("upk%d" % i, [128, 2048], I16)) for i in range(2)]
            ubuf = [ex(nc.sbuf_tensor("ubuf%d" % i, [128, 2048], F32)) for i in range(2)]

            fvg0 = b.cnt["ffv"]
            tld_v = []
            uld_v = []
            usp_v = [(b.dsem("usp", oc % 2),
                      b.cnt[b.dsem("usp", oc % 2)] + 16 * (oc // 2 + 1))
                     for oc in range(NOC)]

            @block.vector
            def _(v):
                b.wait(v, "msc", ar3_ready)
                b.inc(v.memset(mxcol[:, 0:1], 0.0), "ffq")
            accg_z = b.cnt["ffq"]

            @block.sync
            def _(s):
                for oc in range(NOC):
                    if oc >= 2:
                        b.wait(s, "ffv", fvg0 + oc - 1)
                    tld_v.append(b.dma(s, "tld", oc % 2, out=tld[oc % 2][:, :],
                                       in_=sp_t[oc * 128:(oc + 1) * 128, :]))
                    uld_v.append(b.dma(s, "uld", oc % 2, out=uld[oc % 2][:, :],
                                       in_=sp_up[oc * 128:(oc + 1) * 128, :]))

            @block.vector
            def _(v):
                b.wait(v, "ffq", accg_z)
                for oc in range(NOC):
                    b.waitv(v, tld_v[oc])
                    b.waitv(v, uld_v[oc])
                    v.tensor_scalar(g2k[oc % 2][:, :], tld[oc % 2][:, :],
                                    col_scl[:, C2:C2 + 1], None, ALU.mult)
                    v.tensor_scalar(upk[oc % 2][:, :], uld[oc % 2][:, :],
                                    col_scl[:, RUP:RUP + 1], None, ALU.mult)
                    if oc >= 2:
                        b.waitv(v, usp_v[oc - 2])
                    v.nop()
                    v.tensor_tensor(ubuf[oc % 2][:, :], g2k[oc % 2][:, :],
                                    upk[oc % 2][:, :], ALU.mult)
                    v.nop()
                    v.reduce_max(stat[:, 0:1], ubuf[oc % 2][:, :], AX.X,
                                 apply_absolute_value=True)
                    v.nop()
                    b.inc(v.tensor_tensor(mxcol[:, 0:1], mxcol[:, 0:1],
                                          stat[:, 0:1], ALU.max), "ffv")

            @block.gpsimd
            def _(g):
                for oc in range(NOC):
                    b.wait(g, "ffv", fvg0 + oc + 1)
                    b.dma(g, "usp", oc % 2, out=sp_u[oc * 128:(oc + 1) * 128, :],
                          in_=ubuf[oc % 2][:, :])
            gamma_done = b.snap()

        if dbg == "ub":
            @block.gpsimd
            def _(g):
                b.wait_snap(g, gamma_done)
                b.dma(g, "dbg", 0, out=dbg_out[:, :], in_=sp_u[:, :])

        if not _ph_on("full"):
            _finish(nc, b, block, out)
            pb.close()
            return nc

        @block.vector
        def _(v):
            b.wait(v, "ffv", b.cnt["ffv"])
            b.inc(v.nop(), "msc")
        finalize_max(0)
        run_max_ar(3, 0, 1, "u")

        @block.vector
        def _(v):
            b.wait(v, "msc", b.cnt["msc"])
            v.tensor_tensor(row_scr[0:1, 8:9], row_scr[0:1, 12:13],
                            row_scr[0:1, 9:10], ALU.mult)
            v.nop()
            v.tensor_tensor(row_scr[0:1, 15:16], row_scr[0:1, 0:1],
                            row_scr[0:1, 8:9], ALU.mult)
            v.nop()
            b.inc(v.tensor_scalar(row_scr[0:1, SO:SO + 1], row_scr[0:1, 15:16],
                                  1.0 / 32767.0, 1e-12, ALU.mult, ALU.max),
                  "msc")
        row_recip(SO, 11, 10)

        @block.vector
        def _(v):
            b.wait(v, "msc", b.cnt["msc"])
            b.inc(v.tensor_tensor(row_scr[0:1, C3:C3 + 1], row_scr[0:1, 8:9],
                                  row_scr[0:1, 11:12], ALU.mult), "msc")
        bcast_cols("c3", C3, 1, C3)
        bcast_cols("so", SO, 1, SO)
        ar4_ready = b.cnt["msc"]

        if dbg == "scl":
            @block.gpsimd
            def _(g):
                b.wait(g, "msc", ar4_ready)
                b.dma(g, "dbg", 0, out=dbg_out[:, :], in_=col_scl[:, :])

        with ExitStack() as pdx:
            ex = pdx.enter_context
            wdT = ex(nc.sbuf_tensor("wdT", [128, NOC * 1024], BF16))
            oq = ex(nc.sbuf_tensor("oq", [128, NOC * 1024], BF16))
            uld2 = [ex(nc.sbuf_tensor("uld2%d" % i, [128, 1024], F32))
                    for i in range(2)]
            ok16 = [ex(nc.sbuf_tensor("ok16%d" % i, [128, 1024], I16))
                    for i in range(2)]
            outsb = [ex(nc.sbuf_tensor("outsb%d" % i, [128, 1024], F32))
                     for i in range(2)]

            @block.sync
            def _(s):
                b.wait(s, "cc", ag_done)
                for blk in range(NCORES):
                    src = ag_out[blk * AGN + DN_OFF:blk * AGN + DN_OFF + 393216] \
                        .rearrange("(ic p hh) -> p ic hh", p=128, hh=128)
                    dst = wdT[:, :].rearrange("p (ic hg) -> p ic hg", hg=1024) \
                        [:, :, blk * 128:(blk + 1) * 128]
                    b.dma(s, "wdl", 0, out=dst, in_=src)
            wdt_loaded = b.dfinal("wdl", 1)[0]

            if dbg == "wdt":
                @block.gpsimd
                def _(g):
                    b.waitv(g, wdt_loaded)
                    b.dma(g, "dbg", 0, out=dbg_out[:, :], in_=wdT[:, :])

            ln_dn = b.cnt["ln_cp"]
            for hf2 in range(2):
                fqd0 = b.cnt["ffq"]
                dnm0 = b.cnt["dn_mm"]
                dna0 = b.cnt["dn_ad"]
                u2_v = []
                xv3 = []
                ost_v = [(b.dsem("ost", t8 % 2),
                          b.cnt[b.dsem("ost", t8 % 2)] + 16 * (t8 // 2 + 1))
                         for t8 in range(8)]

                @block.sync
                def _(s):
                    for oc in range(NOC):
                        if oc >= 2:
                            b.wait(s, "ffq", fqd0 + oc - 1)
                        u2_v.append(b.dma(
                            s, "u2l", oc % 2, out=uld2[oc % 2][:, :],
                            in_=sp_u[oc * 128:(oc + 1) * 128,
                                     hf2 * 1024:(hf2 + 1) * 1024]))

                @block.vector
                def _(v):
                    b.wait(v, "msc", ar4_ready)
                    if hf2 == 1:
                        b.wait(v, "dn_mm", dnm0)
                    for oc in range(NOC):
                        b.waitv(v, u2_v[oc])
                        v.tensor_scalar(ok16[oc % 2][:, :], uld2[oc % 2][:, :],
                                        col_scl[:, C3:C3 + 1], None, ALU.mult)
                        v.nop()
                        b.inc(v.tensor_scalar(oq[:, oc * 1024:(oc + 1) * 1024],
                                              ok16[oc % 2][:, :],
                                              col_scl[:, SO:SO + 1], None, ALU.mult),
                              "ffq")
                oq_ready = b.cnt["ffq"]

                if dbg == "oqd" and hf2 == 1:
                    @block.gpsimd
                    def _(g):
                        b.wait(g, "ffq", oq_ready)
                        b.dma(g, "dbg", 0, out=dbg_out[:, :], in_=oq[:, :])

                @block.sync
                def _(s):
                    if hf2 == 0:
                        b.wait(s, "ln_cp", ln_dn)
                    for t8 in range(8):
                        tt = hf2 * 8 + t8
                        if t8 >= 3:
                            b.wait(s, "dn_ad", dna0 + t8 - 2)
                        elif hf2 == 1:
                            b.wait(s, "dn_ad", dna0)
                        xv3.append(b.dma(s, "xp", t8 % 3, out=xpool[t8 % 3][:, :],
                                         in_=resid_hbm[tt * 128:(tt + 1) * 128, :]))

                @block.tensor
                def _(t):
                    b.wait(t, "ffq", oq_ready)
                    b.waitv(t, wdt_loaded)
                    for t8 in range(8):
                        bank = (t8 % 2) * 2
                        if t8 >= 2:
                            b.wait(t, "dn_ad", dna0 + t8 - 1)
                        for oc in range(NOC):
                            for hf in range(2):
                                b.inc(t.matmul(
                                    psF[bank + hf][:, :],
                                    oq[:, oc * 1024 + t8 * 128:oc * 1024 + (t8 + 1) * 128],
                                    wdT[:, oc * 1024 + hf * 512:
                                        oc * 1024 + (hf + 1) * 512],
                                    start=(oc == 0), stop=(oc == NOC - 1)), "dn_mm")

                @block.vector
                def _(v):
                    for t8 in range(8):
                        bank = (t8 % 2) * 2
                        b.wait(v, "dn_mm", dnm0 + 48 * (t8 + 1))
                        b.waitv(v, xv3[t8])
                        if t8 >= 2:
                            b.waitv(v, ost_v[t8 - 2])
                        for hf in range(2):
                            inst = v.tensor_tensor(
                                outsb[t8 % 2][:, hf * 512:(hf + 1) * 512],
                                psF[bank + hf][:, :],
                                xpool[t8 % 3][:, hf * 512:(hf + 1) * 512], ALU.add)
                            if hf == 1:
                                b.inc(inst, "dn_ad")

                @block.gpsimd
                def _(g):
                    for t8 in range(8):
                        tt = hf2 * 8 + t8
                        b.wait(g, "dn_ad", dna0 + t8 + 1)
                        b.dma(g, "ost", t8 % 2, out=out[tt * 128:(tt + 1) * 128, :],
                              in_=outsb[t8 % 2][:, :])

        pb.close()
        _finish(nc, b, block, None)

    return nc


def _finish(nc, b, block, out_ap):
    snap = b.snap()

    @block.sync
    def _(s):
        b.wait_snap(s, snap)

    @block.gpsimd
    def _(g):
        b.wait_snap(g, snap)


def _shard_inputs(inputs):
    x = np.asarray(inputs["L0_attn_hidden_states"], dtype=np.float32)
    xf = np.ascontiguousarray(x.reshape(NCORES * T, D))
    consts, _ = _consts()
    w_up = np.asarray(inputs["L0_ffn_w_up"], np.float32)
    w_gate = np.asarray(inputs["L0_ffn_w_gate"], np.float32)
    w_down = np.asarray(inputs["L0_ffn_w_down"], np.float32)
    in_maps = []
    for c in range(NCORES):
        m = {
            "x": np.ascontiguousarray(xf[c * T:(c + 1) * T]),
            "Wq": np.asarray(inputs["L0_attn_Wq"], np.float32),
            "Wk": np.asarray(inputs["L0_attn_Wk"], np.float32),
            "Wv": np.asarray(inputs["L0_attn_Wv"], np.float32),
            "Wo": np.asarray(inputs["L0_attn_Wo"], np.float32),
            "wup_blk": np.ascontiguousarray(w_up[c * WRU:(c + 1) * WRU]),
            "wgt_blk": np.ascontiguousarray(w_gate[c * WRU:(c + 1) * WRU]),
            "wdn_blk": np.ascontiguousarray(w_down[c * WRD:(c + 1) * WRD]),
            "masks": np.array([[1.0, 0.0]] if c % 2 == 0 else [[0.0, 1.0]],
                              np.float32),
        }
        m.update({k: v for k, v in consts.items()})
        in_maps.append(m)
    return in_maps


def kernel(**inputs):
    in_maps = _shard_inputs(inputs)
    nc = build_nc(dbg=DBG)
    res = run_bass_kernel_spmd(nc, in_maps, core_ids=list(range(NCORES)))
    if DBG is not None:
        return [r.get("dbg") for r in res.results], \
            [r.get("out") for r in res.results]
    outs = np.concatenate([res.results[c]["out"] for c in range(NCORES)], axis=0)
    return outs.reshape(4, 4096, 1024)


# version 05 (final)
# speedup vs baseline: 1.0434x; 1.0200x over previous
import math
from contextlib import ExitStack

import numpy as np
import ml_dtypes

import concourse.bass as bass
import concourse.mybir as mybir
from concourse.bass_utils import run_bass_kernel_spmd

F32 = mybir.dt.float32
BF16 = mybir.dt.bfloat16
I16 = mybir.dt.int16
U16 = mybir.dt.uint16
AF = mybir.ActivationFunctionType
ALU = mybir.AluOpType
AX = mybir.AxisListType

NCORES = 8
T = 2048
NT = 16
D = 1024
NIC = 8
H = 8
NCH = 16
FF = 3072
NOC = 24
WRU = FF // NCORES
WRD = D // NCORES
AGN = 3 * 384 * 1024
U_OFF, G_OFF, DN_OFF = 0, 393216, 786432
BF = ml_dtypes.bfloat16

DBG = None
STOP = None
PHASES = ["pw", "wload", "ln1", "proj", "schain", "ophase", "ln2", "x2q", "alpha", "beta",
          "gamma", "full"]

DBG_SPEC = {
    "x1n": [D, 1024], "kfm": [D, T], "vtm": [T, D], "qfm": [D, T],
    "sbf": [128, NCH * 1024], "resid": [T, D], "x2n": [T, D], "x2q": [D, T],
    "wt": [128, NIC * FF], "wdt": [128, NOC * 1024],
    "upg": [FF, T], "tb": [FF, T], "ub": [FF, T], "scl": [128, 16],
    "lnst": [128, 128], "lnst2": [128, 128], "oqd": [128, NOC * 1024],
}

DMA_POOL = ["q%02d" % i for i in range(40)]
SEM_NAMES = DMA_POOL + [
    "sd", "gd", "cc",
    "w_dq", "w_tr", "w_ev",
    "ln_st1", "ln_sq", "ln_nm", "ln_cp", "x_fr", "tr", "trev",
    "pj_mm", "pj_ev", "pj_done",
    "sc_kj", "sc_st", "sc_sn", "sc_mm", "sc_tr",
    "oph_sc", "oph_dm", "oph_o", "oph_oe", "op_mm", "op_re",
    "corr", "msc", "brd",
    "ffa_mm", "ffa_ev", "ffa_mx",
    "ffq", "ffs", "ffv", "dn_mm", "dn_ad", "w_sw", "w_rw", "w_ln",
]


class Bld:
    def __init__(self, nc, ctx):
        self.nc = nc
        self.sems = {n: ctx.enter_context(nc.semaphore(n)) for n in SEM_NAMES}
        self.cnt = {n: 0 for n in SEM_NAMES}
        self._dma_assign = {}
        self._dma_next = 0

    def inc(self, inst, name, n=1):
        inst.then_inc(self.sems[name], n)
        self.cnt[name] += n
        return self.cnt[name]

    def dsem(self, stream, slot=0):
        key = (stream, slot)
        if key not in self._dma_assign:
            self._dma_assign[key] = DMA_POOL[self._dma_next]
            self._dma_next += 1
        return self._dma_assign[key]

    def dma(self, eng, stream, slot, out, in_):
        sem = self.dsem(stream, slot)
        v = self.inc(eng.dma_start(out=out, in_=in_), sem, 16)
        return (sem, v)

    def waitv(self, eng, sv):
        if sv is not None:
            self.wait(eng, sv[0], sv[1])

    def dfinal(self, stream, nslots):
        return [(self.dsem(stream, i), self.cnt[self.dsem(stream, i)])
                for i in range(nslots)]

    def wait(self, eng, name, val):
        if val > 0:
            eng.wait_ge(self.sems[name], val)

    def snap(self):
        return dict(self.cnt)

    def wait_snap(self, eng, snap):
        for n, v in snap.items():
            if v > 0:
                eng.wait_ge(self.sems[n], v)


def _consts():
    g = 1.0 - np.exp2(-5.0 - np.arange(H, dtype=np.float64))
    jj = np.arange(128, dtype=np.float64)
    ii = np.arange(128, dtype=np.float64)
    dmask = np.zeros((128, H * 128), np.float64)
    dq = np.zeros((128, H * 128), np.float64)
    dk = np.zeros((128, H), np.float64)
    ds = np.zeros((128, H * 128), np.float64)
    for h in range(H):
        dmask[:, h * 128:(h + 1) * 128] = (ii[None, :] >= jj[:, None]) * \
            (g[h] ** (-(jj[:, None] + 1.0)))
        dq[:, h * 128:(h + 1) * 128] = (g[h] ** (ii[None, :] + 1.0)) / math.sqrt(128.0)
        dk[:, h] = g[h] ** (127.0 - jj)
        ds[:, h * 128:(h + 1) * 128] = g[h] ** 128.0
    return {
        "cst_dmask": dmask.astype(np.float32),
        "cst_dq": dq.astype(np.float32),
        "cst_dk": dk.astype(np.float32),
        "cst_ds": ds.astype(np.float32).astype(BF),
        "cst_ident": np.eye(128, dtype=BF),
        "cst_identf": np.eye(128, dtype=np.float32),
        "cst_ones": np.ones((1, 128), np.float32),
    }, [float(np.float32(g[h] ** 128.0)) for h in range(H)]


def _ph_on(name):
    if STOP is None:
        return True
    return PHASES.index(name) <= PHASES.index(STOP)


def build_nc(dbg=None):
    nc = bass.Bass("TRN2", target_bir_lowering=False, debug=False,
                   detect_race_conditions=False)

    xin = nc.dram_tensor("x", [T, D], F32, kind="ExternalInput").ap()
    Wmap = {nm: nc.dram_tensor(nm, [D, D], F32, kind="ExternalInput").ap()
            for nm in ("Wq", "Wk", "Wv", "Wo")}
    wup_blk = nc.dram_tensor("wup_blk", [WRU, D], F32, kind="ExternalInput").ap()
    wgt_blk = nc.dram_tensor("wgt_blk", [WRU, D], F32, kind="ExternalInput").ap()
    wdn_blk = nc.dram_tensor("wdn_blk", [WRD, FF], F32, kind="ExternalInput").ap()
    cdt = {"cst_ds": BF16, "cst_ident": BF16}
    cin = {}
    for nm, shape in (("cst_dmask", [128, H * 128]), ("cst_dq", [128, H * 128]),
                      ("cst_dk", [128, H]), ("cst_ds", [128, H * 128]),
                      ("cst_ident", [128, 128]), ("cst_identf", [128, 128]),
                      ("cst_ones", [1, 128])):
        cin[nm] = nc.dram_tensor(nm, shape, cdt.get(nm, F32), kind="ExternalInput").ap()
    masks_in = nc.dram_tensor("masks", [1, 2], F32, kind="ExternalInput").ap()

    out = nc.dram_tensor("out", [T, D], F32, kind="ExternalOutput").ap()
    dbg_out = None
    if dbg is not None:
        dbg_out = nc.dram_tensor("dbg", DBG_SPEC[dbg], F32, kind="ExternalOutput").ap()

    ag_in = nc.dram_tensor("ag_in", [AGN], BF16)
    ag_out = nc.dram_tensor("ag_out", [NCORES * AGN], BF16, addr_space="Shared")
    ccp_in = nc.dram_tensor("ccp_in", [128, 1024], F32)
    ccp_out = nc.dram_tensor("ccp_out", [128, 1024], F32)
    mxio = [(nc.dram_tensor("mxi%d" % k, [1, 2], F32),
             nc.dram_tensor("mxo%d" % k, [1, 2], F32)) for k in range(4)]
    resid_hbm = nc.dram_tensor("resid_hbm", [T, D], F32)
    sp_up = nc.dram_tensor("sp_up", [FF, T], BF16)
    sp_gate = nc.dram_tensor("sp_gate", [FF, T], BF16)
    sp_t = nc.dram_tensor("sp_t", [FF, T], F32)
    sp_u = nc.dram_tensor("sp_u", [FF, T], F32)

    consts, ds_py = _consts()
    del consts

    with ExitStack() as ctx:
        e = ctx.enter_context
        b = Bld(nc, ctx)
        block = e(nc.Block())

        ATT_CST = ("cst_dmask", "cst_dq", "cst_dk", "cst_ds")
        cst = {}
        for nm in cin:
            if nm in ATT_CST:
                continue
            shp = list(cin[nm].tensor.shape)
            cst[nm] = e(nc.sbuf_tensor(nm + "_s", shp, cdt.get(nm, F32)))
        msk_sb = e(nc.sbuf_tensor("msk_sb", [1, 2], F32))
        col_msk = e(nc.sbuf_tensor("col_msk", [128, 2], F32))
        row_scr = e(nc.sbuf_tensor("row_scr", [1, 16], F32))
        col_scl = e(nc.sbuf_tensor("col_scl", [128, 16], F32))
        mxcol = e(nc.sbuf_tensor("mxcol", [128, 4], F32))
        stat = e(nc.sbuf_tensor("stat", [128, 64], F32))
        lnb = [e(nc.sbuf_tensor("lnb%d" % i, [128, 32], F32)) for i in range(2)]
        sqd = e(nc.sbuf_tensor("sqd", [128, 512], BF16))
        xpool = [e(nc.sbuf_tensor("xp%d" % i, [128, D], F32)) for i in range(3)]
        psXh = {}
        psX_stack = ExitStack()
        psXh["t"] = psX_stack.enter_context(nc.psum_tensor("psX", [128, 512], F32))

        S1, R1, RUP, RG, SG, C2, C3, SO = range(8)

        def row_recip(src, dst, tmp):
            @block.scalar
            def _(a):
                b.wait(a, "msc", b.cnt["msc"])
                b.inc(a.activation(row_scr[0:1, tmp:tmp + 1],
                                   row_scr[0:1, src:src + 1], AF.Ln), "ffs")
            rrv = [None]

            @block.sync
            def _(s):
                b.wait(s, "ffs", b.cnt["ffs"])
                rrv[0] = b.dma(s, "lnb", 2, out=row_scr[0:1, 14:15],
                               in_=row_scr[0:1, tmp:tmp + 1])

            @block.scalar
            def _(a):
                b.waitv(a, rrv[0])
                b.inc(a.activation(row_scr[0:1, dst:dst + 1],
                                   row_scr[0:1, 14:15], AF.Exp, scale=-1.0),
                      "ffs")

            @block.vector
            def _(v):
                b.wait(v, "ffs", b.cnt["ffs"])
                v.tensor_tensor(row_scr[0:1, tmp:tmp + 1],
                                row_scr[0:1, src:src + 1],
                                row_scr[0:1, dst:dst + 1], ALU.mult)
                v.nop()
                v.tensor_scalar(row_scr[0:1, tmp:tmp + 1],
                                row_scr[0:1, tmp:tmp + 1], -1.0, 2.0,
                                ALU.mult, ALU.add)
                v.nop()
                b.inc(v.tensor_tensor(row_scr[0:1, dst:dst + 1],
                                      row_scr[0:1, dst:dst + 1],
                                      row_scr[0:1, tmp:tmp + 1], ALU.mult), "msc")

        def bcast_cols(tag, src_lo, n, dst_lo):
            @block.tensor
            def _(t):
                b.wait(t, "msc", b.cnt["msc"])
                b.inc(t.matmul(psXh["t"][:, dst_lo:dst_lo + n], cst["cst_ones"][:, :],
                               row_scr[0:1, src_lo:src_lo + n],
                               start=True, stop=True), "brd")

            @block.vector
            def _(v):
                b.wait(v, "brd", b.cnt["brd"])
                b.inc(v.tensor_copy(col_scl[:, dst_lo:dst_lo + n],
                                    psXh["t"][:, dst_lo:dst_lo + n]), "msc")

        def finalize_max(slot):
            @block.tensor
            def _(t):
                b.wait(t, "msc", b.cnt["msc"])
                b.inc(t.transpose(psXh["t"][0:1, 128:256], mxcol[:, slot:slot + 1],
                                  cst["cst_identf"][:, :]), "brd")

            @block.vector
            def _(v):
                b.wait(v, "brd", b.cnt["brd"])
                b.inc(v.reduce_max(row_scr[0:1, slot:slot + 1],
                                   psXh["t"][0:1, 128:256], AX.X), "msc")

        def run_max_ar(k, src_lo, n, tag):
            vs = [None]

            @block.sync
            def _(s):
                b.wait(s, "msc", b.cnt["msc"])
                vs[0] = b.dma(s, "ar", 0, out=mxio[k][0][0:1, 0:n],
                              in_=row_scr[0:1, src_lo:src_lo + n])
            v_cc = [None]

            @block.gpsimd
            def _(g):
                b.waitv(g, vs[0])
                b.inc(g.collective_compute(
                    "AllReduce", ALU.max, replica_groups=[list(range(NCORES))],
                    ins=[mxio[k][0][:, :]], outs=[mxio[k][1][:, :]]), "cc", 1)
                v_cc[0] = b.cnt["cc"]

            @block.sync
            def _(s):
                b.wait(s, "cc", v_cc[0])
                vs[0] = b.dma(s, "ar", 0, out=row_scr[0:1, src_lo:src_lo + n],
                              in_=mxio[k][1][0:1, 0:n])

            @block.vector
            def _(v):
                b.waitv(v, vs[0])
                b.inc(v.nop(), "msc")

        @block.sync
        def _(s):
            for nm in cst:
                b.dma(s, "boot", 0, out=cst[nm][:, :], in_=cin[nm][:, :])
            b.dma(s, "boot", 0, out=msk_sb[:, :], in_=masks_in[:, :])
        boot_fin = b.dfinal("boot", 1)[0]

        @block.tensor
        def _(t):
            b.waitv(t, boot_fin)
            b.inc(t.matmul(psXh["t"][:, 0:2], cst["cst_ones"][:, :], msk_sb[:, :],
                           start=True, stop=True), "brd")

        @block.vector
        def _(v):
            b.wait(v, "brd", b.cnt["brd"])
            v.memset(mxcol[:, :], 0.0)
            b.inc(v.tensor_copy(col_msk[:, :], psXh["t"][:, 0:2]), "msc")

        ag_done = 0
        with ExitStack() as pw:
            ew = pw.enter_context
            wld = [ew(nc.sbuf_tensor("wld%d" % i, [128, 1024], F32)) for i in range(2)]
            wsc = ew(nc.sbuf_tensor("wsc", [128, 96], F32))
            wsc2 = ew(nc.sbuf_tensor("wsc2", [128, 32], F32))
            wsc3 = ew(nc.sbuf_tensor("wsc3", [128, 32], F32))
            wsc4 = ew(nc.sbuf_tensor("wsc4", [128, 32], F32))
            wkq = ew(nc.sbuf_tensor("wkq", [128, 1024], F32))
            wki = ew(nc.sbuf_tensor("wki", [128, 1024], I16))
            wdq = [ew(nc.sbuf_tensor("wdq%d" % i, [128, 1024], BF16)) for i in range(2)]
            agst = [ew(nc.sbuf_tensor("agst%d" % i, [128, 128], BF16)) for i in range(2)]
            psW = [ew(nc.psum_tensor("psW%d" % i, [128, 128], BF16)) for i in range(2)]

            wtiles = []
            for t_ in range(3):
                wtiles.append(("up", wup_blk[t_ * 128:(t_ + 1) * 128, :], t_))
            for t_ in range(3):
                wtiles.append(("gt", wgt_blk[t_ * 128:(t_ + 1) * 128, :], t_))
            for t_ in range(3):
                wtiles.append(("dn", wdn_blk[:, t_ * 1024:(t_ + 1) * 1024], t_))

            dq0, tr0, ev0 = (b.cnt[k] for k in ("w_dq", "w_tr", "w_ev"))
            wld_v = []
            ags_v = [(b.dsem("ags", k % 2),
                      b.cnt[b.dsem("ags", k % 2)] + 16 * (k // 2 + 1))
                     for k in range(72)]

            ln0 = b.cnt["w_ln"]
            rw0p = b.cnt["w_rw"]
            bn_sem = b.dsem("lnb", 1)
            bn0 = b.cnt[bn_sem]
            bn_v = [(bn_sem, bn0 + 16 * (i + 1)) for i in range(9)]
            bq_sem = b.dsem("lnb", 3)
            bq0 = b.cnt[bq_sem]
            bq_v = [(bq_sem, bq0 + 16 * (i + 1)) for i in range(9)]

            @block.sync
            def _(s):
                for i, (_k, src, _t) in enumerate(wtiles):
                    if i >= 2:
                        b.wait(s, "w_dq", dq0 + i - 1)
                    wld_v.append(b.dma(s, "wld", i % 2, out=wld[i % 2][:, :], in_=src))
                    if i >= 1:
                        b.wait(s, "w_ln", ln0 + i)
                        b.dma(s, "lnb", 1, out=wsc2[:, :], in_=wsc[:, 64:96])
                        b.wait(s, "w_rw", rw0p + i)
                        b.dma(s, "lnb", 3, out=wsc4[:, :], in_=wsc3[:, :])
                b.wait(s, "w_ln", ln0 + 9)
                b.dma(s, "lnb", 1, out=wsc2[:, :], in_=wsc[:, 64:96])
                b.wait(s, "w_rw", rw0p + 9)
                b.dma(s, "lnb", 3, out=wsc4[:, :], in_=wsc3[:, :])

            sw0 = b.cnt["w_sw"]
            rw0 = b.cnt["w_rw"]

            @block.vector
            def _(v):
                rb = wsc4[:, :].rearrange("p (bl el) -> p bl el", el=1) \
                    .broadcast_to([128, 32, 32])
                sbb = wsc[:, 32:64].rearrange("p (bl el) -> p bl el", el=1) \
                    .broadcast_to([128, 32, 32])
                kq3 = wkq[:, :].rearrange("p (bl el) -> p bl el", el=32)
                ki3 = wki[:, :].rearrange("p (bl el) -> p bl el", el=32)

                def dequant(j):
                    w3p = wld[j % 2][:, :].rearrange("p (bl el) -> p bl el", el=32)
                    dq3 = wdq[j % 2][:, :].rearrange("p (bl el) -> p bl el", el=32)
                    b.waitv(v, bq_v[j])
                    if j >= 2:
                        b.wait(v, "w_tr", tr0 + 8 * (j - 1))
                    v.tensor_tensor(wsc[:, 64:96], wsc[:, 32:64], wsc4[:, :],
                                    ALU.mult)
                    v.nop()
                    v.tensor_scalar(wsc[:, 64:96], wsc[:, 64:96], -1.0, 2.0,
                                    ALU.mult, ALU.add)
                    v.nop()
                    v.tensor_tensor(wsc4[:, :], wsc4[:, :], wsc[:, 64:96],
                                    ALU.mult)
                    v.nop()
                    v.tensor_tensor(kq3, w3p, rb, ALU.mult)
                    v.nop()
                    v.tensor_scalar(wki[:, :], wkq[:, :], 7.0, -8.0,
                                    ALU.min, ALU.max)
                    v.nop()
                    b.inc(v.tensor_tensor(dq3, ki3, sbb, ALU.mult), "w_dq")

                for i in range(9):
                    b.waitv(v, wld_v[i])
                    w3 = wld[i % 2][:, :].rearrange("p (bl el) -> p bl el", el=32)
                    v.reduce_max(wsc[:, 0:32], w3, AX.X, apply_absolute_value=True)
                    if i >= 1:
                        dequant(i - 1)
                    else:
                        v.reduce_sum(wsc[:, 64:65], wld[0][:, :], AX.X)
                        v.nop()
                        v.reduce_sum(wsc[:, 64:65], wld[0][:, :], AX.X)
                    b.inc(v.tensor_scalar(wsc[:, 32:64], wsc[:, 0:32], 1.0 / 7.0,
                                          1e-12, ALU.mult, ALU.max), "w_sw")
                dequant(8)

            @block.tensor
            def _(t):
                for i in range(9):
                    b.wait(t, "w_dq", dq0 + i + 1)
                    for ic in range(8):
                        k = i * 8 + ic
                        if k >= 2:
                            b.wait(t, "w_ev", ev0 + k - 1)
                        b.inc(t.transpose(psW[k % 2][:, :],
                                          wdq[i % 2][:, ic * 128:(ic + 1) * 128],
                                          cst["cst_ident"][:, :]), "w_tr")

            @block.scalar
            def _(a):
                for i in range(9):
                    b.wait(a, "w_sw", sw0 + i + 1)
                    b.inc(a.activation(wsc[:, 64:96], wsc[:, 32:64], AF.Ln), "w_ln")
                    b.waitv(a, bn_v[i])
                    b.inc(a.activation(wsc3[:, :], wsc2[:, :], AF.Exp,
                                       scale=-1.0), "w_rw")
                    for ic in range(8):
                        k = i * 8 + ic
                        b.wait(a, "w_tr", tr0 + k + 1)
                        if k >= 2:
                            b.waitv(a, ags_v[k - 2])
                        b.inc(a.copy(agst[k % 2][:, :], psW[k % 2][:, :]), "w_ev")

            @block.gpsimd
            def _(g):
                for i, (kind, _src, tt) in enumerate(wtiles):
                    for ic in range(8):
                        k = i * 8 + ic
                        b.wait(g, "w_ev", ev0 + k + 1)
                        if kind in ("up", "gt"):
                            base = (U_OFF if kind == "up" else G_OFF) \
                                + ic * 128 * 384 + tt * 128
                            dst = ag_in[base:base + 128 * 384] \
                                .rearrange("(p c) -> p c", c=384)[:, 0:128]
                        else:
                            icol = tt * 8 + ic
                            base = DN_OFF + icol * 128 * 128
                            dst = ag_in[base:base + 128 * 128] \
                                .rearrange("(p c) -> p c", c=128)
                        b.dma(g, "ags", k % 2,
                              out=dst, in_=agst[k % 2][:, :])
                for sv in b.dfinal("ags", 2):
                    b.waitv(g, sv)
                b.inc(g.collective_compute(
                    "AllGather", ALU.bypass, replica_groups=[list(range(NCORES))],
                    ins=[ag_in[:]], outs=[ag_out[:]]), "cc", 1)
            ag_done = b.cnt["cc"]

        psX_stack.close()

        pa_alloc = ExitStack()
        ea = pa_alloc.enter_context
        Wslot = [ea(nc.sbuf_tensor("Wsl0", [128, NIC * 1024], BF16))]
        k_fm = ea(nc.sbuf_tensor("k_fm", [128, H * T], BF16))
        v_tm = ea(nc.sbuf_tensor("v_tm", [128, NT * 1024], BF16))
        q_fm = ea(nc.sbuf_tensor("q_fm", [128, H * T], BF16))
        S_fp = ea(nc.sbuf_tensor("S_fp", [128, H * 128], F32))
        S_bf = ea(nc.sbuf_tensor("S_bf", [128, NCH * 1024], BF16))
        x1v = [S_bf[:, 0:1024], S_bf[:, 1024:2048]]
        for nm in ATT_CST:
            shp = list(cin[nm].tensor.shape)
            cst[nm] = ea(nc.sbuf_tensor(nm + "_s", shp, cdt.get(nm, F32)))

        @block.sync
        def _(s):
            for nm in ATT_CST:
                b.dma(s, "boot", 1, out=cst[nm][:, :], in_=cin[nm][:, :])
        att_cst_fin = b.dfinal("boot", 2)[1]
        x1n_tm = [ea(nc.sbuf_tensor("x1ntm%d" % i, [128, 1024], BF16))
                  for i in range(2)]
        x1n = ea(nc.sbuf_tensor("x1n_sh", [128, NIC * 1024], BF16))

        kadj = [ea(nc.sbuf_tensor("kadj%d" % i, [128, 128], BF16)) for i in range(2)]
        sc_bf = [ea(nc.sbuf_tensor("scbf%d" % i, [128, 512], BF16)) for i in range(2)]
        o_sb = [ea(nc.sbuf_tensor("osb%d" % i, [128, 1024], BF16)) for i in range(2)]
        T_bf = o_sb[0]
        rs_sb = [ea(nc.sbuf_tensor("rssb%d" % i, [128, 1024], F32)) for i in range(2)]
        E_scr = rs_sb[0]
        psT = [ea(nc.psum_tensor("psT%d" % i, [128, 128], BF16)) for i in range(2)]
        psK = [ea(nc.psum_tensor("psK%d" % i, [128, 512], F32)) for i in range(4)]
        psA = [ea(nc.psum_tensor("psA%d" % i, [128, 512], F32)) for i in range(2)]

        if _ph_on("wload"):
            wl_gd = []
            pjd0 = b.cnt["pj_done"]
            trev_pa0 = b.cnt["trev"]
            mm_pa0 = b.cnt["pj_mm"]
            x1l_v = {}

            wnames = ["Wk", "Wv", "Wq", "Wk", "Wv", "Wq", "Wo"]
            if not _ph_on("proj"):
                wnames = wnames[:1]
            elif not _ph_on("ophase"):
                wnames = wnames[:6]

            @block.gpsimd
            def _(g):
                for p, nm in enumerate(wnames):
                    if p >= 1:
                        b.wait(g, "pj_done", pjd0 + p)
                    src = Wmap[nm][:, :].rearrange("(ic p) o -> p ic o", p=128)
                    dst = Wslot[0][:, :].rearrange("p (ic o) -> p ic o", o=1024)
                    wl_gd.append(b.dma(g, "wsl", 0, out=dst, in_=src))
                    if nm == "Wv" and _ph_on("proj"):
                        w_ = p // 3
                        for j in range(8):
                            b.wait(g, "trev", trev_pa0 + 64 * w_ + 8 * (j + 1))
                            if j >= 2:
                                b.wait(g, "pj_mm",
                                       mm_pa0 + 384 * w_ + 128 + 16 * (j - 1))
                            lsrc = x1n[:, :].rearrange(
                                "p (ic jj el) -> p ic jj el", jj=8, el=128)[:, :, j, :]
                            ldst = x1v[j % 2][:, :].rearrange(
                                "p (ic el) -> p ic el", el=128)
                            x1l_v[(w_, j)] = b.dma(g, "x1l", j % 2,
                                                   out=ldst, in_=lsrc)

            nwin = 2 if _ph_on("proj") else (1 if _ph_on("ln1") else 0)
            for w in range(nwin):

                st10 = b.cnt["ln_st1"]
                sq0 = b.cnt["ln_sq"]
                nm0 = b.cnt["ln_nm"]
                cp0 = b.cnt["ln_cp"]
                tr0 = b.cnt["tr"]
                tev0 = b.cnt["trev"]
                xv = []
                xv2 = []
                lnv = []

                @block.sync
                def _(s):
                    if w == 1:
                        b.wait(s, "pj_done", pjd0 + 3)
                        b.wait(s, "ln_cp", cp0)
                    for j in range(8):
                        tt = w * 8 + j
                        if j >= 3:
                            b.wait(s, "ln_st1", st10 + j - 2)
                            b.wait(s, "ln_sq", sq0 + j - 2)
                        xv.append(b.dma(s, "xp", j % 3, out=xpool[j % 3][:, :],
                                        in_=xin[tt * 128:(tt + 1) * 128, :]))

                @block.scalar
                def _(a):
                    for j in range(8):
                        b.waitv(a, xv[j])
                        xt = xpool[j % 3]
                        a.activation(sqd[:, :], xt[:, 0:512], AF.Square,
                                     accum_out=stat[:, 8 + j:9 + j])
                        b.inc(a.activation(sqd[:, :], xt[:, 512:1024], AF.Square,
                                           accum_out=stat[:, 16 + j:17 + j]),
                              "ln_sq")

                @block.vector
                def _(v):
                    for j in range(8):
                        b.waitv(v, xv[j])
                        b.inc(v.reduce_sum(stat[:, j:j + 1], xpool[j % 3][:, :],
                                           AX.X), "ln_st1")
                    b.wait(v, "ln_sq", sq0 + 8)
                    v.tensor_tensor(stat[:, 8:16], stat[:, 8:16], stat[:, 16:24],
                                    ALU.add)
                    v.nop()
                    v.tensor_scalar(stat[:, 8:16], stat[:, 8:16], 1.0 / D, None,
                                    ALU.mult)
                    v.tensor_scalar(stat[:, 24:32], stat[:, 0:8], 1.0 / D, None,
                                    ALU.mult)
                    v.nop()
                    v.tensor_tensor(stat[:, 0:8], stat[:, 24:32], stat[:, 24:32],
                                    ALU.mult)
                    v.nop()
                    v.scalar_tensor_tensor(stat[:, 8:16], stat[:, 0:8], -1.0,
                                           stat[:, 8:16], ALU.mult, ALU.add)
                    v.nop()
                    b.inc(v.tensor_scalar(stat[:, 8:16], stat[:, 8:16], 1e-5, None,
                                          ALU.add), "ln_nm")

                @block.scalar
                def _(a):
                    b.wait(a, "ln_nm", nm0 + 1)
                    b.inc(a.activation(lnb[0][:, 0:8], stat[:, 8:16], AF.Ln), "ln_sq")

                @block.sync
                def _(s):
                    b.wait(s, "ln_sq", sq0 + 9)
                    lnv.append(b.dma(s, "lnb", 0, out=lnb[1][:, 0:8],
                                     in_=lnb[0][:, 0:8]))

                @block.scalar
                def _(a):
                    b.waitv(a, lnv[-1])
                    b.inc(a.activation(lnb[0][:, 8:16], lnb[1][:, 0:8], AF.Exp,
                                       scale=-0.5), "ln_sq")

                @block.sync
                def _(s):
                    b.wait(s, "ln_sq", sq0 + 10)
                    lnv.append(b.dma(s, "lnb", 0, out=lnb[1][:, 8:16],
                                     in_=lnb[0][:, 8:16]))
                    for j in range(8):
                        tt = w * 8 + j
                        if j >= 3:
                            b.wait(s, "ln_cp", cp0 + j - 2)
                        xv2.append(b.dma(s, "xp", j % 3, out=xpool[j % 3][:, :],
                                         in_=xin[tt * 128:(tt + 1) * 128, :]))

                @block.vector
                def _(v):
                    b.waitv(v, lnv[-1])
                    b.inc(v.scalar_tensor_tensor(stat[:, 24:32], stat[:, 24:32],
                                                 -1.0, lnb[1][:, 8:16],
                                                 ALU.mult, ALU.mult), "ln_nm")
                    v.nop()
                    for j in range(8):
                        b.waitv(v, xv2[j])
                        if w > 0 or j >= 2:
                            b.wait(v, "tr", tr0 + 8 * (j - 1))
                        xt = xpool[j % 3]
                        ngb = stat[:, 24 + j:25 + j] \
                            .rearrange("p (r el) -> p r el", el=1) \
                            .broadcast_to([128, 1024, 1])
                        x3 = xt[:, :].rearrange("p (r el) -> p r el", el=1)
                        o3 = x1n_tm[j % 2][:, :].rearrange("p (r el) -> p r el", el=1)
                        b.inc(v.scalar_tensor_tensor(
                            o3, x3, lnb[1][:, 8 + j:9 + j], ngb,
                            ALU.mult, ALU.add), "ln_cp")

                @block.tensor
                def _(t):
                    for j in range(8):
                        b.wait(t, "ln_cp", cp0 + j + 1)
                        for ic in range(NIC):
                            kk = j * NIC + ic
                            if kk >= 2:
                                b.wait(t, "trev", tev0 + kk - 1)
                            b.inc(t.transpose(psT[kk % 2][:, :],
                                              x1n_tm[j % 2][:, ic * 128:(ic + 1) * 128],
                                              cst["cst_ident"][:, :]), "tr")

                @block.scalar
                def _(a):
                    for j in range(8):
                        for ic in range(NIC):
                            kk = j * NIC + ic
                            b.wait(a, "tr", tr0 + kk + 1)
                            b.inc(a.copy(x1n[:, ic * 1024 + j * 128:
                                             ic * 1024 + (j + 1) * 128],
                                         psT[kk % 2][:, :]), "trev")

                win_ready = tev0 + 64

                if dbg == "x1n" and w == 0:
                    x1nsnap = b.snap()

                    @block.gpsimd
                    def _(g):
                        b.wait_snap(g, x1nsnap)
                        dv = dbg_out[:, :].rearrange("(ic p) t -> p ic t", p=128)
                        sv = x1n[:, :].rearrange("p (ic t) -> p ic t", t=1024)
                        b.dma(g, "dbg", 0, out=dv, in_=sv)

                if not _ph_on("proj"):
                    continue

                mm0 = b.cnt["pj_mm"]
                ev0p = b.cnt["pj_ev"]

                @block.tensor
                def _(t):
                    b.wait(t, "trev", win_ready)
                    b.waitv(t, wl_gd[3 * w + 0])
                    for oc in range(H):
                        if oc >= 2:
                            b.wait(t, "pj_ev", ev0p + 2 * (oc - 1))
                        bank = (oc % 2) * 2
                        for ic in range(NIC):
                            for tb in range(2):
                                b.inc(t.matmul(
                                    psK[bank + tb][:, :],
                                    Wslot[0][:, ic * 1024 + oc * 128:
                                                       ic * 1024 + (oc + 1) * 128],
                                    x1n[:, ic * 1024 + tb * 512:
                                        ic * 1024 + (tb + 1) * 512],
                                    start=(ic == 0), stop=(ic == NIC - 1)), "pj_mm")
                    b.inc(t.nop(), "pj_done")

                @block.scalar
                def _(a):
                    for oc in range(H):
                        bank = (oc % 2) * 2
                        for tb in range(2):
                            b.wait(a, "pj_mm", mm0 + oc * 16 + 15 + tb)
                            dst = k_fm[:, oc * T + w * 1024 + tb * 512:
                                       oc * T + w * 1024 + (tb + 1) * 512]
                            b.inc(a.copy(dst, psK[bank + tb][:, :]), "pj_ev")

                mm1 = b.cnt["pj_mm"]
                ev1 = b.cnt["pj_ev"]

                @block.tensor
                def _(t):
                    b.waitv(t, wl_gd[3 * w + 1])
                    b.wait(t, "pj_ev", ev1)
                    for j in range(8):
                        b.waitv(t, x1l_v[(w, j)])
                        if j >= 2:
                            b.wait(t, "pj_ev", ev1 + 2 * (j - 1))
                        bank = (j % 2) * 2
                        for ic in range(NIC):
                            for hf in range(2):
                                b.inc(t.matmul(
                                    psK[bank + hf][:, :],
                                    x1v[j % 2][:, ic * 128:(ic + 1) * 128],
                                    Wslot[0][:, ic * 1024 + hf * 512:
                                                           ic * 1024 + (hf + 1) * 512],
                                    start=(ic == 0), stop=(ic == NIC - 1)), "pj_mm")
                    b.inc(t.nop(), "pj_done")

                @block.scalar
                def _(a):
                    for j in range(8):
                        tt = w * 8 + j
                        bank = (j % 2) * 2
                        for hf in range(2):
                            b.wait(a, "pj_mm", mm1 + j * 16 + 15 + hf)
                            dst = v_tm[:, tt * 1024 + hf * 512:
                                       tt * 1024 + (hf + 1) * 512]
                            b.inc(a.copy(dst, psK[bank + hf][:, :]), "pj_ev")

                mm2 = b.cnt["pj_mm"]
                ev2 = b.cnt["pj_ev"]

                @block.tensor
                def _(t):
                    b.waitv(t, wl_gd[3 * w + 2])
                    b.wait(t, "pj_ev", ev2)
                    for oc in range(H):
                        if oc >= 2:
                            b.wait(t, "pj_ev", ev2 + 2 * (oc - 1))
                        bank = (oc % 2) * 2
                        for ic in range(NIC):
                            for tb in range(2):
                                b.inc(t.matmul(
                                    psK[bank + tb][:, :],
                                    Wslot[0][:, ic * 1024 + oc * 128:
                                                           ic * 1024 + (oc + 1) * 128],
                                    x1n[:, ic * 1024 + tb * 512:
                                        ic * 1024 + (tb + 1) * 512],
                                    start=(ic == 0), stop=(ic == NIC - 1)), "pj_mm")
                    b.inc(t.nop(), "pj_done")

                @block.vector
                def _(v):
                    if w == 0:
                        b.waitv(v, att_cst_fin)
                    for oc in range(H):
                        dqb = cst["cst_dq"][:, oc * 128:(oc + 1) * 128] \
                            .rearrange("p (r el) -> p r el", el=128) \
                            .broadcast_to([128, 4, 128])
                        bank = (oc % 2) * 2
                        for tb in range(2):
                            b.wait(v, "pj_mm", mm2 + oc * 16 + 15 + tb)
                            dst = q_fm[:, oc * T + w * 1024 + tb * 512:
                                       oc * T + w * 1024 + (tb + 1) * 512] \
                                .rearrange("p (r el) -> p r el", el=128)
                            src = psK[bank + tb][:, :] \
                                .rearrange("p (r el) -> p r el", el=128)
                            b.inc(v.tensor_tensor(dst, src, dqb, ALU.mult), "pj_ev")

        if _ph_on("schain"):
            kj0 = b.cnt["sc_kj"]
            st0 = b.cnt["sc_st"]
            sn0 = b.cnt["sc_sn"]
            smm0 = b.cnt["sc_mm"]
            str0 = b.cnt["sc_tr"]

            @block.vector
            def _(v):
                b.wait(v, "pj_done", b.cnt["pj_done"])
                v.memset(S_fp[:, :], 0.0)
                b.inc(v.memset(S_bf[:, 0:1024], 0.0), "sc_kj")
            sci = b.cnt["sc_kj"]

            @block.tensor
            def _(t):
                b.wait(t, "sc_kj", sci)
                for c in range(NCH):
                    for h in range(H):
                        k = c * H + h
                        if k >= 2:
                            b.wait(t, "sc_kj", sci + k - 1)
                        b.inc(t.transpose(psT[k % 2][:, :],
                                          k_fm[:, h * T + c * 128:h * T + (c + 1) * 128],
                                          cst["cst_ident"][:, :]), "sc_tr")
                        b.wait(t, "sc_kj", sci + k + 1)
                        if k >= 8:
                            b.wait(t, "sc_st", st0 + 4 * (k // 4 - 1))
                        b.inc(t.matmul(
                            psA[(k // 4) % 2][:, (k % 4) * 128:(k % 4 + 1) * 128],
                            kadj[k % 2][:, :],
                            v_tm[:, c * 1024 + h * 128:c * 1024 + (h + 1) * 128],
                            start=True, stop=True), "sc_mm")

            @block.vector
            def _(v):
                for c in range(NCH):
                    for h in range(H):
                        k = c * H + h
                        b.wait(v, "sc_tr", str0 + k + 1)
                        if k >= 2:
                            b.wait(v, "sc_mm", smm0 + k - 1)
                        b.inc(v.tensor_scalar(kadj[k % 2][:, :], psT[k % 2][:, :],
                                              cst["cst_dk"][:, h:h + 1], None,
                                              ALU.mult), "sc_kj")
                        b.wait(v, "sc_mm", smm0 + k + 1)
                        if k >= 8:
                            b.wait(v, "sc_sn", sn0 + k - 8 + 1)
                        b.inc(v.scalar_tensor_tensor(
                            S_fp[:, h * 128:(h + 1) * 128],
                            S_fp[:, h * 128:(h + 1) * 128], ds_py[h],
                            psA[(k // 4) % 2][:, (k % 4) * 128:(k % 4 + 1) * 128],
                            ALU.mult, ALU.add), "sc_st")

            @block.scalar
            def _(a):
                for c in range(NCH - 1):
                    for h in range(H):
                        k = c * H + h
                        b.wait(a, "sc_st", st0 + k + 1)
                        b.inc(a.copy(S_bf[:, (c + 1) * 1024 + h * 128:
                                         (c + 1) * 1024 + (h + 1) * 128],
                                     S_fp[:, h * 128:(h + 1) * 128]), "sc_sn")
            schain_st = b.cnt["sc_st"]
            schain_sn = b.cnt["sc_sn"]

            @block.vector
            def _(v):
                b.wait(v, "sc_st", schain_st)
                b.wait(v, "msc", 1)
                b.inc(v.tensor_scalar(E_scr[:, :], S_fp[:, :], col_msk[:, 0:1], None,
                                      ALU.mult), "corr")

            hand_v = [None]

            @block.sync
            def _(s):
                b.wait(s, "corr", b.cnt["corr"])
                hand_v[0] = b.dma(s, "ar", 0, out=ccp_in[:, :], in_=E_scr[:, :])

            @block.gpsimd
            def _(g):
                b.waitv(g, hand_v[0])
                b.inc(g.collective_compute(
                    "AllReduce", ALU.add,
                    replica_groups=[[0, 1], [2, 3], [4, 5], [6, 7]],
                    ins=[ccp_in[:, :]], outs=[ccp_out[:, :]]), "cc", 1)
            hand_cc = b.cnt["cc"]

            @block.sync
            def _(s):
                b.wait(s, "cc", hand_cc)
                hand_v[0] = b.dma(s, "ar", 0, out=E_scr[:, :], in_=ccp_out[:, :])

            @block.vector
            def _(v):
                b.waitv(v, hand_v[0])
                b.wait(v, "sc_sn", schain_sn)
                b.inc(v.tensor_scalar(T_bf[:, :], E_scr[:, :], col_msk[:, 1:2], None,
                                      ALU.mult), "corr")
                for c in range(NCH):
                    v.nop()
                    b.inc(v.tensor_tensor(S_bf[:, c * 1024:(c + 1) * 1024],
                                          S_bf[:, c * 1024:(c + 1) * 1024],
                                          T_bf[:, :], ALU.add), "corr")
                    if c < NCH - 1:
                        v.nop()
                        b.inc(v.tensor_tensor(T_bf[:, :], T_bf[:, :],
                                              cst["cst_ds"][:, :], ALU.mult), "corr")
            corr_done = b.cnt["corr"]

        if _ph_on("ophase"):
            osc0 = b.cnt["oph_sc"]
            odm0 = b.cnt["oph_dm"]
            oo0 = b.cnt["oph_o"]
            ooe0 = b.cnt["oph_oe"]
            opm0 = b.cnt["op_mm"]
            ore0 = b.cnt["op_re"]
            xv5 = []
            rsp_v = [(b.dsem("rsp", c % 2),
                      b.cnt[b.dsem("rsp", c % 2)] + 16 * (c // 2 + 1))
                     for c in range(NCH)]

            @block.sync
            def _(s):
                b.wait(s, "pj_done", pjd0 + 6)
                for c in range(NCH):
                    if c >= 3:
                        b.wait(s, "op_re", ore0 + c - 2)
                    xv5.append(b.dma(s, "xp", c % 3, out=xpool[c % 3][:, :],
                                     in_=xin[c * 128:(c + 1) * 128, :]))

            @block.gpsimd
            def _(g):
                for c in range(NCH):
                    b.wait(g, "op_re", ore0 + c + 1)
                    b.dma(g, "rsp", c % 2,
                          out=resid_hbm[c * 128:(c + 1) * 128, :],
                          in_=rs_sb[c % 2][:, :])

            @block.tensor
            def _(t):
                b.wait(t, "corr", corr_done)
                b.waitv(t, wl_gd[6])
                for c in range(NCH):
                    for g_ in range(2):
                        if c >= 1:
                            b.wait(t, "oph_dm", odm0 + 2 * (c - 1) + g_ + 1)
                        for hh in range(4):
                            h = g_ * 4 + hh
                            b.inc(t.matmul(
                                psA[g_][:, hh * 128:(hh + 1) * 128],
                                k_fm[:, h * T + c * 128:h * T + (c + 1) * 128],
                                q_fm[:, h * T + c * 128:h * T + (c + 1) * 128],
                                start=True, stop=True), "oph_sc")
                    for g_ in range(2):
                        b.wait(t, "oph_dm", odm0 + 2 * c + g_ + 1)
                        if c >= 1:
                            b.wait(t, "oph_oe", ooe0 + 2 * (c - 1) + g_ + 1)
                        for hh in range(4):
                            h = g_ * 4 + hh
                            b.inc(t.matmul(
                                psK[g_][:, hh * 128:(hh + 1) * 128],
                                S_bf[:, c * 1024 + h * 128:c * 1024 + (h + 1) * 128],
                                q_fm[:, h * T + c * 128:h * T + (c + 1) * 128],
                                start=True, stop=False), "oph_o")
                            b.inc(t.matmul(
                                psK[g_][:, hh * 128:(hh + 1) * 128],
                                v_tm[:, c * 1024 + h * 128:c * 1024 + (h + 1) * 128],
                                sc_bf[g_][:, hh * 128:(hh + 1) * 128],
                                start=False, stop=True), "oph_o")
                    b.wait(t, "oph_oe", ooe0 + 2 * c + 2)
                    if c >= 1:
                        b.wait(t, "op_re", ore0 + c)
                    for h in range(H):
                        for hf in range(2):
                            b.inc(t.matmul(
                                psK[2 + hf][:, :],
                                o_sb[c % 2][:, h * 128:(h + 1) * 128],
                                Wslot[0][:, h * 1024 + hf * 512:
                                       h * 1024 + (hf + 1) * 512],
                                start=(h == 0), stop=(h == H - 1)), "op_mm")

            @block.vector
            def _(v):
                for c in range(NCH):
                    for g_ in range(2):
                        kk = 2 * c + g_
                        b.wait(v, "oph_sc", osc0 + 4 * kk + 4)
                        b.inc(v.tensor_tensor(
                            sc_bf[g_][:, :], psA[g_][:, :],
                            cst["cst_dmask"][:, g_ * 512:(g_ + 1) * 512],
                            ALU.mult), "oph_dm")
                    b.wait(v, "op_mm", opm0 + 16 * (c + 1))
                    b.waitv(v, xv5[c])
                    if c >= 2:
                        b.waitv(v, rsp_v[c - 2])
                    for hf in range(2):
                        inst = v.tensor_tensor(
                            rs_sb[c % 2][:, hf * 512:(hf + 1) * 512],
                            psK[2 + hf][:, :],
                            xpool[c % 3][:, hf * 512:(hf + 1) * 512], ALU.add)
                        if hf == 1:
                            b.inc(inst, "op_re")

            @block.scalar
            def _(a):
                for c in range(NCH):
                    for g_ in range(2):
                        kk = 2 * c + g_
                        b.wait(a, "oph_o", oo0 + 8 * (kk + 1))
                        if c >= 2:
                            b.wait(a, "op_mm", opm0 + 16 * (c - 1))
                        b.inc(a.copy(o_sb[c % 2][:, g_ * 512:(g_ + 1) * 512],
                                     psK[g_][:, :]), "oph_oe")
            ophase_done = b.snap()

        if dbg in ("x1n", "kfm", "vtm", "qfm", "sbf", "resid"):
            snap = b.snap()

            @block.gpsimd
            def _(g):
                b.wait_snap(g, snap)
                if dbg == "kfm":
                    dv = dbg_out[:, :].rearrange("(h p) t -> p h t", p=128)
                    sv = k_fm[:, :].rearrange("p (h t) -> p h t", t=T)
                    b.dma(g, "dbg", 0, out=dv, in_=sv)
                elif dbg == "qfm":
                    dv = dbg_out[:, :].rearrange("(h p) t -> p h t", p=128)
                    sv = q_fm[:, :].rearrange("p (h t) -> p h t", t=T)
                    b.dma(g, "dbg", 0, out=dv, in_=sv)
                elif dbg == "vtm":
                    dv = dbg_out[:, :].rearrange("(tt p) e -> p tt e", p=128)
                    sv = v_tm[:, :].rearrange("p (tt e) -> p tt e", e=1024)
                    b.dma(g, "dbg", 0, out=dv, in_=sv)
                elif dbg == "sbf":
                    b.dma(g, "dbg", 0, out=dbg_out[:, :], in_=S_bf[:, :])
                elif dbg == "resid":
                    b.dma(g, "dbg", 0, out=dbg_out[:, :], in_=resid_hbm[:, :])

        if not _ph_on("ln2"):
            _finish(nc, b, block, out)
            pa_alloc.close()
            return nc

        pa_done = b.snap()
        pa_alloc.close()
        pb = ExitStack()
        eb = pb.enter_context
        psXh["t"] = eb(nc.psum_tensor("psX2", [128, 512], F32))
        x2q_fm = eb(nc.sbuf_tensor("x2q_fm", [128, NIC * T], BF16))
        accU = eb(nc.sbuf_tensor("accU", [128, 2048], BF16))
        accG = eb(nc.sbuf_tensor("accG", [128, 2048], BF16))
        spb = [eb(nc.sbuf_tensor("spb%d" % i, [128, 2048], BF16)) for i in range(4)]
        psF = [eb(nc.psum_tensor("psF%d" % i, [128, 512], F32)) for i in range(4)]
        pwt = ExitStack()
        wT_up = pwt.enter_context(nc.sbuf_tensor("wT_up", [128, NIC * FF], BF16))
        wT_gt = pwt.enter_context(nc.sbuf_tensor("wT_gt", [128, NIC * FF], BF16))

        @block.sync
        def _(s):
            b.wait(s, "cc", ag_done)
            b.wait_snap(s, pa_done)
            for blk in range(NCORES):
                src = ag_out[blk * AGN + U_OFF:blk * AGN + U_OFF + 393216] \
                    .rearrange("(ic p o) -> p ic o", p=128, o=384)
                dst = wT_up[:, :].rearrange("p (ic og) -> p ic og", og=FF) \
                    [:, :, blk * 384:(blk + 1) * 384]
                b.dma(s, "wtl", 0, out=dst, in_=src)
                src = ag_out[blk * AGN + G_OFF:blk * AGN + G_OFF + 393216] \
                    .rearrange("(ic p o) -> p ic o", p=128, o=384)
                dst = wT_gt[:, :].rearrange("p (ic og) -> p ic og", og=FF) \
                    [:, :, blk * 384:(blk + 1) * 384]
                b.dma(s, "wtl", 0, out=dst, in_=src)
        wt_loaded = b.dfinal("wtl", 1)[0]

        if dbg == "wt":
            @block.gpsimd
            def _(g):
                b.waitv(g, wt_loaded)
                b.dma(g, "dbg", 0, out=dbg_out[:, :], in_=wT_up[:, :])

        with ExitStack() as pln:
            el = pln.enter_context
            x2n = el(nc.sbuf_tensor("x2n", [128, NT * 1024], BF16))

            st10 = b.cnt["ln_st1"]
            sq0 = b.cnt["ln_sq"]
            nm0 = b.cnt["ln_nm"]
            cp0 = b.cnt["ln_cp"]
            rsp_fin = b.dfinal("rsp", 2)
            xv = []
            xv2 = []
            lnv = []

            @block.sync
            def _(s):
                for sv in rsp_fin:
                    b.waitv(s, sv)
                for j in range(NT):
                    if j >= 3:
                        b.wait(s, "ln_st1", st10 + j - 2)
                        b.wait(s, "ln_sq", sq0 + j - 2)
                    xv.append(b.dma(s, "xp", j % 3, out=xpool[j % 3][:, :],
                                    in_=resid_hbm[j * 128:(j + 1) * 128, :]))

            @block.scalar
            def _(a):
                for j in range(NT):
                    b.waitv(a, xv[j])
                    xt = xpool[j % 3]
                    a.activation(sqd[:, :], xt[:, 0:512], AF.Square,
                                 accum_out=stat[:, 16 + j:17 + j])
                    b.inc(a.activation(sqd[:, :], xt[:, 512:1024], AF.Square,
                                       accum_out=stat[:, 32 + j:33 + j]), "ln_sq")

            @block.vector
            def _(v):
                for j in range(NT):
                    b.waitv(v, xv[j])
                    b.inc(v.reduce_sum(stat[:, j:j + 1], xpool[j % 3][:, :], AX.X),
                          "ln_st1")
                b.wait(v, "ln_sq", sq0 + NT)
                if dbg == "lnst2":
                    b.wait(v, "gd", 1)
                v.tensor_tensor(stat[:, 16:32], stat[:, 16:32], stat[:, 32:48],
                                ALU.add)
                v.nop()
                v.tensor_scalar(stat[:, 16:32], stat[:, 16:32], 1.0 / D, None,
                                ALU.mult)
                v.tensor_scalar(stat[:, 48:64], stat[:, 0:16], 1.0 / D, None,
                                ALU.mult)
                v.nop()
                v.tensor_tensor(stat[:, 0:16], stat[:, 48:64], stat[:, 48:64],
                                ALU.mult)
                v.nop()
                v.scalar_tensor_tensor(stat[:, 16:32], stat[:, 0:16], -1.0,
                                       stat[:, 16:32], ALU.mult, ALU.add)
                v.nop()
                b.inc(v.tensor_scalar(stat[:, 16:32], stat[:, 16:32], 1e-5, None,
                                      ALU.add), "ln_nm")

            @block.scalar
            def _(a):
                b.wait(a, "ln_nm", nm0 + 1)
                if dbg == "lnst2":
                    b.wait(a, "gd", 2)
                b.inc(a.activation(lnb[0][:, 0:16], stat[:, 16:32], AF.Ln), "ln_sq")

            @block.sync
            def _(s):
                b.wait(s, "ln_sq", sq0 + NT + 1)
                lnv.append(b.dma(s, "lnb", 0, out=lnb[1][:, 0:16],
                                 in_=lnb[0][:, 0:16]))

            @block.scalar
            def _(a):
                b.waitv(a, lnv[-1])
                b.inc(a.activation(lnb[0][:, 16:32], lnb[1][:, 0:16], AF.Exp,
                                   scale=-0.5), "ln_sq")

            @block.sync
            def _(s):
                b.wait(s, "ln_sq", sq0 + NT + 2)
                lnv.append(b.dma(s, "lnb", 0, out=lnb[1][:, 16:32],
                                 in_=lnb[0][:, 16:32]))
                for j in range(NT):
                    if j >= 3:
                        b.wait(s, "ln_cp", cp0 + j - 2)
                    xv2.append(b.dma(s, "xp", j % 3, out=xpool[j % 3][:, :],
                                     in_=resid_hbm[j * 128:(j + 1) * 128, :]))

            @block.vector
            def _(v):
                b.waitv(v, lnv[-1])
                b.inc(v.scalar_tensor_tensor(stat[:, 48:64], stat[:, 48:64],
                                             -1.0, lnb[1][:, 16:32],
                                             ALU.mult, ALU.mult), "ln_nm")
                v.nop()
                for j in range(NT):
                    b.waitv(v, xv2[j])
                    xt = xpool[j % 3]
                    scr = x2n[:, j * 1024:(j + 1) * 1024]
                    ngb = stat[:, 48 + j:49 + j] \
                        .rearrange("p (r el) -> p r el", el=1) \
                        .broadcast_to([128, 1024, 1])
                    x3 = xt[:, :].rearrange("p (r el) -> p r el", el=1)
                    o3 = scr.rearrange("p (r el) -> p r el", el=1)
                    b.inc(v.scalar_tensor_tensor(
                        o3, x3, lnb[1][:, 16 + j:17 + j], ngb,
                        ALU.mult, ALU.add), "ln_cp")
                    v.nop()
                    v.reduce_max(stat[:, 0:1], scr, AX.X,
                                 apply_absolute_value=True)
                    v.nop()
                    b.inc(v.tensor_tensor(mxcol[:, 0:1], mxcol[:, 0:1],
                                          stat[:, 0:1], ALU.max), "msc")
            ln2_done = b.snap()

            if dbg == "x2n":
                @block.gpsimd
                def _(g):
                    b.wait_snap(g, ln2_done)
                    dv = dbg_out[:, :].rearrange("(tt p) i -> p tt i", p=128)
                    sv = x2n[:, :].rearrange("p (tt i) -> p tt i", i=1024)
                    b.dma(g, "dbg", 0, out=dv, in_=sv)

            if dbg == "lnst2":
                @block.gpsimd
                def _(g):
                    b.wait(g, "ln_sq", sq0 + NT)
                    b.wait(g, "ln_st1", st10 + NT)
                    dv1 = b.dma(g, "dbg", 0, out=dbg_out[:, 0:48],
                                in_=stat[:, 16:64])
                    b.waitv(g, dv1)
                    b.inc(g.nop(), "gd")
                    b.wait(g, "ln_nm", nm0 + 1)
                    dv2 = b.dma(g, "dbg", 0, out=dbg_out[:, 48:80],
                                in_=stat[:, 16:48])
                    dv3 = b.dma(g, "dbg", 0, out=dbg_out[:, 80:96],
                                in_=stat[:, 0:16])
                    b.waitv(g, dv3)
                    b.inc(g.nop(), "gd")
                    b.wait(g, "ln_sq", sq0 + NT + 2)
                    b.dma(g, "dbg", 0, out=dbg_out[:, 96:112],
                          in_=lnb[0][:, 16:32])
                    b.dma(g, "dbg", 0, out=dbg_out[:, 112:128],
                          in_=lnb[1][:, 16:32])

            if dbg == "lnst":
                @block.gpsimd
                def _(g):
                    b.wait_snap(g, ln2_done)
                    b.dma(g, "dbg", 0, out=dbg_out[:, 0:64], in_=stat[:, :])
                    b.dma(g, "dbg", 0, out=dbg_out[:, 64:96], in_=lnb[0][:, :])
                    b.dma(g, "dbg", 0, out=dbg_out[:, 96:128], in_=lnb[1][:, :])

            if not _ph_on("x2q"):
                pln.close()
                pwt.close()
                _finish(nc, b, block, out)
                pb.close()
                return nc

            finalize_max(0)
            run_max_ar(0, 0, 1, "s1")

            @block.vector
            def _(v):
                b.wait(v, "msc", b.cnt["msc"])
                b.inc(v.tensor_scalar(row_scr[0:1, S1:S1 + 1], row_scr[0:1, 0:1],
                                      1.0 / 32767.0, 1e-12, ALU.mult, ALU.max),
                      "msc")
            row_recip(S1, R1, 15)
            bcast_cols("s1r1", S1, 2, S1)
            s1_ready = b.cnt["msc"]

            with ExitStack() as px:
                ex = px.enter_context
                xq16 = [ex(nc.sbuf_tensor("xq16%d" % i, [128, 1024], I16))
                        for i in range(2)]
                xqbf = [ex(nc.sbuf_tensor("xqbf%d" % i, [128, 1024], BF16))
                        for i in range(2)]
                psT2 = [ex(nc.psum_tensor("psT2%d" % i, [128, 128], BF16))
                        for i in range(2)]

                ffq0 = b.cnt["ffq"]
                tr0 = b.cnt["tr"]
                tev0 = b.cnt["trev"]

                @block.vector
                def _(v):
                    b.wait(v, "msc", s1_ready)
                    for j in range(NT):
                        if j >= 2:
                            b.wait(v, "tr", tr0 + 8 * (j - 1))
                        v.tensor_scalar(xq16[j % 2][:, :],
                                        x2n[:, j * 1024:(j + 1) * 1024],
                                        col_scl[:, R1:R1 + 1], None, ALU.mult)
                        v.nop()
                        b.inc(v.tensor_scalar(xqbf[j % 2][:, :], xq16[j % 2][:, :],
                                              col_scl[:, S1:S1 + 1], None, ALU.mult),
                              "ffq")

                @block.tensor
                def _(t):
                    for j in range(NT):
                        b.wait(t, "ffq", ffq0 + j + 1)
                        for ic in range(NIC):
                            kk = j * NIC + ic
                            if kk >= 2:
                                b.wait(t, "trev", tev0 + kk - 1)
                            b.inc(t.transpose(psT2[kk % 2][:, :],
                                              xqbf[j % 2][:, ic * 128:(ic + 1) * 128],
                                              cst["cst_ident"][:, :]), "tr")

                @block.scalar
                def _(a):
                    for j in range(NT):
                        for ic in range(NIC):
                            kk = j * NIC + ic
                            b.wait(a, "tr", tr0 + kk + 1)
                            b.inc(a.copy(x2q_fm[:, ic * T + j * 128:
                                             ic * T + (j + 1) * 128],
                                         psT2[kk % 2][:, :]), "trev")
                x2q_done = b.snap()

            pln.close()

        if dbg == "x2q":
            @block.gpsimd
            def _(g):
                b.wait_snap(g, x2q_done)
                dv = dbg_out[:, :].rearrange("(ic p) t -> p ic t", p=128)
                sv = x2q_fm[:, :].rearrange("p (ic t) -> p ic t", t=T)
                b.dma(g, "dbg", 0, out=dv, in_=sv)

        if not _ph_on("alpha"):
            pwt.close()
            _finish(nc, b, block, out)
            pb.close()
            return nc

        fam0 = b.cnt["ffa_mm"]
        fae0 = b.cnt["ffa_ev"]
        x2qr = b.cnt["trev"]
        spl_v = [(b.dsem("spl", u % 4),
                  b.cnt[b.dsem("spl", u % 4)] + 16 * (u // 4 + 1))
                 for u in range(2 * NOC)]

        @block.vector
        def _(v):
            v.memset(accU[:, :], 0.0)
            b.inc(v.memset(accG[:, :], 0.0), "ffa_mx")
        famx_base = b.cnt["ffa_mx"]

        @block.tensor
        def _(t):
            b.wait(t, "trev", x2qr)
            b.waitv(t, wt_loaded)
            for u in range(2 * NOC):
                oc, wsel = u // 2, u % 2
                wt = wT_up if wsel == 0 else wT_gt
                for tbp in range(2):
                    su = 2 * u + tbp
                    grp = (su % 2) * 2
                    if su >= 2:
                        b.wait(t, "ffa_ev", fae0 + 2 * (su - 1))
                    for ic in range(NIC):
                        for tb2 in range(2):
                            tb = tbp * 2 + tb2
                            b.inc(t.matmul(
                                psF[grp + tb2][:, :],
                                wt[:, ic * FF + oc * 128:ic * FF + (oc + 1) * 128],
                                x2q_fm[:, ic * T + tb * 512:ic * T + (tb + 1) * 512],
                                start=(ic == 0), stop=(ic == NIC - 1)), "ffa_mm")

        @block.scalar
        def _(a):
            for u in range(2 * NOC):
                if u >= 4:
                    b.waitv(a, spl_v[u - 4])
                    b.wait(a, "ffa_mx", famx_base + (u - 4) + 1)
                for tbp in range(2):
                    su = 2 * u + tbp
                    grp = (su % 2) * 2
                    for tb2 in range(2):
                        tb = tbp * 2 + tb2
                        b.wait(a, "ffa_mm", fam0 + 16 * su + 14 + tb2 + 1)
                        b.inc(a.copy(spb[u % 4][:, tb * 512:(tb + 1) * 512],
                                     psF[grp + tb2][:, :]), "ffa_ev")

        @block.vector
        def _(v):
            for u in range(2 * NOC):
                b.wait(v, "ffa_ev", fae0 + 4 * (u + 1))
                acc = accU if u % 2 == 0 else accG
                v.scalar_tensor_tensor(acc[:, :], spb[u % 4][:, :], -1.0,
                                       acc[:, :], ALU.mult, ALU.max)
                v.nop()
                b.inc(v.tensor_tensor(acc[:, :], acc[:, :], spb[u % 4][:, :],
                                      ALU.max), "ffa_mx")

        @block.sync
        def _(s):
            for u in range(2 * NOC):
                oc = u // 2
                sp = sp_up if u % 2 == 0 else sp_gate
                b.wait(s, "ffa_ev", fae0 + 4 * (u + 1))
                b.dma(s, "spl", u % 4, out=sp[oc * 128:(oc + 1) * 128, :],
                      in_=spb[u % 4][:, :])
        alpha_done = b.snap()

        if dbg == "upg":
            @block.gpsimd
            def _(g):
                b.wait_snap(g, alpha_done)
                b.dma(g, "dbg", 0, out=dbg_out[:, :], in_=sp_up[:, :])

        if not _ph_on("beta"):
            pwt.close()
            _finish(nc, b, block, out)
            pb.close()
            return nc

        pwt.close()

        @block.vector
        def _(v):
            b.wait(v, "ffa_mx", b.cnt["ffa_mx"])
            v.reduce_max(mxcol[:, 1:2], accU[:, :], AX.X, apply_absolute_value=True)
            b.inc(v.reduce_max(mxcol[:, 2:3], accG[:, :], AX.X,
                               apply_absolute_value=True), "msc")
        finalize_max(1)
        finalize_max(2)

        run_max_ar(1, 1, 2, "upgate")

        @block.vector
        def _(v):
            b.wait(v, "msc", b.cnt["msc"])
            v.memset(mxcol[:, 3:4], 0.0)
            v.tensor_scalar(row_scr[0:1, SG:SG + 1], row_scr[0:1, 2:3],
                            1.0 / 32767.0, 1e-12, ALU.mult, ALU.max)
            b.inc(v.tensor_scalar(row_scr[0:1, 9:10], row_scr[0:1, 1:2],
                                  1.0 / 32767.0, 1e-12, ALU.mult, ALU.max),
                  "msc")
        row_recip(9, RUP, 15)
        row_recip(SG, RG, 15)
        bcast_cols("b2", RUP, 1, RUP)
        bcast_cols("b3", RG, 1, RG)
        bcast_cols("b4", SG, 1, SG)
        ar2_ready = b.cnt["msc"]

        with ExitStack() as pbx:
            ex = pbx.enter_context
            gld = [ex(nc.sbuf_tensor("gld%d" % i, [128, 2048], BF16)) for i in range(2)]
            gk16 = [ex(nc.sbuf_tensor("gk%d" % i, [128, 2048], I16)) for i in range(2)]
            sigf = [ex(nc.sbuf_tensor("sigf%d" % i, [128, 2048], F32)) for i in range(2)]
            sgk = [ex(nc.sbuf_tensor("sgk%d" % i, [128, 2048], U16)) for i in range(2)]
            tbuf = [ex(nc.sbuf_tensor("tbuf%d" % i, [128, 2048], F32)) for i in range(2)]

            fq0 = b.cnt["ffq"]
            fs0 = b.cnt["ffs"]
            fv0 = b.cnt["ffv"]
            ffa_fin = b.cnt["ffa_mm"]
            gld_v = []
            tsp_v = [(b.dsem("tsp", oc % 2),
                      b.cnt[b.dsem("tsp", oc % 2)] + 16 * (oc // 2 + 1))
                     for oc in range(NOC)]

            spl_fin = b.dfinal("spl", 4)

            @block.sync
            def _(s):
                b.wait(s, "ffa_mm", ffa_fin)
                for sv in spl_fin:
                    b.waitv(s, sv)
                for oc in range(NOC):
                    if oc >= 2:
                        b.wait(s, "ffq", fq0 + oc - 1)
                    gld_v.append(b.dma(s, "gld", oc % 2, out=gld[oc % 2][:, :],
                                       in_=sp_gate[oc * 128:(oc + 1) * 128, :]))

            @block.vector
            def _(v):
                b.wait(v, "msc", ar2_ready)
                for oc in range(NOC):
                    b.waitv(v, gld_v[oc])
                    if oc >= 2:
                        b.wait(v, "ffs", fs0 + oc - 1)
                    b.inc(v.tensor_scalar(gk16[oc % 2][:, :], gld[oc % 2][:, :],
                                          col_scl[:, RG:RG + 1], None, ALU.mult),
                          "ffq")
                    b.wait(v, "ffs", fs0 + oc + 1)
                    v.tensor_scalar(sgk[oc % 2][:, :], sigf[oc % 2][:, :],
                                    65536.0, 65535.0, ALU.mult, ALU.min)
                    if oc >= 2:
                        b.waitv(v, tsp_v[oc - 2])
                    v.nop()
                    v.tensor_tensor(tbuf[oc % 2][:, :], gk16[oc % 2][:, :],
                                    sgk[oc % 2][:, :], ALU.mult)
                    v.nop()
                    v.reduce_max(stat[:, 0:1], tbuf[oc % 2][:, :], AX.X,
                                 apply_absolute_value=True)
                    v.nop()
                    b.inc(v.tensor_tensor(mxcol[:, 3:4], mxcol[:, 3:4],
                                          stat[:, 0:1], ALU.max), "ffv")

            @block.scalar
            def _(a):
                for oc in range(NOC):
                    b.wait(a, "ffq", fq0 + oc + 1)
                    if oc >= 2:
                        b.wait(a, "ffv", fv0 + oc - 1)
                    b.inc(a.activation(sigf[oc % 2][:, :], gk16[oc % 2][:, :],
                                       AF.Sigmoid, scale=col_scl[:, SG:SG + 1]),
                          "ffs")

            @block.gpsimd
            def _(g):
                for oc in range(NOC):
                    b.wait(g, "ffv", fv0 + oc + 1)
                    b.dma(g, "tsp", oc % 2, out=sp_t[oc * 128:(oc + 1) * 128, :],
                          in_=tbuf[oc % 2][:, :])
            beta_done = b.snap()

        if dbg == "tb":
            @block.gpsimd
            def _(g):
                b.wait_snap(g, beta_done)
                b.dma(g, "dbg", 0, out=dbg_out[:, :], in_=sp_t[:, :])

        if not _ph_on("gamma"):
            _finish(nc, b, block, out)
            pb.close()
            return nc

        @block.vector
        def _(v):
            b.wait(v, "ffv", b.cnt["ffv"])
            b.inc(v.nop(), "msc")
        finalize_max(3)
        run_max_ar(2, 3, 1, "t")

        @block.vector
        def _(v):
            b.wait(v, "msc", b.cnt["msc"])
            v.tensor_scalar(row_scr[0:1, 10:11], row_scr[0:1, SG:SG + 1],
                            1.0 / 65536.0, None, ALU.mult)
            v.nop()
            v.tensor_tensor(row_scr[0:1, 11:12], row_scr[0:1, 3:4],
                            row_scr[0:1, 10:11], ALU.mult)
            v.nop()
            b.inc(v.tensor_scalar(row_scr[0:1, 12:13], row_scr[0:1, 11:12],
                                  1.0 / 32767.0, 1e-12, ALU.mult, ALU.max),
                  "msc")
        row_recip(12, 13, 15)

        @block.vector
        def _(v):
            b.wait(v, "msc", b.cnt["msc"])
            b.inc(v.tensor_tensor(row_scr[0:1, C2:C2 + 1], row_scr[0:1, 10:11],
                                  row_scr[0:1, 13:14], ALU.mult), "msc")
        bcast_cols("c2", C2, 1, C2)
        ar3_ready = b.cnt["msc"]

        with ExitStack() as pgx:
            ex = pgx.enter_context
            tld = [ex(nc.sbuf_tensor("tld%d" % i, [128, 2048], F32)) for i in range(2)]
            uld = [ex(nc.sbuf_tensor("uld%d" % i, [128, 2048], BF16)) for i in range(2)]
            g2k = [ex(nc.sbuf_tensor("g2k%d" % i, [128, 2048], I16)) for i in range(2)]
            upk = [ex(nc.sbuf_tensor("upk%d" % i, [128, 2048], I16)) for i in range(2)]
            ubuf = [ex(nc.sbuf_tensor("ubuf%d" % i, [128, 2048], F32)) for i in range(2)]

            fvg0 = b.cnt["ffv"]
            tld_v = []
            uld_v = []
            usp_v = [(b.dsem("usp", oc % 2),
                      b.cnt[b.dsem("usp", oc % 2)] + 16 * (oc // 2 + 1))
                     for oc in range(NOC)]

            @block.vector
            def _(v):
                b.wait(v, "msc", ar3_ready)
                b.inc(v.memset(mxcol[:, 0:1], 0.0), "ffq")
            accg_z = b.cnt["ffq"]

            tsp_fin = b.dfinal("tsp", 2)

            @block.sync
            def _(s):
                for sv in tsp_fin:
                    b.waitv(s, sv)
                for oc in range(NOC):
                    if oc >= 2:
                        b.wait(s, "ffv", fvg0 + oc - 1)
                    tld_v.append(b.dma(s, "tld", oc % 2, out=tld[oc % 2][:, :],
                                       in_=sp_t[oc * 128:(oc + 1) * 128, :]))
                    uld_v.append(b.dma(s, "uld", oc % 2, out=uld[oc % 2][:, :],
                                       in_=sp_up[oc * 128:(oc + 1) * 128, :]))

            @block.vector
            def _(v):
                b.wait(v, "ffq", accg_z)
                for oc in range(NOC):
                    b.waitv(v, tld_v[oc])
                    b.waitv(v, uld_v[oc])
                    v.tensor_scalar(g2k[oc % 2][:, :], tld[oc % 2][:, :],
                                    col_scl[:, C2:C2 + 1], None, ALU.mult)
                    v.tensor_scalar(upk[oc % 2][:, :], uld[oc % 2][:, :],
                                    col_scl[:, RUP:RUP + 1], None, ALU.mult)
                    if oc >= 2:
                        b.waitv(v, usp_v[oc - 2])
                    v.nop()
                    v.tensor_tensor(ubuf[oc % 2][:, :], g2k[oc % 2][:, :],
                                    upk[oc % 2][:, :], ALU.mult)
                    v.nop()
                    v.reduce_max(stat[:, 0:1], ubuf[oc % 2][:, :], AX.X,
                                 apply_absolute_value=True)
                    v.nop()
                    b.inc(v.tensor_tensor(mxcol[:, 0:1], mxcol[:, 0:1],
                                          stat[:, 0:1], ALU.max), "ffv")

            @block.gpsimd
            def _(g):
                for oc in range(NOC):
                    b.wait(g, "ffv", fvg0 + oc + 1)
                    b.dma(g, "usp", oc % 2, out=sp_u[oc * 128:(oc + 1) * 128, :],
                          in_=ubuf[oc % 2][:, :])
            gamma_done = b.snap()
        usp_fin = b.dfinal("usp", 2)

        if dbg == "ub":
            @block.gpsimd
            def _(g):
                b.wait_snap(g, gamma_done)
                b.dma(g, "dbg", 0, out=dbg_out[:, :], in_=sp_u[:, :])

        if not _ph_on("full"):
            _finish(nc, b, block, out)
            pb.close()
            return nc

        @block.vector
        def _(v):
            b.wait(v, "ffv", b.cnt["ffv"])
            b.inc(v.nop(), "msc")
        finalize_max(0)
        run_max_ar(3, 0, 1, "u")

        @block.vector
        def _(v):
            b.wait(v, "msc", b.cnt["msc"])
            v.tensor_tensor(row_scr[0:1, 8:9], row_scr[0:1, 12:13],
                            row_scr[0:1, 9:10], ALU.mult)
            v.nop()
            v.tensor_tensor(row_scr[0:1, 15:16], row_scr[0:1, 0:1],
                            row_scr[0:1, 8:9], ALU.mult)
            v.nop()
            b.inc(v.tensor_scalar(row_scr[0:1, SO:SO + 1], row_scr[0:1, 15:16],
                                  1.0 / 32767.0, 1e-12, ALU.mult, ALU.max),
                  "msc")
        row_recip(SO, 11, 10)

        @block.vector
        def _(v):
            b.wait(v, "msc", b.cnt["msc"])
            b.inc(v.tensor_tensor(row_scr[0:1, C3:C3 + 1], row_scr[0:1, 8:9],
                                  row_scr[0:1, 11:12], ALU.mult), "msc")
        bcast_cols("c3", C3, 1, C3)
        bcast_cols("so", SO, 1, SO)
        ar4_ready = b.cnt["msc"]

        if dbg == "scl":
            @block.gpsimd
            def _(g):
                b.wait(g, "msc", ar4_ready)
                b.dma(g, "dbg", 0, out=dbg_out[:, :], in_=col_scl[:, :])

        with ExitStack() as pdx:
            ex = pdx.enter_context
            wdT = ex(nc.sbuf_tensor("wdT", [128, NOC * 1024], BF16))
            oq = ex(nc.sbuf_tensor("oq", [128, NOC * 1024], BF16))
            uld2 = [ex(nc.sbuf_tensor("uld2%d" % i, [128, 1024], F32))
                    for i in range(2)]
            ok16 = [ex(nc.sbuf_tensor("ok16%d" % i, [128, 1024], I16))
                    for i in range(2)]
            outsb = [ex(nc.sbuf_tensor("outsb%d" % i, [128, 1024], F32))
                     for i in range(2)]

            @block.sync
            def _(s):
                b.wait(s, "cc", ag_done)
                for blk in range(NCORES):
                    src = ag_out[blk * AGN + DN_OFF:blk * AGN + DN_OFF + 393216] \
                        .rearrange("(ic p hh) -> p ic hh", p=128, hh=128)
                    dst = wdT[:, :].rearrange("p (ic hg) -> p ic hg", hg=1024) \
                        [:, :, blk * 128:(blk + 1) * 128]
                    b.dma(s, "wdl", 0, out=dst, in_=src)
            wdt_loaded = b.dfinal("wdl", 1)[0]

            if dbg == "wdt":
                @block.gpsimd
                def _(g):
                    b.waitv(g, wdt_loaded)
                    b.dma(g, "dbg", 0, out=dbg_out[:, :], in_=wdT[:, :])

            ln_dn = b.cnt["ln_cp"]
            for hf2 in range(2):
                fqd0 = b.cnt["ffq"]
                dnm0 = b.cnt["dn_mm"]
                dna0 = b.cnt["dn_ad"]
                u2_v = []
                xv3 = []
                ost_v = [(b.dsem("ost", t8 % 2),
                          b.cnt[b.dsem("ost", t8 % 2)] + 16 * (t8 // 2 + 1))
                         for t8 in range(8)]

                @block.sync
                def _(s):
                    if hf2 == 0:
                        for sv in usp_fin:
                            b.waitv(s, sv)
                    for oc in range(NOC):
                        if oc >= 2:
                            b.wait(s, "ffq", fqd0 + oc - 1)
                        u2_v.append(b.dma(
                            s, "u2l", oc % 2, out=uld2[oc % 2][:, :],
                            in_=sp_u[oc * 128:(oc + 1) * 128,
                                     hf2 * 1024:(hf2 + 1) * 1024]))

                @block.vector
                def _(v):
                    b.wait(v, "msc", ar4_ready)
                    if hf2 == 1:
                        b.wait(v, "dn_mm", dnm0)
                    for oc in range(NOC):
                        b.waitv(v, u2_v[oc])
                        v.tensor_scalar(ok16[oc % 2][:, :], uld2[oc % 2][:, :],
                                        col_scl[:, C3:C3 + 1], None, ALU.mult)
                        v.nop()
                        b.inc(v.tensor_scalar(oq[:, oc * 1024:(oc + 1) * 1024],
                                              ok16[oc % 2][:, :],
                                              col_scl[:, SO:SO + 1], None, ALU.mult),
                              "ffq")
                oq_ready = b.cnt["ffq"]

                if dbg == "oqd" and hf2 == 1:
                    @block.gpsimd
                    def _(g):
                        b.wait(g, "ffq", oq_ready)
                        b.dma(g, "dbg", 0, out=dbg_out[:, :], in_=oq[:, :])

                @block.sync
                def _(s):
                    if hf2 == 0:
                        b.wait(s, "ln_cp", ln_dn)
                    for t8 in range(8):
                        tt = hf2 * 8 + t8
                        if t8 >= 3:
                            b.wait(s, "dn_ad", dna0 + t8 - 2)
                        elif hf2 == 1:
                            b.wait(s, "dn_ad", dna0)
                        xv3.append(b.dma(s, "xp", t8 % 3, out=xpool[t8 % 3][:, :],
                                         in_=resid_hbm[tt * 128:(tt + 1) * 128, :]))

                @block.tensor
                def _(t):
                    b.waitv(t, wdt_loaded)
                    for t8 in range(8):
                        bank = (t8 % 2) * 2
                        if t8 >= 2:
                            b.wait(t, "dn_ad", dna0 + t8 - 1)
                        for oc in range(NOC):
                            if t8 == 0:
                                b.wait(t, "ffq", fqd0 + oc + 1)
                            for hf in range(2):
                                b.inc(t.matmul(
                                    psF[bank + hf][:, :],
                                    oq[:, oc * 1024 + t8 * 128:oc * 1024 + (t8 + 1) * 128],
                                    wdT[:, oc * 1024 + hf * 512:
                                        oc * 1024 + (hf + 1) * 512],
                                    start=(oc == 0), stop=(oc == NOC - 1)), "dn_mm")

                @block.vector
                def _(v):
                    for t8 in range(8):
                        bank = (t8 % 2) * 2
                        b.wait(v, "dn_mm", dnm0 + 48 * (t8 + 1))
                        b.waitv(v, xv3[t8])
                        if t8 >= 2:
                            b.waitv(v, ost_v[t8 - 2])
                        for hf in range(2):
                            inst = v.tensor_tensor(
                                outsb[t8 % 2][:, hf * 512:(hf + 1) * 512],
                                psF[bank + hf][:, :],
                                xpool[t8 % 3][:, hf * 512:(hf + 1) * 512], ALU.add)
                            if hf == 1:
                                b.inc(inst, "dn_ad")

                @block.gpsimd
                def _(g):
                    for t8 in range(8):
                        tt = hf2 * 8 + t8
                        b.wait(g, "dn_ad", dna0 + t8 + 1)
                        b.dma(g, "ost", t8 % 2, out=out[tt * 128:(tt + 1) * 128, :],
                              in_=outsb[t8 % 2][:, :])

        pb.close()
        _finish(nc, b, block, None)

    return nc


def _finish(nc, b, block, out_ap):
    snap = b.snap()

    @block.sync
    def _(s):
        b.wait_snap(s, snap)

    @block.gpsimd
    def _(g):
        b.wait_snap(g, snap)


def _shard_inputs(inputs):
    x = np.asarray(inputs["L0_attn_hidden_states"], dtype=np.float32)
    xf = np.ascontiguousarray(x.reshape(NCORES * T, D))
    consts, _ = _consts()
    w_up = np.asarray(inputs["L0_ffn_w_up"], np.float32)
    w_gate = np.asarray(inputs["L0_ffn_w_gate"], np.float32)
    w_down = np.asarray(inputs["L0_ffn_w_down"], np.float32)
    in_maps = []
    for c in range(NCORES):
        m = {
            "x": np.ascontiguousarray(xf[c * T:(c + 1) * T]),
            "Wq": np.asarray(inputs["L0_attn_Wq"], np.float32),
            "Wk": np.asarray(inputs["L0_attn_Wk"], np.float32),
            "Wv": np.asarray(inputs["L0_attn_Wv"], np.float32),
            "Wo": np.asarray(inputs["L0_attn_Wo"], np.float32),
            "wup_blk": np.ascontiguousarray(w_up[c * WRU:(c + 1) * WRU]),
            "wgt_blk": np.ascontiguousarray(w_gate[c * WRU:(c + 1) * WRU]),
            "wdn_blk": np.ascontiguousarray(w_down[c * WRD:(c + 1) * WRD]),
            "masks": np.array([[1.0, 0.0]] if c % 2 == 0 else [[0.0, 1.0]],
                              np.float32),
        }
        m.update({k: v for k, v in consts.items()})
        in_maps.append(m)
    return in_maps


def kernel(**inputs):
    in_maps = _shard_inputs(inputs)
    nc = build_nc(dbg=DBG)
    res = run_bass_kernel_spmd(nc, in_maps, core_ids=list(range(NCORES)))
    if DBG is not None:
        return [r.get("dbg") for r in res.results], \
            [r.get("out") for r in res.results]
    outs = np.concatenate([res.results[c]["out"] for c in range(NCORES)], axis=0)
    return outs.reshape(4, 4096, 1024)
